# version 13
# baseline (speedup 1.0000x reference)
import numpy as np
import ml_dtypes
import concourse.bass as bass
import concourse.mybir as mybir
from concourse.bass_utils import run_bass_kernel_spmd

F32 = mybir.dt.float32
BF16 = mybir.dt.bfloat16
I32 = mybir.dt.int32
AF = mybir.ActivationFunctionType
ALU = mybir.AluOpType
AX = mybir.AxisListType

BF = ml_dtypes.bfloat16

D = 1024
T = 2048
NH = 16
SEQ = 4096
NE = 8
FF = 4096
DIN = 2048
DST = 128
NHM = 32
CONVD = 2304
DPROJ = 4384
EPS = 1e-5
TWO_PI = float(2 * np.pi)


class Ser:

    ENGS = ('tensor', 'vector', 'scalar', 'sync', 'gpsimd', 'gdma', 'sdma')

    def __init__(self, nc):
        self.nc = nc
        self.sem = {k: nc.alloc_semaphore(f"s_{k}") for k in self.ENGS}
        self.cnt = {k: 0 for k in self.ENGS}
        self.cnt['gpsimd'] = 1
        self.steps = []
        self.consts = {}
        self.cpool = nc.alloc_sbuf_tensor("cpool", [128, 32], F32)
        self.ncp = 0
        self.chain_sem = {}
        self.chain_cnt = {}

    def const(self, v, p=128):
        v = float(v)
        if v not in self.consts:
            assert self.ncp < 32
            self.consts[v] = self.ncp
            self.ncp += 1
        return self.cpool[0:p, self.consts[v]:self.consts[v] + 1]

    def snap(self):
        return [(k, v) for k, v in self.cnt.items() if v > 0]

    def _norm(self, wait):
        if wait is None:
            return self.snap()
        if isinstance(wait, tuple):
            return [wait]
        return [w for w in wait if w is not None]

    def _chain(self, eng_name):
        if eng_name not in self.chain_sem:
            self.chain_sem[eng_name] = self.nc.alloc_semaphore(f"ch_{eng_name}")
            self.chain_cnt[eng_name] = 0
        return self.chain_sem[eng_name]

    def comp(self, eng, fn, wait=None):
        cname = 'sync' if eng == 'sync2' else eng
        waits = self._norm(wait)
        def wrap(e, ew, fn=fn, cname=cname):
            ew(e)
            proxy = _ChainEng(self, cname, e)
            fn(proxy)
            assert proxy.last is not None
            proxy.last.then_inc(self.sem[cname], 1)
        self.steps.append((cname, wrap, waits))
        self.cnt[cname] += 1
        return (cname, self.cnt[cname])

    def dma(self, eng, pairs, wait=None):
        waits = self._norm(wait)
        stream = {'gpsimd': 'gdma', 'scalar': 'sdma'}.get(eng, 'sync')
        body = {'gpsimd': 'gpsimd', 'scalar': 'scalar'}.get(eng, 'sync')
        def fnw(e, ew, pairs=pairs, stream=stream):
            ew(e)
            for (o, i_) in pairs:
                e.dma_start(out=o, in_=i_).then_inc(self.sem[stream], 16)
        self.steps.append((body, fnw, waits))
        self.cnt[stream] += 16 * len(pairs)
        return (stream, self.cnt[stream])

    def xchg(self, idx, pay_src, recv_dst, xch_pay, xch_flag, par_in, wait=None):
        nc = self.nc
        xs = nc.alloc_semaphore(f"xs{idx}")
        ps = nc.alloc_semaphore(f"ps{idx}")
        rs = nc.alloc_semaphore(f"rs{idx}")
        waits = self._norm(wait)

        def fn(e, ew):
            ew(e)
            nonce = nc.values_load(par_in_sb[0:1, 1:2], engines=[mybir.EngineType.SP])
            nreg = e.to_reg(nonce)
            f = nc.alloc_register(mybir.EngineType.SP, f"xf{idx}")
            cnt = nc.alloc_register(mybir.EngineType.SP, f"xc{idx}")
            w0 = nc.alloc_register(mybir.EngineType.SP, f"xw0{idx}")
            w1 = nc.alloc_register(mybir.EngineType.SP, f"xw1{idx}")
            for i, arm in enumerate(e.switch_core_id(n=128)):
                if (i & 1) == 0:
                    e.dma_start(out=xch_pay(0), in_=pay_src).then_inc(xs, 16)
                    e.wait_ge(xs, 16)
                    e.dma_start(out=xch_flag(0, 0), in_=par_in[0:1, 1:2]).then_inc(xs, 16)
                    e.dma_start(out=xch_flag(0, 1), in_=par_in[0:1, 1:2]).then_inc(xs, 16)
                    e.wait_ge(xs, 48)
                else:
                    e.reg_mov(f, 1)
                    e.reg_mov(cnt, 0)
                    with e.While(f):
                        e.reg_add(cnt, cnt, 16)
                        e.dma_start(out=xflag_sb[:], in_=xch_flag(0, None)).then_inc(ps, 16)
                        e.wait_ge(ps, cnt)
                        e.reg_load(w0, xflag_sb[0:1, 0:1])
                        e.reg_load(w1, xflag_sb[0:1, 1:2])
                        e.reg_alu(w0, w0, nreg, ALU.not_equal)
                        e.reg_alu(w1, w1, nreg, ALU.not_equal)
                        e.reg_alu(f, w0, w1, ALU.bitwise_or)
                    e.dma_start(out=recv_dst, in_=xch_pay(0)).then_inc(rs, 16)
                    e.wait_ge(rs, 16)
            e.nop().then_inc(self.sem['sync'], 1)
        par_in_sb = self.xpar_sb
        xflag_sb = self.xflag_sb
        self.steps.append(('sync', fn, waits))
        self.cnt['sync'] += 1
        return ('sync', self.cnt['sync'])

    def build(self):
        nc = self.nc
        finals = [(k, v) for k, v in self.cnt.items() if v > 0]
        OWN_SKIP = set()

        def run_body(e, beng):
            seen = {k: 0 for k in self.ENGS}
            seen['gpsimd'] = 1
            if beng != 'gpsimd':
                e.wait_ge(self.sem['gpsimd'], 1)
            def ew_for(waits):
                def ew(ee, waits=waits):
                    for (k, v) in waits:
                        if k == beng and k in OWN_SKIP:
                            continue
                        if seen[k] >= v:
                            continue
                        ee.wait_ge(self.sem[k], v)
                        seen[k] = v
                return ew
            for st_eng, fn, waits in self.steps:
                if st_eng == beng:
                    fn(e, ew_for(waits))
            if beng == 'sync':
                for k, v in finals:
                    if seen[k] < v and k != 'sync':
                        e.wait_ge(self.sem[k], v)
                        seen[k] = v
                if self.cnt['sync']:
                    e.wait_ge(self.sem['sync'], self.cnt['sync'])

        with nc.Block() as blk:
            @blk.sync
            def _(e):
                run_body(e, 'sync')

            @blk.tensor
            def _(e):
                run_body(e, 'tensor')

            @blk.vector
            def _(e):
                run_body(e, 'vector')

            @blk.scalar
            def _(e):
                run_body(e, 'scalar')

            @blk.gpsimd
            def _(e):
                for v, i in list(self.consts.items()):
                    e.memset(self.cpool[:, i:i + 1], v)
                e.engine_nop().then_inc(self.sem['gpsimd'], 1)
                run_body(e, 'gpsimd')
        return nc


class _ChainEng:

    def __init__(self, ser, name, e):
        self._ser = ser
        self._name = name
        self._e = e
        self.last = None

    def __getattr__(self, attr):
        real = getattr(self._e, attr)
        if attr in ('wait_ge', 'engine_nop'):
            return real

        def wrapper(*a, **k):
            ser, name, e = self._ser, self._name, self._e
            if self.last is not None and name != 'tensor':
                ser._chain(name)
                self.last.then_inc(ser.chain_sem[name], 1)
                ser.chain_cnt[name] += 1
                e.wait_ge(ser.chain_sem[name], ser.chain_cnt[name])
            inst = real(*a, **k)
            self.last = inst
            return inst
        return wrapper


class Arena:

    def __init__(self, nc, words):
        self.t = nc.alloc_sbuf_tensor("arena", [128, words], F32)
        self.words = words
        self.off = 0
        self.marks = []
        self.peak = 0

    def alloc(self, p, n, dtype):
        bytes_per = {F32: 4, I32: 4, BF16: 2}[dtype]
        w = (n * bytes_per + 31) // 32 * 8
        assert self.off + w <= self.words, f"arena overflow {self.off + w} > {self.words}"
        base = self.off
        self.off += w
        self.peak = max(self.peak, self.off)
        view = self.t[0:p, base:base + w]
        if dtype == F32:
            return view[:, 0:n]
        return view.bitcast(dtype)[:, 0:n]

    def push(self):
        self.marks.append(self.off)

    def pop(self):
        self.off = self.marks.pop()


def mk_nc():
    nc = bass.Bass("TRN2", target_bir_lowering=False, detect_race_conditions=False)
    s = Ser(nc)
    return nc, s


def setup_common(nc, s):
    ident = nc.alloc_sbuf_tensor("ident", [128, 128], BF16)
    ones_row = nc.alloc_sbuf_tensor("ones_row", [1, 128], BF16)
    ones_col = nc.alloc_sbuf_tensor("ones_col", [128, 1], BF16)

    def fn(e):
        e.memset(ident[:], 1.0)
        e.memset(ones_row[:], 1.0)
        return e.memset(ones_col[:], 1.0)
    s.comp('gpsimd', fn)
    s.comp('gpsimd', lambda e: e.affine_select(
        ident[:], ident[:], pattern=[[1, 128]], base=0,
        channel_multiplier=-1, compare_op=ALU.is_equal, fill=0.0))
    return ident, ones_row, ones_col


def build_l0a():
    nc, s = mk_nc()
    x = nc.dram_tensor("x", [T, D], F32, kind="ExternalInput").ap()
    pos = nc.dram_tensor("pos", [1, T], F32, kind="ExternalInput").ap()
    wq = nc.dram_tensor("wq_t", [D, D], F32, kind="ExternalInput").ap()
    wk = nc.dram_tensor("wk_t", [D, D], F32, kind="ExternalInput").ap()
    wv = nc.dram_tensor("wv_t", [D, D], F32, kind="ExternalInput").ap()
    bq = nc.dram_tensor("bq_row", [1, D], F32, kind="ExternalInput").ap()
    bk = nc.dram_tensor("bk_row", [1, D], F32, kind="ExternalInput").ap()
    bv = nc.dram_tensor("bv_row", [1, D], F32, kind="ExternalInput").ap()

    o_xn = nc.dram_tensor("xn", [T, D], F32, kind="ExternalOutput").ap()
    o_xnT = nc.dram_tensor("xnT", [128, 8, T], BF16, kind="ExternalOutput").ap()
    o_q = nc.dram_tensor("qrTa", [65, NH, T], BF16, kind="ExternalOutput").ap()
    o_k = nc.dram_tensor("krTa", [65, NH, T], BF16, kind="ExternalOutput").ap()
    o_v = nc.dram_tensor("v", [T, D], BF16, kind="ExternalOutput").ap()

    ident, ones_row, ones_col = setup_common(nc, s)

    sx = nc.alloc_sbuf_tensor("sx", [128, D], F32)
    dump = nc.alloc_sbuf_tensor("dump", [128, D], F32)
    stat = nc.alloc_sbuf_tensor("stat", [128, 8], F32)
    sxn_bf = nc.alloc_sbuf_tensor("sxn_bf", [128, D], BF16)
    xnT = nc.alloc_sbuf_tensor("xnT_s", [128, 8 * T], BF16)
    sw = nc.alloc_sbuf_tensor("sw", [128, 8 * D], BF16)
    sb_row = nc.alloc_sbuf_tensor("sb_row", [1, D], BF16)
    spos = nc.alloc_sbuf_tensor("spos", [1, T], F32)
    qtok = nc.alloc_sbuf_tensor("qtok", [128, D], F32)
    qrope = nc.alloc_sbuf_tensor("qrope", [128, 16 * D], BF16)
    qT_sb = nc.alloc_sbuf_tensor("qT_sb", [128, T], BF16)
    vsb = nc.alloc_sbuf_tensor("vsb", [128, D], BF16)
    rwork = nc.alloc_sbuf_tensor("rwork", [32, 3 * T], F32)
    csT_bf = nc.alloc_sbuf_tensor("csT_bf", [32, 2 * T], BF16)
    cs_tok = nc.alloc_sbuf_tensor("cs_tok", [128, 16 * 64], BF16)
    ri32 = nc.alloc_sbuf_tensor("ri32", [32, T], I32)
    invf = nc.alloc_sbuf_tensor("invf", [1, 32], F32)
    zrow_bf = nc.alloc_sbuf_tensor("zrow_bf", [1, T], BF16)

    poolA = nc.alloc_psum_tensor("poolA", [128, 2048], F32)
    poolB = nc.alloc_psum_tensor("poolB", [128, 1024], F32)
    ptp = nc.alloc_psum_tensor("ptp", [128, 128], BF16)
    pfr = poolA[0:32, :]

    s.dma('sync', [(spos[:], pos)])

    s.comp('gpsimd', lambda e: e.iota(invf[:], pattern=[[1, 32]], base=0,
                                      channel_multiplier=0, allow_small_or_imprecise_dtypes=True))
    s.comp('vector', lambda e: e.memset(zrow_bf[:], 0.0))
    s.comp('scalar', lambda e: e.activation(invf[:], invf[:], AF.Exp,
                                            scale=s.const(-np.log(10000.0) / 32.0, 1)))

    def mm_freq(e):
        for g in range(4):
            last = e.matmul(pfr[:, g * 512:(g + 1) * 512], invf[:],
                            spos[:, g * 512:(g + 1) * 512], start=True, stop=True)
        return last
    s.comp('tensor', mm_freq)

    def rangered(e):
        e.tensor_scalar(rwork[:, 0:T], pfr[:], 1.0 / TWO_PI, 0.25, ALU.mult, ALU.add)
        e.tensor_scalar(rwork[:, T:2 * T], pfr[:], 1.0 / TWO_PI, None, ALU.mult)
        for c in (0, 1):
            u = rwork[:, c * T:(c + 1) * T]
            w = rwork[:, 2 * T:3 * T]
            e.tensor_copy(ri32[:], u)
            e.tensor_copy(w, ri32[:])
            e.tensor_sub(u, u, w)
            e.tensor_scalar(w, u, 0.5, None, ALU.is_ge)
            last = e.tensor_sub(u, u, w)
        return last
    s.comp('vector', rangered)

    def mk_cs(e):
        e.activation(csT_bf[:, 0:T], rwork[:, 0:T], AF.Sin, scale=s.const(TWO_PI, 32))
        return e.activation(csT_bf[:, T:2 * T], rwork[:, T:2 * T], AF.Sin, scale=s.const(TWO_PI, 32))
    s.comp('scalar', mk_cs)

    for t in range(16):
        s.comp('tensor', lambda e, t=t: e.transpose(
            ptp[:, 0:32], csT_bf[:, t * 128:(t + 1) * 128], ident[0:32, 0:32]))
        s.comp('vector', lambda e, t=t: e.tensor_copy(cs_tok[:, t * 64:t * 64 + 32], ptp[:, 0:32]))
        s.comp('tensor', lambda e, t=t: e.transpose(
            ptp[:, 0:32], csT_bf[:, T + t * 128:T + (t + 1) * 128], ident[0:32, 0:32]))
        s.comp('vector', lambda e, t=t: e.tensor_copy(cs_tok[:, t * 64 + 32:t * 64 + 64], ptp[:, 0:32]))

    for t in range(16):
        s.dma('sync', [(sx[:], x[t * 128:(t + 1) * 128, :])])

        def stats_v(e):
            return e.tensor_scalar(dump[:], sx[:], 1.0, 0.0, ALU.mult, ALU.add,
                                   accum_out=stat[:, 0:1])
        s.comp('vector', stats_v)

        def stats(e):
            return e.activation(dump[:], sx[:], AF.Square, accum_out=stat[:, 1:2])
        s.comp('scalar', stats)

        def mv(e):
            e.tensor_scalar(stat[:, 2:3], stat[:, 0:1], 1.0 / D, None, ALU.mult)
            e.tensor_scalar(stat[:, 3:4], stat[:, 1:2], 1.0 / D, None, ALU.mult)
            e.tensor_scalar(stat[:, 4:5], stat[:, 2:3], stat[:, 2:3], None, ALU.mult)
            return e.tensor_sub(stat[:, 3:4], stat[:, 3:4], stat[:, 4:5])
        s.comp('vector', mv)

        def rstd(e):
            e.activation(stat[:, 4:5], stat[:, 3:4], AF.Ln, bias=s.const(EPS))
            return e.activation(stat[:, 4:5], stat[:, 4:5], AF.Exp, scale=s.const(-0.5))
        s.comp('scalar', rstd)

        def norm(e):
            e.tensor_scalar(sx[:], sx[:], stat[:, 2:3], stat[:, 4:5], ALU.subtract, ALU.mult)
            return e.tensor_copy(sxn_bf[:], sx[:])
        s.comp('vector', norm)
        s.dma('sync', [(o_xn[t * 128:(t + 1) * 128, :], sx[:])])

        for j in range(8):
            s.comp('tensor', lambda e, j=j: e.transpose(ptp[:], sxn_bf[:, j * 128:(j + 1) * 128], ident[:]))
            s.comp('vector', lambda e, t=t, j=j: e.tensor_copy(
                xnT[:, j * T + t * 128: j * T + (t + 1) * 128], ptp[:]))

    s.dma('sync', [(o_xnT.rearrange("p j t -> p (j t)"), xnT[:])])

    for which, wdram, bdram, odst, scale in (("q", wq, bq, o_q, 0.125), ("k", wk, bk, o_k, 1.0)):
        s.dma('gpsimd', [(sw.rearrange("p (j d) -> p j d", j=8),
                          wdram.rearrange("(j p) d -> p j d", p=128)),
                         (sb_row[:], bdram)])
        for t in range(16):
            def proj(e, t=t):
                for j in range(8):
                    lhsT = xnT[:, j * T + t * 128: j * T + (t + 1) * 128]
                    e.matmul(poolB[:, 0:512], lhsT, sw[:, j * D + 0: j * D + 512],
                             start=(j == 0), stop=False)
                    e.matmul(poolB[:, 512:1024], lhsT, sw[:, j * D + 512: j * D + 1024],
                             start=(j == 0), stop=False)
                e.matmul(poolB[:, 0:512], ones_row[:], sb_row[:, 0:512], start=False, stop=True)
                return e.matmul(poolB[:, 512:1024], ones_row[:], sb_row[:, 512:1024],
                                start=False, stop=True)
            s.comp('tensor', proj)

            if scale == 1.0:
                s.comp('scalar', lambda e: e.copy(qtok[:], poolB[:]))
            else:
                s.comp('scalar', lambda e, sc=scale: e.activation(
                    qtok[:], poolB[:], AF.Identity, scale=s.const(sc)))

            def rope_t(e, t=t):
                def sv(base_tensor, off, w=32):
                    ap0 = base_tensor[:, 0:1]
                    import dataclasses
                    a = base_tensor[:, off:off + w]
                    return dataclasses.replace(a, ap=[a.ap[0], [64, 16], [1, w]])
                lo = sv(qtok, 0)
                hi = sv(qtok, 32)
                olo = sv(qrope, t * D + 0)
                ohi = sv(qrope, t * D + 32)
                co = cs_tok[:, t * 64:t * 64 + 32]
                si = cs_tok[:, t * 64 + 32:t * 64 + 64]
                import dataclasses
                cob = dataclasses.replace(co, ap=[co.ap[0], [0, 16], [1, 32]])
                sib = dataclasses.replace(si, ap=[si.ap[0], [0, 16], [1, 32]])
                w1 = sv(dump, 0)
                w2 = sv(dump, 32)
                e.tensor_tensor(w1, lo, cob, ALU.mult)
                e.tensor_tensor(w2, hi, sib, ALU.mult)
                e.tensor_tensor(olo, w1, w2, ALU.subtract)
                e.tensor_tensor(w1, hi, cob, ALU.mult)
                e.tensor_tensor(w2, lo, sib, ALU.mult)
                return e.tensor_tensor(ohi, w1, w2, ALU.add)
            s.comp('vector', rope_t)

        for j in range(8):
            def tph(e, j=j):
                for t in range(16):
                    last = e.transpose(ptp[:], qrope[:, t * D + j * 128:t * D + (j + 1) * 128],
                                       ident[:])
                    last.then_inc(s.flow, 0) if False else None
                return last
            pass
        for j in range(8):
            for t in range(16):
                s.comp('tensor', lambda e, j=j, t=t: e.transpose(
                    ptp[:], qrope[:, t * D + j * 128:t * D + (j + 1) * 128], ident[:]))
                s.comp('vector', lambda e, t=t: e.tensor_copy(
                    qT_sb[:, t * 128:(t + 1) * 128], ptp[:]))
            s.dma('sync', [(odst[0:64, 2 * j, :], qT_sb[0:64, :]),
                           (odst[0:64, 2 * j + 1, :], qT_sb[64:128, :]),
                           (odst[64:65, 2 * j, :], zrow_bf[:]),
                           (odst[64:65, 2 * j + 1, :], zrow_bf[:])])

    s.dma('gpsimd', [(sw.rearrange("p (j d) -> p j d", j=8),
                      wv.rearrange("(j p) d -> p j d", p=128)),
                     (sb_row[:], bv)])
    for t in range(16):
        def vproj(e, t=t):
            for j in range(8):
                lhsT = xnT[:, j * T + t * 128: j * T + (t + 1) * 128]
                e.matmul(poolB[:, 0:512], lhsT, sw[:, j * D + 0: j * D + 512],
                         start=(j == 0), stop=False)
                e.matmul(poolB[:, 512:1024], lhsT, sw[:, j * D + 512: j * D + 1024],
                         start=(j == 0), stop=False)
            e.matmul(poolB[:, 0:512], ones_row[:], sb_row[:, 0:512], start=False, stop=True)
            return e.matmul(poolB[:, 512:1024], ones_row[:], sb_row[:, 512:1024],
                            start=False, stop=True)
        s.comp('tensor', vproj)
        s.comp('vector', lambda e: e.tensor_copy(vsb[:], poolB[:]))
        s.dma('sync', [(o_v[t * 128:(t + 1) * 128, :], vsb[:])])

    return s.build()


def build_b1():
    nc, s = mk_nc()
    qr = nc.dram_tensor("qrTa", [65, NH, T], BF16, kind="ExternalInput").ap()
    kr = nc.dram_tensor("krTa_full", [65, NH, SEQ], BF16, kind="ExternalInput").ap()
    vf = nc.dram_tensor("v_full", [SEQ, D], BF16, kind="ExternalInput").ap()
    wo = nc.dram_tensor("wo_t", [D, D], F32, kind="ExternalInput").ap()
    bo = nc.dram_tensor("bo_row", [1, D], F32, kind="ExternalInput").ap()
    o_s = nc.dram_tensor("s_attn", [T, D], F32, kind="ExternalOutput").ap()

    ident, ones_row, ones_col = setup_common(nc, s)
    onesf = nc.alloc_sbuf_tensor("onesf", [1, 64], F32)
    s.comp('vector', lambda e: e.memset(onesf[:], 1.0))

    qh = nc.alloc_sbuf_tensor("qh", [65, T], BF16)
    kh = nc.alloc_sbuf_tensor("kh", [65, SEQ], BF16)
    vaug = nc.alloc_sbuf_tensor("vaug", [128, 32 * 65], BF16)
    qsq = nc.alloc_sbuf_tensor("qsq", [64, SEQ], BF16)
    P_bf = nc.alloc_sbuf_tensor("P_bf", [128, T], BF16)
    rZ = nc.alloc_sbuf_tensor("rZ", [1, T], F32)
    lnk = nc.alloc_sbuf_tensor("lnk", [1, 4], F32)
    ubr = nc.alloc_sbuf_tensor("ubr", [1, T], F32)
    rZb_sb = nc.alloc_sbuf_tensor("rZb_sb", [64, T], F32)
    attnT = nc.alloc_sbuf_tensor("attnT", [64, NH * T], BF16)
    sw = nc.alloc_sbuf_tensor("sw", [64, NH * D], BF16)
    sbo = nc.alloc_sbuf_tensor("sbo", [1, D], BF16)
    ssb = nc.alloc_sbuf_tensor("ssb", [128, D], F32)

    pS = nc.alloc_psum_tensor("pS", [128, 2048], F32)
    pO = nc.alloc_psum_tensor("pO", [65, 2048], F32)

    s.dma('gpsimd', [(sw.rearrange("p (h d) -> p h d", h=NH),
                      wo.rearrange("(h p) d -> p h d", p=64)),
                     (sbo[:], bo)])

    for h in range(NH):
        s.dma('sync', [(qh[:], qr[:, h, :]),
                       (kh[:], kr[:, h, :]),
                       (vaug.rearrange("p (c w) -> p c w", w=65)[:, :, 0:64],
                        vf.rearrange("(c p) d -> p c d", p=128)[:, :, h * 64:(h + 1) * 64])])
        s.comp('vector', lambda e: e.memset(
            vaug.rearrange("p (c w) -> p c w", w=65)[:, :, 64:65], 1.0))
        s.comp('vector', lambda e: e.memset(kh[64:65, :], 1.0))
        def sq_q(e):
            return e.tensor_mul(qsq[:, 0:T], qh[0:64, :], qh[0:64, :])
        s.comp('vector', sq_q)

        def mm_qsq(e):
            for g in range(4):
                last = e.matmul(pO[0:1, g * 512:(g + 1) * 512], ones_col[0:64, :],
                                qsq[:, g * 512:(g + 1) * 512], start=True, stop=True)
            return last
        s.comp('tensor', mm_qsq)
        s.comp('scalar', lambda e: e.activation(ubr[:], pO[0:1, 0:T], AF.Ln))

        def sq_k(e):
            return e.tensor_mul(qsq[:], kh[0:64, :], kh[0:64, :])
        s.comp('vector', sq_k)

        def mm_ksq2(e):
            for g in range(8):
                last = e.matmul(pS[0:1, (g % 4) * 512:((g % 4) + 1) * 512],
                                ones_col[0:64, :],
                                qsq[:, g * 512:(g + 1) * 512],
                                start=(g < 4), stop=(g >= 4))
            return last
        s.comp('tensor', mm_ksq2)
        def kmax2(e):
            return e.reduce_max(lnk[:, 0:1], pS[0:1, 0:T], axis=AX.X)
        s.comp('vector', kmax2)
        s.comp('scalar', lambda e: e.activation(lnk[:, 1:2], lnk[:, 0:1], AF.Ln))

        def mk_ub(e):
            return e.tensor_scalar(ubr[:], ubr[:], lnk[0:1, 1:2], None, ALU.add)
        s.comp('vector', mk_ub)
        s.comp('scalar', lambda e: e.activation(ubr[:], ubr[:], AF.Exp, scale=s.const(0.5, 1)))
        s.comp('vector', lambda e: e.tensor_scalar(qh[64:65, :], ubr[:], -1.0, None, ALU.mult))

        for c in range(32):
            def mmS(e, c=c):
                for g in range(4):
                    last = e.matmul(pS[:, g * 512:(g + 1) * 512],
                                    kh[:, c * 128:(c + 1) * 128],
                                    qh[:, g * 512:(g + 1) * 512], start=True, stop=True)
                return last
            s.comp('tensor', mmS)
            s.comp('scalar', lambda e: e.activation(P_bf[:], pS[:], AF.Exp))

            def mmV(e, c=c):
                for g in range(4):
                    last = e.matmul(pO[:, g * 512:(g + 1) * 512],
                                    vaug[:, c * 65:(c + 1) * 65],
                                    P_bf[:, g * 512:(g + 1) * 512],
                                    start=(c == 0), stop=(c == 31))
                return last
            s.comp('tensor', mmV)

        s.comp('vector', lambda e: e.reciprocal(rZ[:], pO[64:65, :]))

        def mm_rzb(e):
            for g in range(4):
                last = e.matmul(pS[0:64, g * 512:(g + 1) * 512], onesf[:],
                                rZ[:, g * 512:(g + 1) * 512], start=True, stop=True)
            return last
        s.comp('tensor', mm_rzb)
        s.comp('vector', lambda e: e.tensor_copy(rZb_sb[:], pS[0:64, 0:T]))
        s.comp('vector', lambda e, h=h: e.tensor_mul(
            attnT[:, h * T:(h + 1) * T], pO[0:64, :], rZb_sb[:]))

    for t in range(16):
        def wop(e, t=t):
            for h in range(NH):
                lhsT = attnT[:, h * T + t * 128:h * T + (t + 1) * 128]
                rhs0 = sw[:, h * D + 0: h * D + 512]
                rhs1 = sw[:, h * D + 512: h * D + 1024]
                e.matmul(pS[:, 0:512], lhsT, rhs0, start=(h == 0), stop=False)
                e.matmul(pS[:, 512:1024], lhsT, rhs1, start=(h == 0), stop=False)
            e.matmul(pS[:, 0:512], ones_row[:], sbo[:, 0:512], start=False, stop=True)
            return e.matmul(pS[:, 512:1024], ones_row[:], sbo[:, 512:1024],
                            start=False, stop=True)
        s.comp('tensor', wop)
        s.comp('vector', lambda e: e.tensor_copy(ssb[:], pS[:, 0:1024]))
        s.dma('sync', [(o_s[t * 128:(t + 1) * 128, :], ssb[:])])

    return s.build()


def run_b1(inputs, static, r0):
    nc = build_b1()
    in_maps = []
    for c in range(8):
        b = c // 2
        kr_full = np.concatenate([np.asarray(r0[2 * b]['krTa']),
                                  np.asarray(r0[2 * b + 1]['krTa'])], axis=2)
        v_full = np.concatenate([np.asarray(r0[2 * b]['v']),
                                 np.asarray(r0[2 * b + 1]['v'])], axis=0)
        m = {'qrTa': np.asarray(r0[c]['qrTa']), 'krTa_full': np.ascontiguousarray(kr_full),
             'v_full': np.ascontiguousarray(v_full),
             'wo_t': static['wo_t'], 'bo_row': static['bo_row']}
        in_maps.append(m)
    res = run_bass_kernel_spmd(nc, in_maps, core_ids=list(range(8)))
    return res.results


def build_b2():
    nc, s = mk_nc()
    xnT_d = nc.dram_tensor("xnT", [128, 8, T], BF16, kind="ExternalInput").ap()
    xn_d = nc.dram_tensor("xn", [T, D], F32, kind="ExternalInput").ap()
    x_d = nc.dram_tensor("x", [T, D], F32, kind="ExternalInput").ap()
    s_d = nc.dram_tensor("s_attn", [T, D], F32, kind="ExternalInput").ap()
    cent = nc.dram_tensor("cent_t", [D, NE], F32, kind="ExternalInput").ap()
    gam = nc.dram_tensor("gamma_t", [D, NE], F32, kind="ExternalInput").ap()
    bet = nc.dram_tensor("beta_t", [D, NE], F32, kind="ExternalInput").ap()
    w1 = nc.dram_tensor("w1_t", [NE, D, FF], F32, kind="ExternalInput").ap()
    b1 = nc.dram_tensor("b1_c", [NE, 128, 32], F32, kind="ExternalInput").ap()
    w2 = nc.dram_tensor("w2_t", [NE, FF, D], F32, kind="ExternalInput").ap()
    b2d = nc.dram_tensor("b2_r", [1, NE * D], F32, kind="ExternalInput").ap()
    o_x1 = nc.dram_tensor("x1", [T, D], F32, kind="ExternalOutput").ap()

    ident, ones_row, ones_col = setup_common(nc, s)
    identf = nc.alloc_sbuf_tensor("identf", [128, 128], F32)
    s.comp('gpsimd', lambda e: e.memset(identf[:], 1.0))
    s.comp('gpsimd', lambda e: e.affine_select(identf[:], identf[:], pattern=[[1, 128]],
                                               base=0, channel_multiplier=-1,
                                               compare_op=ALU.is_equal, fill=0.0))

    xnT = nc.alloc_sbuf_tensor("xnT_s", [128, 8 * T], BF16)
    xsq = nc.alloc_sbuf_tensor("xsq", [128, T], BF16)
    RSTDF = nc.alloc_sbuf_tensor("RSTDF", [128, T], F32)
    scent = nc.alloc_sbuf_tensor("scent", [128, 8 * NE], BF16)
    sgam = nc.alloc_sbuf_tensor("sgam", [128, 8 * NE], F32)
    sbet = nc.alloc_sbuf_tensor("sbet", [128, 8 * NE], F32)
    scoT = nc.alloc_sbuf_tensor("scoT", [8, T], F32)
    sco = nc.alloc_sbuf_tensor("sco", [128, 16 * NE], F32)
    m1c = nc.alloc_sbuf_tensor("m1c", [128, 16], F32)
    ohc = nc.alloc_sbuf_tensor("ohc", [128, 16 * NE], F32)
    alc = nc.alloc_sbuf_tensor("alc", [128, 16], F32)
    sb2 = nc.alloc_sbuf_tensor("sb2", [1, D], BF16)
    sb1c = nc.alloc_sbuf_tensor("sb1c", [128, 32], F32)
    nfeT = nc.alloc_sbuf_tensor("nfeT", [128, 8 * 1024], BF16)
    w1blk = nc.alloc_sbuf_tensor("w1blk", [128, 8 * 128], BF16)
    w2blk = nc.alloc_sbuf_tensor("w2blk", [128, 512], BF16)
    rh = nc.alloc_sbuf_tensor("rh", [128, 32 * 1024], BF16)
    yac = nc.alloc_sbuf_tensor("yac", [128, 8 * 1024], F32)
    xtile = nc.alloc_sbuf_tensor("xtile", [128, D], F32)
    stile = nc.alloc_sbuf_tensor("stile", [128, D], F32)
    ntile = nc.alloc_sbuf_tensor("ntile", [128, D], F32)
    gtile = nc.alloc_sbuf_tensor("gtile", [128, D], F32)

    pA = nc.alloc_psum_tensor("pA", [128, 2048], F32)
    pY = nc.alloc_psum_tensor("pY", [128, 2048], F32)

    s.dma('sync', [(xnT.rearrange("p (j t) -> p j t", j=8), xnT_d)])
    s.dma('gpsimd', [(scent.rearrange("p (j e) -> p j e", j=8),
                      cent.rearrange("(j p) e -> p j e", p=128))])
    s.dma('sync', [(sgam.rearrange("p (j e) -> p j e", j=8),
                    gam.rearrange("(j p) e -> p j e", p=128)),
                   (sbet.rearrange("p (j e) -> p j e", j=8),
                    bet.rearrange("(j p) e -> p j e", p=128))])

    def mm_sco(e):
        for j in range(8):
            for g in range(4):
                last = e.matmul(pA[0:8, g * 512:(g + 1) * 512],
                                scent[:, j * NE:(j + 1) * NE],
                                xnT[:, j * T + g * 512:j * T + (g + 1) * 512],
                                start=(j == 0), stop=(j == 7))
        return last
    s.comp('tensor', mm_sco)
    s.comp('vector', lambda e: e.tensor_copy(scoT[:], pA[0:8, 0:T]))
    for t in range(16):
        s.comp('tensor', lambda e, t=t: e.transpose(
            pA[:, 1024:1024 + 8], scoT[:, t * 128:(t + 1) * 128], identf[0:8, 0:8]))
        s.comp('vector', lambda e, t=t: e.tensor_copy(sco[:, t * NE:(t + 1) * NE],
                                                      pA[:, 1024:1024 + 8]))
    for t in range(16):
        def sel(e, t=t):
            e.reduce_max(m1c[:, t:t + 1], sco[:, t * NE:(t + 1) * NE], axis=AX.X)
            return e.tensor_scalar(ohc[:, t * NE:(t + 1) * NE], sco[:, t * NE:(t + 1) * NE],
                                   m1c[:, t:t + 1], None, ALU.is_ge)
        s.comp('vector', sel)
        s.comp('scalar', lambda e, t=t: e.activation(alc[:, t:t + 1], m1c[:, t:t + 1], AF.Sigmoid))

    def mk_rstdf(e):
        for j in range(8):
            e.tensor_mul(xsq[:], xnT[:, j * T:(j + 1) * T], xnT[:, j * T:(j + 1) * T])
        return e.engine_nop()
    for j in range(8):
        s.comp('vector', lambda e, j=j: e.tensor_mul(
            xsq[:], xnT[:, j * T:(j + 1) * T], xnT[:, j * T:(j + 1) * T]))
        def mmsq(e, j=j):
            for g in range(4):
                last = e.matmul(pA[0:1, g * 512:(g + 1) * 512], ones_col[:],
                                xsq[:, g * 512:(g + 1) * 512],
                                start=(j == 0), stop=(j == 7))
            return last
        s.comp('tensor', mmsq)

    def rstdf_row(e):
        rsr = RSTDF[0:1, :]
        e.activation(rsr, pA[0:1, 0:T], AF.Ln, bias=s.const(EPS, 1), scale=s.const(1.0 / D, 1))
        return e.activation(rsr, rsr, AF.Exp, scale=s.const(-0.5, 1))
    s.comp('scalar', rstdf_row)
    onesf = nc.alloc_sbuf_tensor("onesf", [1, 128], F32)
    s.comp('vector', lambda e: e.memset(onesf[:], 1.0))

    def mm_rb(e):
        rsr = RSTDF[0:1, :]
        for g in range(4):
            last = e.matmul(pA[:, g * 512:(g + 1) * 512], onesf[:],
                            rsr[:, g * 512:(g + 1) * 512], start=True, stop=True)
        return last
    s.comp('tensor', mm_rb)
    s.comp('vector', lambda e: e.tensor_copy(RSTDF[:], pA[:, 0:T]))
    for j in range(8):
        s.comp('vector', lambda e, j=j: e.tensor_mul(
            xnT[:, j * T:(j + 1) * T], xnT[:, j * T:(j + 1) * T], RSTDF[:]))

    for half in range(2):
        hoff = half * 1024
        s.comp('vector', lambda e: e.memset(yac[:], 0.0))
        for ee in range(NE):
            s.dma('sync', [(sb1c[:], b1[ee, :, :])])
            s.dma('gpsimd', [(sb2[:], b2d[0:1, ee * D:(ee + 1) * D])])
            for j in range(8):
                s.comp('vector', lambda e, ee=ee, j=j, hoff=hoff: e.tensor_scalar(
                    nfeT[:, j * 1024:(j + 1) * 1024],
                    xnT[:, j * T + hoff:j * T + hoff + 1024],
                    sgam[:, j * NE + ee:j * NE + ee + 1],
                    sbet[:, j * NE + ee:j * NE + ee + 1], ALU.mult, ALU.add))
            for f in range(32):
                s.dma('gpsimd', [(w1blk.rearrange("p (j c) -> p j c", j=8),
                                  w1[ee].rearrange("(j p) c -> p j c", p=128)[:, :, f * 128:(f + 1) * 128])])
                def mmw1(e, f=f):
                    for j in range(8):
                        e.matmul(pA[:, 0:512], w1blk[:, j * 128:(j + 1) * 128],
                                 nfeT[:, j * 1024:j * 1024 + 512],
                                 start=(j == 0), stop=(j == 7))
                        last = e.matmul(pA[:, 512:1024], w1blk[:, j * 128:(j + 1) * 128],
                                        nfeT[:, j * 1024 + 512:(j + 1) * 1024],
                                        start=(j == 0), stop=(j == 7))
                    return last
                s.comp('tensor', mmw1)
                s.comp('scalar', lambda e, f=f: e.activation(
                    rh[:, f * 1024:(f + 1) * 1024], pA[:, 0:1024], AF.Relu,
                    bias=sb1c[:, f:f + 1]))
            for oh in range(2):
                ooff = oh * 512
                for f in range(32):
                    s.dma('gpsimd', [(w2blk[:], w2[ee, f * 128:(f + 1) * 128,
                                                   ooff:ooff + 512])])
                    def mmw2(e, f=f):
                        for tl in range(8):
                            ps = pA if tl < 4 else pY
                            last = e.matmul(ps[:, (tl % 4) * 512:(tl % 4 + 1) * 512],
                                            rh[:, f * 1024 + tl * 128:f * 1024 + (tl + 1) * 128],
                                            w2blk[:], start=(f == 0), stop=False)
                        return last
                    s.comp('tensor', mmw2)

                def mmb2(e, ee=ee, ooff=ooff):
                    for tl in range(8):
                        ps = pA if tl < 4 else pY
                        last = e.matmul(ps[:, (tl % 4) * 512:(tl % 4 + 1) * 512],
                                        ones_row[:], sb2[:, ooff:ooff + 512],
                                        start=False, stop=True)
                    return last
                s.comp('tensor', mmb2)
                for tl in range(8):
                    def acc(e, ee=ee, tl=tl, ooff=ooff, half=half):
                        ps = pA if tl < 4 else pY
                        tg = half * 8 + tl
                        return e.scalar_tensor_tensor(
                            yac[:, tl * 1024 + ooff:tl * 1024 + ooff + 512],
                            ps[:, (tl % 4) * 512:(tl % 4 + 1) * 512],
                            ohc[:, tg * NE + ee:tg * NE + ee + 1],
                            yac[:, tl * 1024 + ooff:tl * 1024 + ooff + 512],
                            ALU.mult, ALU.add)
                    s.comp('vector', acc)

        for tl in range(8):
            tg = half * 8 + tl
            s.dma('sync', [(ntile[:], xn_d[tg * 128:(tg + 1) * 128, :]),
                           (stile[:], s_d[tg * 128:(tg + 1) * 128, :]),
                           (xtile[:], x_d[tg * 128:(tg + 1) * 128, :])])
            def blend(e, tl=tl, tg=tg):
                return e.scalar_tensor_tensor(gtile[:], yac[:, tl * 1024:(tl + 1) * 1024],
                                              alc[:, tg:tg + 1], ntile[:], ALU.mult, ALU.add)
            s.comp('vector', blend)
            s.comp('scalar', lambda e: e.activation(gtile[:], gtile[:], AF.Silu))
            def resid(e):
                e.tensor_mul(gtile[:], gtile[:], stile[:])
                return e.tensor_add(xtile[:], xtile[:], gtile[:])
            s.comp('vector', resid)
            s.dma('sync', [(o_x1[tg * 128:(tg + 1) * 128, :], xtile[:])])

    return s.build()


def run_b2(inputs, static, r0, r1):
    nc = build_b2()
    x = inputs['L0_attn_x']
    in_maps = []
    for c in range(8):
        b, h = c // 2, c % 2
        m = {k: static[k] for k in ('cent_t', 'gamma_t', 'beta_t', 'w1_t', 'b1_c',
                                    'w2_t', 'b2_r')}
        m['xnT'] = np.asarray(r0[c]['xnT'])
        m['xn'] = np.asarray(r0[c]['xn'])
        m['x'] = np.ascontiguousarray(x[b, h * T:(h + 1) * T, :])
        m['s_attn'] = np.asarray(r1[c]['s_attn'])
        in_maps.append(m)
    res = run_bass_kernel_spmd(nc, in_maps, core_ids=list(range(8)))
    return res.results


def build_l1a():
    nc, s = mk_nc()
    x1 = nc.dram_tensor("x1", [T, D], F32, kind="ExternalInput").ap()
    x1h = nc.dram_tensor("x1h", [3, D], F32, kind="ExternalInput").ap()
    win = nc.dram_tensor("w_in", [D, DPROJ], F32, kind="ExternalInput").ap()
    cw = nc.dram_tensor("conv_wc", [128, 18, 4], F32, kind="ExternalInput").ap()
    cb = nc.dram_tensor("conv_bc", [128, 18], F32, kind="ExternalInput").ap()
    dtb = nc.dram_tensor("dtb_c", [32, 1], F32, kind="ExternalInput").ap()
    alog = nc.dram_tensor("alog_c", [32, 1], F32, kind="ExternalInput").ap()

    o_xn1 = nc.dram_tensor("xn1", [T, D], F32, kind="ExternalOutput").ap()
    o_xn1T = nc.dram_tensor("xn1T", [128, 8, T], BF16, kind="ExternalOutput").ap()
    o_z = nc.dram_tensor("z", [T, DIN], BF16, kind="ExternalOutput").ap()
    o_xsT = nc.dram_tensor("xsT", [128, 16, T], BF16, kind="ExternalOutput").ap()
    o_BT = nc.dram_tensor("BT", [128, T], BF16, kind="ExternalOutput").ap()
    o_CT = nc.dram_tensor("CT", [128, T], BF16, kind="ExternalOutput").ap()
    o_dt = nc.dram_tensor("dtsp", [32, T], F32, kind="ExternalOutput").ap()
    o_LT = nc.dram_tensor("LT", [32, T], F32, kind="ExternalOutput").ap()

    ident, ones_row, ones_col = setup_common(nc, s)

    sx = nc.alloc_sbuf_tensor("sx", [128, D], F32)
    dump = nc.alloc_sbuf_tensor("dump", [128, D], F32)
    stat = nc.alloc_sbuf_tensor("stat", [128, 8], F32)
    sxn_bf = nc.alloc_sbuf_tensor("sxn_bf", [128, D], BF16)
    xn1T = nc.alloc_sbuf_tensor("xn1T_s", [128, 8 * T], BF16)
    xnhT = nc.alloc_sbuf_tensor("xnhT", [128, 8 * 3], BF16)
    swin = nc.alloc_sbuf_tensor("swin", [128, 8 * DPROJ], BF16)
    ztile = nc.alloc_sbuf_tensor("ztile", [128, DIN], BF16)
    xbcpre = nc.alloc_sbuf_tensor("xbcpre", [128, 2051], F32)
    ctmp = nc.alloc_sbuf_tensor("ctmp", [128, T], F32)
    cbf = nc.alloc_sbuf_tensor("cbf", [128, T], BF16)
    scw = nc.alloc_sbuf_tensor("scw", [128, 18 * 4], F32)
    scb = nc.alloc_sbuf_tensor("scb", [128, 18], F32)
    sdtb = nc.alloc_sbuf_tensor("sdtb", [32, 1], F32)
    salog = nc.alloc_sbuf_tensor("salog", [32, 1], F32)
    dtraw = nc.alloc_sbuf_tensor("dtraw", [32, T], F32)
    dwork = nc.alloc_sbuf_tensor("dwork", [32, T], F32)
    dtsp = nc.alloc_sbuf_tensor("dtsp_s", [32, T], F32)
    expA = nc.alloc_sbuf_tensor("expA", [32, 1], F32)

    pA = nc.alloc_psum_tensor("pA", [128, 2048], F32)
    pB = nc.alloc_psum_tensor("pB", [128, 1024], F32)
    ptp = nc.alloc_psum_tensor("ptp", [128, 128], BF16)

    s.dma('gpsimd', [(swin.rearrange("p (j c) -> p j c", j=8),
                      win.rearrange("(j p) c -> p j c", p=128))])
    for ee in range(NE):
        for cc in range(4):
            s.dma('gpsimd', [(w1nb_scr[ee, cc * 256:(cc + 1) * 256, :],
                              w1n_d[ee, cc * 256:(cc + 1) * 256, :])], wait=[])
        for cc in range(4):
            s.dma('gpsimd', [(w2nb_scr[ee, cc * 1024:(cc + 1) * 1024, :],
                              w2n_d[ee, cc * 1024:(cc + 1) * 1024, :])], wait=[])
    for cc in range(4):
        s.dma('gpsimd', [(sw1b_scr[cc * 256:(cc + 1) * 256, :],
                          sw1_d[cc * 256:(cc + 1) * 256, :])], wait=[])
    for cc in range(4):
        s.dma('gpsimd', [(sw2b_scr[cc * 1024:(cc + 1) * 1024, :],
                          sw2_d[cc * 1024:(cc + 1) * 1024, :])], wait=[])
    s.dma('sync', [(scw.rearrange("p (i k) -> p i k", i=18), cw),
                   (scb[:], cb), (sdtb[:], dtb), (salog[:], alog)])
    s.comp('scalar', lambda e: e.activation(expA[:], salog[:], AF.Exp))

    for t in range(16):
        s.dma('sync', [(sx[:], x1[t * 128:(t + 1) * 128, :])])
        def stats_v(e):
            return e.tensor_scalar(dump[:], sx[:], 1.0, 0.0, ALU.mult, ALU.add,
                                   accum_out=stat[:, 0:1])
        s.comp('vector', stats_v)
        s.comp('scalar', lambda e: e.activation(dump[:], sx[:], AF.Square,
                                                accum_out=stat[:, 1:2]))
        def mv(e):
            e.tensor_scalar(stat[:, 2:3], stat[:, 0:1], 1.0 / D, None, ALU.mult)
            e.tensor_scalar(stat[:, 3:4], stat[:, 1:2], 1.0 / D, None, ALU.mult)
            e.tensor_scalar(stat[:, 4:5], stat[:, 2:3], stat[:, 2:3], None, ALU.mult)
            return e.tensor_sub(stat[:, 3:4], stat[:, 3:4], stat[:, 4:5])
        s.comp('vector', mv)
        def rstd(e):
            e.activation(stat[:, 4:5], stat[:, 3:4], AF.Ln, bias=s.const(EPS))
            return e.activation(stat[:, 4:5], stat[:, 4:5], AF.Exp, scale=s.const(-0.5))
        s.comp('scalar', rstd)
        def norm(e):
            e.tensor_scalar(sx[:], sx[:], stat[:, 2:3], stat[:, 4:5], ALU.subtract, ALU.mult)
            return e.tensor_copy(sxn_bf[:], sx[:])
        s.comp('vector', norm)
        s.dma('sync', [(o_xn1[t * 128:(t + 1) * 128, :], sx[:])])
        for j in range(8):
            s.comp('tensor', lambda e, j=j: e.transpose(ptp[:], sxn_bf[:, j * 128:(j + 1) * 128], ident[:]))
            s.comp('vector', lambda e, t=t, j=j: e.tensor_copy(
                xn1T[:, j * T + t * 128: j * T + (t + 1) * 128], ptp[:]))
    s.dma('sync', [(o_xn1T.rearrange("p j t -> p (j t)"), xn1T[:])])

    sxh = nc.alloc_sbuf_tensor("sxh", [3, D], F32)
    sxh_bf = nc.alloc_sbuf_tensor("sxh_bf", [3, D], BF16)
    hstat = nc.alloc_sbuf_tensor("hstat", [3, 8], F32)
    s.dma('sync', [(sxh[:], x1h)])
    def hstats_v(e):
        return e.tensor_scalar(dump[0:3, :], sxh[:], 1.0, 0.0, ALU.mult, ALU.add,
                               accum_out=hstat[:, 0:1])
    s.comp('vector', hstats_v)
    s.comp('scalar', lambda e: e.activation(dump[0:3, :], sxh[:], AF.Square,
                                            accum_out=hstat[:, 1:2]))
    def hmv(e):
        e.tensor_scalar(hstat[:, 2:3], hstat[:, 0:1], 1.0 / D, None, ALU.mult)
        e.tensor_scalar(hstat[:, 3:4], hstat[:, 1:2], 1.0 / D, None, ALU.mult)
        e.tensor_scalar(hstat[:, 4:5], hstat[:, 2:3], hstat[:, 2:3], None, ALU.mult)
        return e.tensor_sub(hstat[:, 3:4], hstat[:, 3:4], hstat[:, 4:5])
    s.comp('vector', hmv)
    def hrstd(e):
        e.activation(hstat[:, 4:5], hstat[:, 3:4], AF.Ln, bias=s.const(EPS, 3))
        return e.activation(hstat[:, 4:5], hstat[:, 4:5], AF.Exp, scale=s.const(-0.5, 3))
    s.comp('scalar', hrstd)
    def hnorm(e):
        e.tensor_scalar(sxh[:], sxh[:], hstat[:, 2:3], hstat[:, 4:5], ALU.subtract, ALU.mult)
        return e.tensor_copy(sxh_bf[:], sxh[:])
    s.comp('vector', hnorm)
    for j in range(8):
        s.comp('tensor', lambda e, j=j: e.transpose(ptp[:, 0:3],
                                                    sxh_bf[:, j * 128:(j + 1) * 128],
                                                    ident[0:3, 0:3]))
        s.comp('vector', lambda e, j=j: e.tensor_copy(xnhT[:, j * 3:(j + 1) * 3], ptp[:, 0:3]))

    for t in range(16):
        def zproj(e, t=t):
            for j in range(8):
                lhsT = xn1T[:, j * T + t * 128: j * T + (t + 1) * 128]
                for g in range(4):
                    last = e.matmul(pA[:, g * 512:(g + 1) * 512], lhsT,
                                    swin[:, j * DPROJ + g * 512: j * DPROJ + (g + 1) * 512],
                                    start=(j == 0), stop=(j == 7))
            return last
        s.comp('tensor', zproj)
        s.comp('vector', lambda e: e.tensor_copy(ztile[:], pA[:, 0:2048]))
        s.dma('sync', [(o_z[t * 128:(t + 1) * 128, :], ztile[:])])

    for i in range(18):
        coff = DIN + i * 128
        for grp in range(2):
            def xbcmm2(e, i=i, coff=coff, grp=grp):
                for j in range(8):
                    lhsT = swin[:, j * DPROJ + coff: j * DPROJ + coff + 128]
                    e.matmul(pB[:, 0:512], lhsT,
                             xn1T[:, j * T + grp * 1024: j * T + grp * 1024 + 512],
                             start=(j == 0), stop=(j == 7))
                    last = e.matmul(pB[:, 512:1024], lhsT,
                                    xn1T[:, j * T + grp * 1024 + 512: j * T + (grp + 1) * 1024],
                                    start=(j == 0), stop=(j == 7))
                return last
            s.comp('tensor', xbcmm2)
            s.comp('vector', lambda e, grp=grp: e.tensor_copy(
                xbcpre[:, 3 + grp * 1024: 3 + (grp + 1) * 1024], pB[:, 0:1024]))
        def xbch(e, i=i, coff=coff):
            for j in range(8):
                last = e.matmul(pB[:, 0:3], swin[:, j * DPROJ + coff: j * DPROJ + coff + 128],
                                xnhT[:, j * 3:(j + 1) * 3], start=(j == 0), stop=(j == 7))
            return last
        s.comp('tensor', xbch)
        s.comp('vector', lambda e: e.tensor_copy(xbcpre[:, 0:3], pB[:, 0:3]))
        def conv2(e, i=i):
            e.tensor_scalar(ctmp[:], xbcpre[:, 0:T], scw[:, i * 4:i * 4 + 1], None, ALU.mult)
            e.scalar_tensor_tensor(ctmp[:], xbcpre[:, 1:1 + T], scw[:, i * 4 + 1:i * 4 + 2],
                                   ctmp[:], ALU.mult, ALU.add)
            e.scalar_tensor_tensor(ctmp[:], xbcpre[:, 2:2 + T], scw[:, i * 4 + 2:i * 4 + 3],
                                   ctmp[:], ALU.mult, ALU.add)
            return e.scalar_tensor_tensor(ctmp[:], xbcpre[:, 3:3 + T], scw[:, i * 4 + 3:i * 4 + 4],
                                          ctmp[:], ALU.mult, ALU.add)
        s.comp('vector', conv2)
        s.comp('scalar', lambda e, i=i: e.activation(cbf[:], ctmp[:], AF.Silu,
                                                     bias=scb[:, i:i + 1]))
        if i < 16:
            s.dma('sync', [(o_xsT[:, i, :], cbf[:])])
        elif i == 16:
            s.dma('sync', [(o_BT, cbf[:])])
        else:
            s.dma('sync', [(o_CT, cbf[:])])

    for grp in range(2):
        def dtmm(e, grp=grp):
            for j in range(8):
                lhsT = swin[:, j * DPROJ + DIN + CONVD: j * DPROJ + DPROJ]
                e.matmul(pB[0:32, 0:512], lhsT,
                         xn1T[:, j * T + grp * 1024: j * T + grp * 1024 + 512],
                         start=(j == 0), stop=(j == 7))
                last = e.matmul(pB[0:32, 512:1024], lhsT,
                                xn1T[:, j * T + grp * 1024 + 512: j * T + (grp + 1) * 1024],
                                start=(j == 0), stop=(j == 7))
            return last
        s.comp('tensor', dtmm)
        s.comp('vector', lambda e, grp=grp: e.tensor_copy(
            dtraw[:, grp * 1024:(grp + 1) * 1024], pB[0:32, 0:1024]))
    def sp1(e):
        e.tensor_scalar(dtraw[:], dtraw[:], sdtb[:], None, ALU.add)
        return e.tensor_scalar(dtsp[:], dtraw[:], 0.0, None, ALU.max)
    s.comp('vector', sp1)
    s.comp('scalar', lambda e: e.activation(dwork[:], dtraw[:], AF.Abs))
    s.comp('scalar', lambda e: e.activation(dwork[:], dwork[:], AF.Exp, scale=s.const(-1.0, 32)))
    s.comp('scalar', lambda e: e.activation(dwork[:], dwork[:], AF.Ln, bias=s.const(1.0, 32)))
    s.comp('vector', lambda e: e.tensor_add(dtsp[:], dtsp[:], dwork[:]))
    s.dma('sync', [(o_dt, dtsp[:])])
    def dta(e):
        e.tensor_scalar(dwork[:], dtsp[:], expA[:], -1.0, ALU.mult, ALU.mult)
        return e.tensor_tensor_scan(dwork[:], dwork[:], dwork[:], 0.0, ALU.add, ALU.bypass)
    s.comp('vector', dta)
    s.dma('sync', [(o_LT, dwork[:])])
    return s.build()


def run_l1a(inputs, static, r2):
    nc = build_l1a()
    in_maps = []
    for c in range(8):
        h = c % 2
        m = {k: static[k] for k in ('w_in', 'conv_wc', 'conv_bc', 'dtb_c', 'alog_c')}
        m['x1'] = np.asarray(r2[c]['x1'])
        if h == 0:
            m['x1h'] = np.zeros((3, D), np.float32)
        else:
            m['x1h'] = np.ascontiguousarray(np.asarray(r2[c - 1]['x1'])[-3:, :])
        in_maps.append(m)
    res = run_bass_kernel_spmd(nc, in_maps, core_ids=list(range(8)))
    return res.results


def build_l1b():
    nc, s = mk_nc()
    xsT_d = nc.dram_tensor("xsT", [128, 16, T], BF16, kind="ExternalInput").ap()
    BT_d = nc.dram_tensor("BT", [128, T], BF16, kind="ExternalInput").ap()
    CT_d = nc.dram_tensor("CT", [128, T], BF16, kind="ExternalInput").ap()
    dt_d = nc.dram_tensor("dtsp", [32, T], F32, kind="ExternalInput").ap()
    LT_d = nc.dram_tensor("LT", [32, T], F32, kind="ExternalInput").ap()
    alogt = nc.dram_tensor("alog_tile", [128, 32], F32, kind="ExternalInput").ap()
    dskt = nc.dram_tensor("dskip_tile", [128, T], F32, kind="ExternalInput").ap()
    o_y = nc.dram_tensor("ylocal", [T, DIN], F32, kind="ExternalOutput").ap()
    o_h = nc.dram_tensor("h_end", [128, DIN], F32, kind="ExternalOutput").ap()

    ident, ones_row, ones_col = setup_common(nc, s)
    identf = nc.alloc_sbuf_tensor("identf", [128, 128], F32)
    s.comp('gpsimd', lambda e: e.memset(identf[:], 1.0))
    s.comp('gpsimd', lambda e: e.affine_select(identf[:], identf[:], pattern=[[1, 128]],
                                               base=0, channel_multiplier=-1,
                                               compare_op=ALU.is_equal, fill=0.0))
    allonesf = nc.alloc_sbuf_tensor("allonesf", [128, 128], F32)
    s.comp('gpsimd', lambda e: e.memset(allonesf[:], 1.0))
    trimask = nc.alloc_sbuf_tensor("trimask", [128, 128], F32)
    s.comp('gpsimd', lambda e: e.memset(trimask[:], 0.0))
    s.comp('gpsimd', lambda e: e.affine_select(trimask[:], trimask[:], pattern=[[1, 128]],
                                               base=0, channel_multiplier=-1,
                                               compare_op=ALU.is_ge, fill=-1e30))
    ehsel = nc.alloc_sbuf_tensor("ehsel", [32, 32 * 128], F32)
    s.comp('gpsimd', lambda e: e.memset(ehsel[:], 1.0))
    s.comp('gpsimd', lambda e: e.affine_select(ehsel[:], ehsel[:],
                                               pattern=[[-1, 32], [0, 128]], base=0,
                                               channel_multiplier=1,
                                               compare_op=ALU.is_equal, fill=0.0))

    xsT = nc.alloc_sbuf_tensor("xsT_s", [128, 16 * T], BF16)
    sBT = nc.alloc_sbuf_tensor("sBT", [128, T], BF16)
    sCT = nc.alloc_sbuf_tensor("sCT", [128, T], BF16)
    sdt = nc.alloc_sbuf_tensor("sdt", [32, T], F32)
    sLT = nc.alloc_sbuf_tensor("sLT", [32, T], F32)
    expAb = nc.alloc_sbuf_tensor("expAb", [128, 32], F32)
    sdsk = nc.alloc_sbuf_tensor("sdsk", [128, T], F32)
    h_sb = nc.alloc_sbuf_tensor("h_sb", [128, DIN], F32)
    h_bf = nc.alloc_sbuf_tensor("h_bf", [128, DIN], BF16)
    Y_sb = nc.alloc_sbuf_tensor("Y_sb", [128, DIN], F32)
    xs_tok = nc.alloc_sbuf_tensor("xs_tok", [128, DIN], BF16)
    Xd_bf = nc.alloc_sbuf_tensor("Xd_bf", [128, DIN], BF16)
    Xe_bf = nc.alloc_sbuf_tensor("Xe_bf", [128, DIN], BF16)
    B_tok = nc.alloc_sbuf_tensor("B_tok", [128, 128], BF16)
    G_sb = nc.alloc_sbuf_tensor("G_sb", [128, 128], BF16)
    Dm_sb = nc.alloc_sbuf_tensor("Dm_sb", [128, 128], F32)
    E_sb = nc.alloc_sbuf_tensor("E_sb", [128, 128], F32)
    MT_bf = nc.alloc_sbuf_tensor("MT_bf", [128, 128], BF16)
    aT = nc.alloc_sbuf_tensor("aT", [32, 128], F32)
    negaT = nc.alloc_sbuf_tensor("negaT", [32, 128], F32)
    expaT = nc.alloc_sbuf_tensor("expaT", [32, 128], F32)
    exprT = nc.alloc_sbuf_tensor("exprT", [32, 128], F32)
    dt_tok = nc.alloc_sbuf_tensor("dt_tok", [128, 32], F32)
    expa_tok = nc.alloc_sbuf_tensor("expa_tok", [128, 32], F32)
    expr_tok = nc.alloc_sbuf_tensor("expr_tok", [128, 32], F32)
    dtE = nc.alloc_sbuf_tensor("dtE", [128, 32], F32)
    lamb = nc.alloc_sbuf_tensor("lamb", [128, 32], F32)
    tws = nc.alloc_sbuf_tensor("tws", [128, DIN], F32)

    pBig = nc.alloc_psum_tensor("pBig", [128, 2048], F32)
    pG = nc.alloc_psum_tensor("pG", [128, 128], F32)
    pD = nc.alloc_psum_tensor("pD", [128, 128], F32)
    ptpb = nc.alloc_psum_tensor("ptpb", [128, 128], BF16)
    ptpf = nc.alloc_psum_tensor("ptpf", [128, 128], F32)

    s.dma('sync', [(xsT.rearrange("p (i t) -> p i t", i=16), xsT_d),
                   (sBT[:], BT_d), (sCT[:], CT_d), (sdt[:], dt_d), (sLT[:], LT_d),
                   (sdsk[:], dskt)])
    s.dma('sync', [(expAb[:], alogt)])
    s.comp('scalar', lambda e: e.activation(expAb[:], expAb[:], AF.Exp))
    s.comp('vector', lambda e: e.memset(h_sb[:], 0.0))

    for c in range(16):
        ck = c * 128
        def prep_a(e, c=c, ck=ck):
            if c == 0:
                e.tensor_copy(aT[:], sLT[:, ck:ck + 128])
            else:
                e.tensor_scalar(aT[:], sLT[:, ck:ck + 128], sLT[:, ck - 1:ck],
                                None, ALU.subtract)
            e.tensor_scalar(negaT[:], aT[:], -1.0, None, ALU.mult)
            return e.tensor_scalar(exprT[:], sLT[:, ck:ck + 128],
                                   sLT[:, ck + 127:ck + 128], None, ALU.subtract)
        s.comp('vector', prep_a)

        def prep_exp(e):
            e.activation(expaT[:], aT[:], AF.Exp)
            return e.activation(exprT[:], exprT[:], AF.Exp, scale=s.const(-1.0, 32))
        s.comp('scalar', prep_exp)

        for srcT, dstT in ((sdt, dt_tok), (expaT, expa_tok), (exprT, expr_tok)):
            if srcT is sdt:
                s.comp('tensor', lambda e, ck=ck: e.transpose(
                    ptpf[:, 0:32], sdt[:, ck:ck + 128], identf[0:32, 0:32]))
            elif srcT is expaT:
                s.comp('tensor', lambda e: e.transpose(ptpf[:, 0:32], expaT[:], identf[0:32, 0:32]))
            else:
                s.comp('tensor', lambda e: e.transpose(ptpf[:, 0:32], exprT[:], identf[0:32, 0:32]))
            s.comp('vector', lambda e, dstT=dstT: e.tensor_copy(dstT[:], ptpf[:, 0:32]))

        def lam1(e):
            return e.tensor_mul(dtE[:], dt_tok[:], expAb[:])
        s.comp('vector', lam1)
        s.comp('tensor', lambda e: e.matmul(ptpf[:, 0:32], allonesf[:], dtE[:],
                                            start=True, stop=True))
        s.comp('scalar', lambda e: e.activation(lamb[:], ptpf[:, 0:32], AF.Exp,
                                                scale=s.const(-1.0)))

        for i in range(16):
            s.comp('tensor', lambda e, i=i, ck=ck: e.transpose(
                ptpb[:], xsT[:, i * T + ck: i * T + ck + 128], ident[:]))
            s.comp('vector', lambda e, i=i: e.tensor_copy(
                xs_tok[:, i * 128:(i + 1) * 128], ptpb[:]))

        def mk_dtE2(e):
            return e.tensor_mul(dtE[:], dt_tok[:], expr_tok[:])
        s.comp('vector', mk_dtE2)
        for h in range(32):
            s.comp('vector', lambda e, h=h: e.tensor_scalar(
                Xd_bf[:, h * 64:(h + 1) * 64], xs_tok[:, h * 64:(h + 1) * 64],
                dt_tok[:, h:h + 1], None, ALU.mult))
            s.comp('vector', lambda e, h=h: e.tensor_scalar(
                Xe_bf[:, h * 64:(h + 1) * 64], xs_tok[:, h * 64:(h + 1) * 64],
                dtE[:, h:h + 1], None, ALU.mult))

        s.comp('tensor', lambda e, ck=ck: e.matmul(pG[:], sBT[:, ck:ck + 128],
                                                   sCT[:, ck:ck + 128], start=True, stop=True))
        s.comp('vector', lambda e: e.tensor_copy(G_sb[:], pG[:]))

        for h in range(32):
            def mmD(e, h=h):
                e.matmul(pD[:], ehsel[:, h * 128:(h + 1) * 128], aT[:], start=True, stop=False)
                return e.matmul(pD[:], negaT[:], ehsel[:, h * 128:(h + 1) * 128],
                                start=False, stop=True)
            s.comp('tensor', mmD)
            s.comp('vector', lambda e: e.tensor_add(Dm_sb[:], pD[:], trimask[:]))
            s.comp('scalar', lambda e: e.activation(E_sb[:], Dm_sb[:], AF.Exp))
            s.comp('vector', lambda e: e.tensor_mul(MT_bf[:], E_sb[:], G_sb[:]))
            s.comp('tensor', lambda e, h=h: e.matmul(
                pBig[:, h * 64:(h + 1) * 64], MT_bf[:], Xd_bf[:, h * 64:(h + 1) * 64],
                start=True, stop=True))

        def ydsk(e, ck=ck):
            e.tensor_mul(tws[:], xs_tok[:], sdsk[:, 0:DIN] if False else sdsk[:, 0:DIN])
            return e.tensor_add(Y_sb[:], pBig[:, 0:DIN], tws[:])
        s.comp('vector', ydsk)

        s.comp('vector', lambda e: e.tensor_copy(h_bf[:], h_sb[:]))
        def mmU(e, ck=ck):
            for g in range(4):
                last = e.matmul(pBig[:, g * 512:(g + 1) * 512], sCT[:, ck:ck + 128],
                                h_bf[:, g * 512:(g + 1) * 512], start=True, stop=True)
            return last
        s.comp('tensor', mmU)
        for h in range(32):
            s.comp('vector', lambda e, h=h: e.scalar_tensor_tensor(
                Y_sb[:, h * 64:(h + 1) * 64], pBig[:, h * 64:(h + 1) * 64],
                expa_tok[:, h:h + 1], Y_sb[:, h * 64:(h + 1) * 64], ALU.mult, ALU.add))
        s.dma('sync', [(o_y[ck:ck + 128, :], Y_sb[:])])

        s.comp('tensor', lambda e, ck=ck: e.transpose(ptpb[:], sBT[:, ck:ck + 128], ident[:]))
        s.comp('vector', lambda e: e.tensor_copy(B_tok[:], ptpb[:]))
        def mmH(e):
            for g in range(4):
                last = e.matmul(pBig[:, g * 512:(g + 1) * 512], B_tok[:],
                                Xe_bf[:, g * 512:(g + 1) * 512], start=True, stop=True)
            return last
        s.comp('tensor', mmH)
        for h in range(32):
            s.comp('vector', lambda e, h=h: e.scalar_tensor_tensor(
                h_sb[:, h * 64:(h + 1) * 64], h_sb[:, h * 64:(h + 1) * 64],
                lamb[:, h:h + 1], pBig[:, h * 64:(h + 1) * 64], ALU.mult, ALU.add))

    s.dma('sync', [(o_h, h_sb[:])])
    return s.build()


def run_l1b(inputs, static, r3):
    nc = build_l1b()
    in_maps = []
    for c in range(8):
        m = {'xsT': np.asarray(r3[c]['xsT']), 'BT': np.asarray(r3[c]['BT']),
             'CT': np.asarray(r3[c]['CT']), 'dtsp': np.asarray(r3[c]['dtsp']),
             'LT': np.asarray(r3[c]['LT']),
             'alog_tile': static['alog_tile'], 'dskip_tile': static['dskip_tile']}
        in_maps.append(m)
    res = run_bass_kernel_spmd(nc, in_maps, core_ids=list(range(8)))
    return res.results


def build_l1c():
    nc, s = mk_nc()
    yl_d = nc.dram_tensor("ylocal", [T, DIN], F32, kind="ExternalInput").ap()
    hin_d = nc.dram_tensor("h_in", [128, DIN], F32, kind="ExternalInput").ap()
    CT_d = nc.dram_tensor("CT", [128, T], BF16, kind="ExternalInput").ap()
    LT_d = nc.dram_tensor("LT", [32, T], F32, kind="ExternalInput").ap()
    z_d = nc.dram_tensor("z", [T, DIN], BF16, kind="ExternalInput").ap()
    xn1_d = nc.dram_tensor("xn1", [T, D], F32, kind="ExternalInput").ap()
    rmsw_d = nc.dram_tensor("rmsw_tile", [128, DIN], F32, kind="ExternalInput").ap()
    wout_d = nc.dram_tensor("w_out", [DIN, D], F32, kind="ExternalInput").ap()
    glw_d = nc.dram_tensor("glu_w", [D, DIN], F32, kind="ExternalInput").ap()
    glb_d = nc.dram_tensor("glu_b_row", [1, DIN], F32, kind="ExternalInput").ap()
    lnw_d = nc.dram_tensor("lnw_tile", [128, D], F32, kind="ExternalInput").ap()
    lnb_d = nc.dram_tensor("lnb_tile", [128, D], F32, kind="ExternalInput").ap()
    o_s1 = nc.dram_tensor("s1", [T, D], F32, kind="ExternalOutput").ap()

    ident, ones_row, ones_col = setup_common(nc, s)
    identf = nc.alloc_sbuf_tensor("identf", [128, 128], F32)
    s.comp('gpsimd', lambda e: e.memset(identf[:], 1.0))
    s.comp('gpsimd', lambda e: e.affine_select(identf[:], identf[:], pattern=[[1, 128]],
                                               base=0, channel_multiplier=-1,
                                               compare_op=ALU.is_equal, fill=0.0))

    sCT = nc.alloc_sbuf_tensor("sCT", [128, T], BF16)
    sLT = nc.alloc_sbuf_tensor("sLT", [32, T], F32)
    expg = nc.alloc_sbuf_tensor("expg", [32, T], F32)
    expg_tok = nc.alloc_sbuf_tensor("expg_tok", [128, 32], F32)
    hin = nc.alloc_sbuf_tensor("hin", [128, DIN], F32)
    hin_bf = nc.alloc_sbuf_tensor("hin_bf", [128, DIN], BF16)
    Yt = nc.alloc_sbuf_tensor("Yt", [128, DIN], F32)
    zt = nc.alloc_sbuf_tensor("zt", [128, DIN], BF16)
    zs = nc.alloc_sbuf_tensor("zs", [128, DIN], BF16)
    stat = nc.alloc_sbuf_tensor("stat", [128, 8], F32)
    dump = nc.alloc_sbuf_tensor("dump", [128, DIN], F32)
    srms = nc.alloc_sbuf_tensor("srms", [128, DIN], F32)
    Ynb = nc.alloc_sbuf_tensor("Ynb", [128, DIN], BF16)
    YnT = nc.alloc_sbuf_tensor("YnT", [128, 16 * 128], BF16)
    swo = nc.alloc_sbuf_tensor("swo", [128, 16 * D], BF16)
    sglw = nc.alloc_sbuf_tensor("sglw", [128, 8 * DIN], BF16)
    sglb = nc.alloc_sbuf_tensor("sglb", [1, DIN], BF16)
    slnw = nc.alloc_sbuf_tensor("slnw", [128, D], F32)
    slnb = nc.alloc_sbuf_tensor("slnb", [128, D], F32)
    xn1t = nc.alloc_sbuf_tensor("xn1t", [128, D], F32)
    y1 = nc.alloc_sbuf_tensor("y1", [128, D], F32)
    y1bf = nc.alloc_sbuf_tensor("y1bf", [128, D], BF16)
    y1T = nc.alloc_sbuf_tensor("y1T", [128, 8 * 128], BF16)
    sgat = nc.alloc_sbuf_tensor("sgat", [128, D], F32)

    pBig = nc.alloc_psum_tensor("pBig", [128, 2048], F32)
    pW = nc.alloc_psum_tensor("pW", [128, 1024], F32)
    ptpb = nc.alloc_psum_tensor("ptpb", [128, 128], BF16)
    ptpf = nc.alloc_psum_tensor("ptpf", [128, 128], F32)

    s.dma('sync', [(sCT[:], CT_d), (sLT[:], LT_d), (hin[:], hin_d),
                   (slnw[:], lnw_d), (slnb[:], lnb_d), (srms[:], rmsw_d)])
    s.dma('gpsimd', [(swo.rearrange("p (j d) -> p j d", j=16),
                      wout_d.rearrange("(j p) d -> p j d", p=128)),
                     (sglw.rearrange("p (j c) -> p j c", j=8),
                      glw_d.rearrange("(j p) c -> p j c", p=128)),
                     (sglb[:], glb_d)])
    s.comp('vector', lambda e: e.tensor_copy(hin_bf[:], hin[:]))
    s.comp('scalar', lambda e: e.activation(expg[:], sLT[:], AF.Exp))

    for t in range(16):
        tk = t * 128
        s.dma('sync', [(Yt[:], yl_d[tk:tk + 128, :]), (zt[:], z_d[tk:tk + 128, :]),
                       (xn1t[:], xn1_d[tk:tk + 128, :])])
        s.comp('tensor', lambda e, tk=tk: e.transpose(ptpf[:, 0:32], expg[:, tk:tk + 128],
                                                      identf[0:32, 0:32]))
        s.comp('vector', lambda e: e.tensor_copy(expg_tok[:], ptpf[:, 0:32]))
        def mmU(e, tk=tk):
            for g in range(4):
                last = e.matmul(pBig[:, g * 512:(g + 1) * 512], sCT[:, tk:tk + 128],
                                hin_bf[:, g * 512:(g + 1) * 512], start=True, stop=True)
            return last
        s.comp('tensor', mmU)
        for h in range(32):
            s.comp('vector', lambda e, h=h: e.scalar_tensor_tensor(
                Yt[:, h * 64:(h + 1) * 64], pBig[:, h * 64:(h + 1) * 64],
                expg_tok[:, h:h + 1], Yt[:, h * 64:(h + 1) * 64], ALU.mult, ALU.add))
        s.comp('scalar', lambda e: e.activation(zs[:], zt[:], AF.Silu))
        s.comp('vector', lambda e: e.tensor_mul(Yt[:], Yt[:], zs[:]))
        s.comp('scalar', lambda e: e.activation(dump[:], Yt[:], AF.Square,
                                                accum_out=stat[:, 0:1]))
        def rstd(e):
            e.activation(stat[:, 1:2], stat[:, 0:1], AF.Ln, bias=s.const(EPS),
                         scale=s.const(1.0 / DIN))
            return e.activation(stat[:, 1:2], stat[:, 1:2], AF.Exp, scale=s.const(-0.5))
        s.comp('scalar', rstd)
        def ynorm(e):
            e.tensor_scalar(dump[:], Yt[:], stat[:, 1:2], None, ALU.mult)
            return e.tensor_mul(Ynb[:], dump[:], srms[:])
        s.comp('vector', ynorm)
        for j in range(16):
            s.comp('tensor', lambda e, j=j: e.transpose(ptpb[:], Ynb[:, j * 128:(j + 1) * 128],
                                                        ident[:]))
            s.comp('vector', lambda e, j=j: e.tensor_copy(YnT[:, j * 128:(j + 1) * 128], ptpb[:]))
        def mmWo(e):
            for j in range(16):
                e.matmul(pW[:, 0:512], YnT[:, j * 128:(j + 1) * 128],
                         swo[:, j * D: j * D + 512], start=(j == 0), stop=(j == 15))
                last = e.matmul(pW[:, 512:1024], YnT[:, j * 128:(j + 1) * 128],
                                swo[:, j * D + 512: (j + 1) * D], start=(j == 0), stop=(j == 15))
            return last
        s.comp('tensor', mmWo)
        def mk_y1(e):
            e.tensor_add(y1[:], pW[:, 0:1024], xn1t[:])
            return e.tensor_copy(y1bf[:], y1[:])
        s.comp('vector', mk_y1)
        for j in range(8):
            s.comp('tensor', lambda e, j=j: e.transpose(ptpb[:], y1bf[:, j * 128:(j + 1) * 128],
                                                        ident[:]))
            s.comp('vector', lambda e, j=j: e.tensor_copy(y1T[:, j * 128:(j + 1) * 128], ptpb[:]))
        def mmGlu(e):
            for j in range(8):
                for g in range(4):
                    e.matmul(pBig[:, g * 512:(g + 1) * 512], y1T[:, j * 128:(j + 1) * 128],
                             sglw[:, j * DIN + g * 512: j * DIN + (g + 1) * 512],
                             start=(j == 0), stop=False)
            for g in range(4):
                last = e.matmul(pBig[:, g * 512:(g + 1) * 512], ones_row[:],
                                sglb[:, g * 512:(g + 1) * 512], start=False, stop=True)
            return last
        s.comp('tensor', mmGlu)
        s.comp('scalar', lambda e: e.activation(sgat[:], pBig[:, 1024:2048], AF.Sigmoid))
        def mk_y2(e):
            e.tensor_mul(sgat[:], pBig[:, 0:1024], sgat[:])
            return e.tensor_add(y1[:], y1[:], sgat[:])
        s.comp('vector', mk_y2)
        def stats_v(e):
            return e.tensor_scalar(dump[:, 0:D], y1[:], 1.0, 0.0, ALU.mult, ALU.add,
                                   accum_out=stat[:, 2:3])
        s.comp('vector', stats_v)
        s.comp('scalar', lambda e: e.activation(dump[:, 0:D], y1[:], AF.Square,
                                                accum_out=stat[:, 3:4]))
        def mv(e):
            e.tensor_scalar(stat[:, 4:5], stat[:, 2:3], 1.0 / D, None, ALU.mult)
            e.tensor_scalar(stat[:, 5:6], stat[:, 3:4], 1.0 / D, None, ALU.mult)
            e.tensor_scalar(stat[:, 6:7], stat[:, 4:5], stat[:, 4:5], None, ALU.mult)
            return e.tensor_sub(stat[:, 5:6], stat[:, 5:6], stat[:, 6:7])
        s.comp('vector', mv)
        def rstd2(e):
            e.activation(stat[:, 6:7], stat[:, 5:6], AF.Ln, bias=s.const(EPS))
            return e.activation(stat[:, 6:7], stat[:, 6:7], AF.Exp, scale=s.const(-0.5))
        s.comp('scalar', rstd2)
        def fin(e):
            e.tensor_scalar(y1[:], y1[:], stat[:, 4:5], stat[:, 6:7], ALU.subtract, ALU.mult)
            e.tensor_mul(y1[:], y1[:], slnw[:])
            return e.tensor_add(y1[:], y1[:], slnb[:])
        s.comp('vector', fin)
        s.dma('sync', [(o_s1[tk:tk + 128, :], y1[:])])

    return s.build()


def run_l1c(inputs, static, r3, r4):
    nc = build_l1c()
    in_maps = []
    for c in range(8):
        h = c % 2
        m = {k: static[k] for k in ('rmsw_tile', 'w_out', 'glu_w', 'glu_b_row',
                                    'lnw_tile', 'lnb_tile')}
        m['ylocal'] = np.asarray(r4[c]['ylocal'])
        m['h_in'] = (np.zeros((128, DIN), np.float32) if h == 0
                     else np.asarray(r4[c - 1]['h_end']))
        m['CT'] = np.asarray(r3[c]['CT'])
        m['LT'] = np.asarray(r3[c]['LT'])
        m['z'] = np.asarray(r3[c]['z'])
        m['xn1'] = np.asarray(r3[c]['xn1'])
        in_maps.append(m)
    res = run_bass_kernel_spmd(nc, in_maps, core_ids=list(range(8)))
    return res.results


def build_l1d():
    nc, s = mk_nc()
    xn1T_d = nc.dram_tensor("xn1T", [128, 8, T], BF16, kind="ExternalInput").ap()
    s1_d = nc.dram_tensor("s1", [T, D], F32, kind="ExternalInput").ap()
    x1_d = nc.dram_tensor("x1", [T, D], F32, kind="ExternalInput").ap()
    gate_d = nc.dram_tensor("gate_t", [D, NE], F32, kind="ExternalInput").ap()
    w1_d = nc.dram_tensor("w1n", [NE, D, FF], F32, kind="ExternalInput").ap()
    b1_d = nc.dram_tensor("b1n_c", [NE, 128, 32], F32, kind="ExternalInput").ap()
    w2_d = nc.dram_tensor("w2n", [NE, FF, D], F32, kind="ExternalInput").ap()
    b2_d = nc.dram_tensor("b2n_r", [1, NE * D], F32, kind="ExternalInput").ap()
    sw1_d = nc.dram_tensor("sw1", [D, FF], F32, kind="ExternalInput").ap()
    sb1_d = nc.dram_tensor("sb1_c", [128, 32], F32, kind="ExternalInput").ap()
    sw2_d = nc.dram_tensor("sw2", [FF, D], F32, kind="ExternalInput").ap()
    sb2_d = nc.dram_tensor("sb2_row", [1, D], F32, kind="ExternalInput").ap()
    o_out = nc.dram_tensor("out", [T, D], F32, kind="ExternalOutput").ap()

    ident, ones_row, ones_col = setup_common(nc, s)
    identf = nc.alloc_sbuf_tensor("identf", [128, 128], F32)
    s.comp('gpsimd', lambda e: e.memset(identf[:], 1.0))
    s.comp('gpsimd', lambda e: e.affine_select(identf[:], identf[:], pattern=[[1, 128]],
                                               base=0, channel_multiplier=-1,
                                               compare_op=ALU.is_equal, fill=0.0))

    xn1T = nc.alloc_sbuf_tensor("xn1T_s", [128, 8 * T], BF16)
    sgt = nc.alloc_sbuf_tensor("sgt", [128, 8 * NE], BF16)
    lgT = nc.alloc_sbuf_tensor("lgT", [8, T], F32)
    lg = nc.alloc_sbuf_tensor("lg", [128, 16 * NE], F32)
    mwork = nc.alloc_sbuf_tensor("mwork", [128, 16 * NE], F32)
    comb = nc.alloc_sbuf_tensor("comb", [128, 16 * NE], F32)
    cwork = nc.alloc_sbuf_tensor("cwork", [128, 8], F32)
    onecol = nc.alloc_sbuf_tensor("onecol", [128, 1], F32)
    sb1c = nc.alloc_sbuf_tensor("sb1c", [128, 32], F32)
    sb2 = nc.alloc_sbuf_tensor("sb2", [1, D], BF16)
    w1blk = nc.alloc_sbuf_tensor("w1blk", [128, 8 * 128], BF16)
    w2blk = nc.alloc_sbuf_tensor("w2blk", [128, 512], BF16)
    rh = nc.alloc_sbuf_tensor("rh", [128, 32 * 1024], BF16)
    yac = nc.alloc_sbuf_tensor("yac", [128, 8 * 1024], F32)
    s1t = nc.alloc_sbuf_tensor("s1t", [128, D], F32)
    x1t = nc.alloc_sbuf_tensor("x1t", [128, D], F32)
    gt = nc.alloc_sbuf_tensor("gt", [128, D], F32)

    pA = nc.alloc_psum_tensor("pA", [128, 2048], F32)
    pY = nc.alloc_psum_tensor("pY", [128, 2048], F32)

    s.dma('sync', [(xn1T.rearrange("p (j t) -> p j t", j=8), xn1T_d),
                   (sb1c[:], sb1_d)])
    s.dma('gpsimd', [(sgt.rearrange("p (j e) -> p j e", j=8),
                      gate_d.rearrange("(j p) e -> p j e", p=128)),
                     (sb2[:], sb2_d)])
    s.comp('vector', lambda e: e.memset(onecol[:], 1.0))

    def mm_lg(e):
        for j in range(8):
            for g in range(4):
                last = e.matmul(pA[0:8, g * 512:(g + 1) * 512],
                                sgt[:, j * NE:(j + 1) * NE],
                                xn1T[:, j * T + g * 512:j * T + (g + 1) * 512],
                                start=(j == 0), stop=(j == 7))
        return last
    s.comp('tensor', mm_lg)
    s.comp('vector', lambda e: e.tensor_copy(lgT[:], pA[0:8, 0:T]))
    for t in range(16):
        s.comp('tensor', lambda e, t=t: e.transpose(
            pA[:, 1024:1024 + 8], lgT[:, t * 128:(t + 1) * 128], identf[0:8, 0:8]))
        s.comp('vector', lambda e, t=t: e.tensor_copy(lg[:, t * NE:(t + 1) * NE],
                                                      pA[:, 1024:1024 + 8]))
    tk1 = [None] * 16
    tk2 = [None] * 16
    tk3 = [None] * 16
    for t in range(16):
        def topk(e, t=t):
            L = lg[:, t * NE:(t + 1) * NE]
            M = mwork[:, t * NE:(t + 1) * NE]
            e.reduce_max(cwork[:, 0:1], L, axis=AX.X)
            e.tensor_scalar(M, L, cwork[:, 0:1], None, ALU.is_ge)
            e.scalar_tensor_tensor(M, M, -1e30, L, ALU.mult, ALU.add)
            e.reduce_max(cwork[:, 1:2], M, axis=AX.X)
            e.tensor_scalar(cwork[:, 2:3], cwork[:, 0:1], -1.0, None, ALU.mult)
            return e.tensor_sub(cwork[:, 3:4], cwork[:, 1:2], cwork[:, 0:1])
        tk1[t] = s.comp('vector', topk,
                        wait=tw0(tk2[t - 1] if t >= 1 else None))
        def topk2(e, t=t):
            e.activation(mwork[:, t * NE:(t + 1) * NE], lg[:, t * NE:(t + 1) * NE],
                         AF.Exp, bias=cwork[:, 2:3])
            return e.activation(cwork[:, 4:5], cwork[:, 3:4], AF.Exp)
        tk2[t] = s.comp('scalar', topk2, wait=[tk1[t]])
        def topk3(e, t=t):
            L = lg[:, t * NE:(t + 1) * NE]
            e.tensor_scalar(cwork[:, 5:6], cwork[:, 4:5], 1.0, None, ALU.add)
            e.reciprocal(cwork[:, 6:7], cwork[:, 5:6])
            e.tensor_scalar(comb[:, t * NE:(t + 1) * NE], L, cwork[:, 1:2], None, ALU.is_ge)
            e.tensor_mul(comb[:, t * NE:(t + 1) * NE], comb[:, t * NE:(t + 1) * NE],
                         mwork[:, t * NE:(t + 1) * NE])
            return e.tensor_scalar(comb[:, t * NE:(t + 1) * NE], comb[:, t * NE:(t + 1) * NE],
                                   cwork[:, 6:7], None, ALU.mult)
        tk3[t] = s.comp('vector', topk3, wait=[tk2[t]])

    for half in range(2):
        hoff = half * 1024
        s.comp('vector', lambda e: e.memset(yac[:], 0.0))
        for ee in range(NE + 1):
            shared = (ee == NE)
            if not shared:
                s.dma('sync', [(sb1c[:], b1_d[ee, :, :])])
                s.dma('gpsimd', [(sb2[:], b2_d[0:1, ee * D:(ee + 1) * D])])
            else:
                s.dma('sync', [(sb1c[:], sb1_d)])
                s.dma('gpsimd', [(sb2[:], sb2_d)])
            for f in range(32):
                if not shared:
                    s.dma('gpsimd', [(w1blk.rearrange("p (j c) -> p j c", j=8),
                                      w1_d[ee].rearrange("(j p) c -> p j c", p=128)[:, :, f * 128:(f + 1) * 128])])
                else:
                    s.dma('gpsimd', [(w1blk.rearrange("p (j c) -> p j c", j=8),
                                      sw1_d.rearrange("(j p) c -> p j c", p=128)[:, :, f * 128:(f + 1) * 128])])
                def mmw1(e, f=f, hoff=hoff):
                    for j in range(8):
                        e.matmul(pA[:, 0:512], w1blk[:, j * 128:(j + 1) * 128],
                                 xn1T[:, j * T + hoff:j * T + hoff + 512],
                                 start=(j == 0), stop=(j == 7))
                        last = e.matmul(pA[:, 512:1024], w1blk[:, j * 128:(j + 1) * 128],
                                        xn1T[:, j * T + hoff + 512:j * T + hoff + 1024],
                                        start=(j == 0), stop=(j == 7))
                    return last
                s.comp('tensor', mmw1)
                s.comp('scalar', lambda e, f=f: e.activation(
                    rh[:, f * 1024:(f + 1) * 1024], pA[:, 0:1024], AF.Gelu,
                    bias=sb1c[:, f:f + 1]))
            for oh in range(2):
                ooff = oh * 512
                for f in range(32):
                    if not shared:
                        s.dma('gpsimd', [(w2blk[:], w2_d[ee, f * 128:(f + 1) * 128,
                                                         ooff:ooff + 512])])
                    else:
                        s.dma('gpsimd', [(w2blk[:], sw2_d[f * 128:(f + 1) * 128,
                                                          ooff:ooff + 512])])
                    def mmw2(e, f=f):
                        for tl in range(8):
                            ps = pA if tl < 4 else pY
                            last = e.matmul(ps[:, (tl % 4) * 512:(tl % 4 + 1) * 512],
                                            rh[:, f * 1024 + tl * 128:f * 1024 + (tl + 1) * 128],
                                            w2blk[:], start=(f == 0), stop=False)
                        return last
                    s.comp('tensor', mmw2)
                def mmb2(e, ooff=ooff):
                    for tl in range(8):
                        ps = pA if tl < 4 else pY
                        last = e.matmul(ps[:, (tl % 4) * 512:(tl % 4 + 1) * 512],
                                        ones_row[:], sb2[:, ooff:ooff + 512],
                                        start=False, stop=True)
                    return last
                s.comp('tensor', mmb2)
                for tl in range(8):
                    def acc(e, ee=ee, tl=tl, ooff=ooff, half=half, shared=shared):
                        ps = pA if tl < 4 else pY
                        tg = half * 8 + tl
                        ccol = onecol[:, 0:1] if shared else comb[:, tg * NE + ee:tg * NE + ee + 1]
                        return e.scalar_tensor_tensor(
                            yac[:, tl * 1024 + ooff:tl * 1024 + ooff + 512],
                            ps[:, (tl % 4) * 512:(tl % 4 + 1) * 512],
                            ccol,
                            yac[:, tl * 1024 + ooff:tl * 1024 + ooff + 512],
                            ALU.mult, ALU.add)
                    s.comp('vector', acc)

        for tl in range(8):
            tg = half * 8 + tl
            s.dma('sync', [(s1t[:], s1_d[tg * 128:(tg + 1) * 128, :]),
                           (x1t[:], x1_d[tg * 128:(tg + 1) * 128, :])])
            s.comp('scalar', lambda e, tl=tl: e.activation(gt[:], yac[:, tl * 1024:(tl + 1) * 1024],
                                                           AF.Silu))
            def fin(e):
                e.tensor_mul(gt[:], gt[:], s1t[:])
                return e.tensor_add(x1t[:], x1t[:], gt[:])
            s.comp('vector', fin)
            s.dma('sync', [(o_out[tg * 128:(tg + 1) * 128, :], x1t[:])])

    return s.build()


def run_l1d(inputs, static, r2, r3, r5):
    nc = build_l1d()
    in_maps = []
    for c in range(8):
        m = {k: static[k] for k in ('gate_t', 'w1n', 'b1n_c', 'w2n', 'b2n_r',
                                    'sw1', 'sb1_c', 'sw2', 'sb2_row')}
        m['xn1T'] = np.asarray(r3[c]['xn1T'])
        m['s1'] = np.asarray(r5[c]['s1'])
        m['x1'] = np.asarray(r2[c]['x1'])
        in_maps.append(m)
    res = run_bass_kernel_spmd(nc, in_maps, core_ids=list(range(8)))
    return res.results


NCORES = 8


def build_fused(dbg=()):
    nc, s = mk_nc()
    ar = Arena(nc, 48 * 1024)

    def tw0(*toks):
        return [t for t in toks if t is not None]

    xf = nc.dram_tensor("xf", [SEQ, D], F32, kind="ExternalInput").ap()
    pos = nc.dram_tensor("pos", [1, SEQ], F32, kind="ExternalInput").ap()
    flag = nc.dram_tensor("flag", [128, 1], F32, kind="ExternalInput").ap()
    wq = nc.dram_tensor("wq_t", [D, D], F32, kind="ExternalInput").ap()
    wk = nc.dram_tensor("wk_t", [D, D], F32, kind="ExternalInput").ap()
    wv = nc.dram_tensor("wv_t", [D, D], F32, kind="ExternalInput").ap()
    bq = nc.dram_tensor("bq_row", [1, D], F32, kind="ExternalInput").ap()
    bk = nc.dram_tensor("bk_row", [1, D], F32, kind="ExternalInput").ap()
    bv = nc.dram_tensor("bv_row", [1, D], F32, kind="ExternalInput").ap()
    wo = nc.dram_tensor("wo_t", [D, D], F32, kind="ExternalInput").ap()
    bo = nc.dram_tensor("bo_row", [1, D], F32, kind="ExternalInput").ap()
    centt = nc.dram_tensor("cent_tile", [NE, 128, D], F32, kind="ExternalInput").ap()
    gam = nc.dram_tensor("gamma_t", [D, NE], F32, kind="ExternalInput").ap()
    bet = nc.dram_tensor("beta_t", [D, NE], F32, kind="ExternalInput").ap()
    w1 = nc.dram_tensor("w1_t", [NE, D, FF], F32, kind="ExternalInput").ap()
    b1 = nc.dram_tensor("b1_c", [NE, 128, 32], F32, kind="ExternalInput").ap()
    w2 = nc.dram_tensor("w2_t", [NE, FF, D], F32, kind="ExternalInput").ap()
    b2d = nc.dram_tensor("b2_r", [1, NE * D], F32, kind="ExternalInput").ap()
    win = nc.dram_tensor("w_in", [D, DPROJ], F32, kind="ExternalInput").ap()
    cw = nc.dram_tensor("conv_wc", [128, 18, 4], F32, kind="ExternalInput").ap()
    cb = nc.dram_tensor("conv_bc", [128, 18], F32, kind="ExternalInput").ap()
    dtb = nc.dram_tensor("dtb_c", [32, 1], F32, kind="ExternalInput").ap()
    alog = nc.dram_tensor("alog_c", [32, 1], F32, kind="ExternalInput").ap()
    alogt = nc.dram_tensor("alog_tile", [128, 32], F32, kind="ExternalInput").ap()
    dskt = nc.dram_tensor("dskip_tile", [128, T], F32, kind="ExternalInput").ap()
    rmsw_d = nc.dram_tensor("rmsw_tile", [128, DIN], F32, kind="ExternalInput").ap()
    wout_d = nc.dram_tensor("w_out", [DIN, D], F32, kind="ExternalInput").ap()
    glw_d = nc.dram_tensor("glu_w", [D, DIN], F32, kind="ExternalInput").ap()
    glb_d = nc.dram_tensor("glu_b_row", [1, DIN], F32, kind="ExternalInput").ap()
    lnw_d = nc.dram_tensor("lnw_tile", [128, D], F32, kind="ExternalInput").ap()
    lnb_d = nc.dram_tensor("lnb_tile", [128, D], F32, kind="ExternalInput").ap()
    gate_d = nc.dram_tensor("gate_tile", [NE, 128, D], F32, kind="ExternalInput").ap()
    w1n_d = nc.dram_tensor("w1n", [NE, D, FF], F32, kind="ExternalInput").ap()
    b1n_d = nc.dram_tensor("b1n_c", [NE, 128, 32], F32, kind="ExternalInput").ap()
    w2n_d = nc.dram_tensor("w2n", [NE, FF, D], F32, kind="ExternalInput").ap()
    b2n_d = nc.dram_tensor("b2n_r", [1, NE * D], F32, kind="ExternalInput").ap()
    sw1_d = nc.dram_tensor("sw1", [D, FF], F32, kind="ExternalInput").ap()
    sb1_d = nc.dram_tensor("sb1_c", [128, 32], F32, kind="ExternalInput").ap()
    sw2_d = nc.dram_tensor("sw2", [FF, D], F32, kind="ExternalInput").ap()
    sb2_d = nc.dram_tensor("sb2_row", [1, D], F32, kind="ExternalInput").ap()

    o_out = nc.dram_tensor("out", [T, D], F32, kind="ExternalOutput").ap()

    def scr(name, shape, dtype):
        kind = "ExternalOutput" if name in dbg else "Internal"
        return nc.dram_tensor(name, shape, dtype, kind=kind).ap()
    xn_scr = scr("xn", [T, D], F32)
    xnbfT_scr = scr("xnT", [128, 8, SEQ], BF16)
    kr_scr = scr("krTa", [65, NH, SEQ], BF16)
    v_scr = scr("v", [SEQ, D], BF16)
    q_scr = scr("qrTa", [65, NH, T], BF16)
    s_scr = scr("s_attn", [T, D], F32)
    x1_scr = scr("x1", [T, D], F32)
    z_scr = scr("z", [T, DIN], BF16)
    xn1_scr = scr("xn1", [T, D], F32)
    xsT_scr = scr("xsT", [128, 16, T], BF16)
    BT_scr = scr("BT", [128, T], BF16)
    CT_scr = scr("CT", [128, T], BF16)
    dt_scr = scr("dtsp", [32, T], F32)
    LT_scr = scr("LT", [32, T], F32)
    yl_scr = scr("ylocal", [T, DIN], F32)
    s1_scr = scr("s1", [T, D], F32)
    hend = nc.dram_tensor("hend", [128, DIN], F32, kind="Internal").ap()
    xch_h = nc.dram_tensor("xch_h", [2, 128, D], F32, kind="Internal",
                           addr_space="Shared").ap()
    xch_s = nc.dram_tensor("xch_s", [2, 128, DIN], F32, kind="Internal",
                           addr_space="Shared").ap()
    xch_f = nc.dram_tensor("xch_f", [2, 8], I32, kind="Internal",
                           addr_space="Shared").ap()
    halo_l = nc.dram_tensor("halo_l", [128, D], F32, kind="Internal").ap()
    w1b_scr = nc.dram_tensor("w1b", [NE, D, FF], BF16, kind="Internal").ap()
    w2b_scr = nc.dram_tensor("w2b", [NE, FF, D], BF16, kind="Internal").ap()
    w1nb_scr = nc.dram_tensor("w1nb", [NE, D, FF], BF16, kind="Internal").ap()
    w2nb_scr = nc.dram_tensor("w2nb", [NE, FF, D], BF16, kind="Internal").ap()
    sw1b_scr = nc.dram_tensor("sw1b", [D, FF], BF16, kind="Internal").ap()
    sw2b_scr = nc.dram_tensor("sw2b", [FF, D], BF16, kind="Internal").ap()
    hin_l = nc.dram_tensor("hin_l", [128, DIN], F32, kind="Internal").ap()
    par_d = nc.dram_tensor("par", [1, 2], I32, kind="ExternalInput").ap()

    ident, ones_row, ones_col = setup_common(nc, s)
    identf = nc.alloc_sbuf_tensor("identf", [128, 128], F32)
    s.comp('gpsimd', lambda e: e.memset(identf[:], 1.0))
    s.comp('gpsimd', lambda e: e.affine_select(identf[:], identf[:], pattern=[[1, 128]],
                                               base=0, channel_multiplier=-1,
                                               compare_op=ALU.is_equal, fill=0.0))
    onesf = nc.alloc_sbuf_tensor("onesf", [1, 128], F32)
    s.comp('vector', lambda e: e.memset(onesf[:], 1.0))
    sflag = nc.alloc_sbuf_tensor("sflag", [128, 1], F32)
    s.dma('sync', [(sflag[:], flag)])
    s.xpar_sb = nc.alloc_sbuf_tensor("xpar_sb", [1, 2], I32)
    s.xflag_sb = nc.alloc_sbuf_tensor("xflag_sb", [1, 2], I32)
    par_tok = s.dma('sync', [(s.xpar_sb[:], par_d)])

    POOL = nc.alloc_psum_tensor("POOL", [128, 4096], F32)
    pA = POOL[:, 0:2048]
    pB = POOL[:, 2048:3072]
    ptpf = POOL[:, 3072:3200]
    ptpb = POOL[:, 3200:3264].bitcast(BF16)
    pO2 = POOL[0:65, 2048:4096]

    def psum8(tl):
        return POOL[:, tl * 512:(tl + 1) * 512]

    def emit_ln(sxa, dmp, sta, out_bf):
        def stats_v(e):
            return e.tensor_scalar(dmp, sxa, 1.0, 0.0, ALU.mult, ALU.add,
                                   accum_out=sta[:, 0:1])
        s.comp('vector', stats_v)
        s.comp('scalar', lambda e: e.activation(dmp, sxa, AF.Square, accum_out=sta[:, 1:2]))
        def mv(e):
            e.tensor_scalar(sta[:, 2:3], sta[:, 0:1], 1.0 / D, None, ALU.mult)
            e.tensor_scalar(sta[:, 3:4], sta[:, 1:2], 1.0 / D, None, ALU.mult)
            e.tensor_scalar(sta[:, 4:5], sta[:, 2:3], sta[:, 2:3], None, ALU.mult)
            return e.tensor_sub(sta[:, 3:4], sta[:, 3:4], sta[:, 4:5])
        s.comp('vector', mv)
        p = sxa.shape[0]
        def rstd(e):
            e.activation(sta[:, 4:5], sta[:, 3:4], AF.Ln, bias=s.const(EPS, p))
            return e.activation(sta[:, 4:5], sta[:, 4:5], AF.Exp, scale=s.const(-0.5, p))
        s.comp('scalar', rstd)
        def norm(e):
            e.tensor_scalar(sxa, sxa, sta[:, 2:3], sta[:, 4:5], ALU.subtract, ALU.mult)
            return e.tensor_copy(out_bf, sxa)
        return s.comp('vector', norm)

    ar.push()
    sx = ar.alloc(128, 2 * D, F32)
    dump = ar.alloc(128, DIN, F32)
    dumpS = ar.alloc(128, DIN, F32)
    stat = ar.alloc(128, 16, F32)
    sxn_bf = ar.alloc(128, 2 * D, BF16)
    xnT_t = ar.alloc(128, 2 * D, BF16)
    cs_tok = ar.alloc(128, 32 * 64, BF16)
    ptA = ptpb
    ptB = POOL[:, 3584:3648].bitcast(BF16)

    ar.push()
    spos = ar.alloc(1, SEQ, F32)
    rwork = ar.alloc(32, 3 * SEQ, F32)
    csT_bf = ar.alloc(32, 2 * SEQ, BF16)
    ri32 = ar.alloc(32, SEQ, I32)
    invf = ar.alloc(1, 32, F32)
    s.dma('sync', [(spos, pos)])
    s.comp('gpsimd', lambda e: e.iota(invf, pattern=[[1, 32]], base=0,
                                      channel_multiplier=0, allow_small_or_imprecise_dtypes=True))
    s.comp('scalar', lambda e: e.activation(invf, invf, AF.Exp,
                                            scale=s.const(-np.log(10000.0) / 32.0, 1)))
    for rnd in range(2):
        def mm_f(e, rnd=rnd):
            for g in range(4):
                last = e.matmul(pA[0:32, g * 512:(g + 1) * 512], invf,
                                spos[:, rnd * 2048 + g * 512: rnd * 2048 + (g + 1) * 512],
                                start=True, stop=True)
            return last
        s.comp('tensor', mm_f)
        def rr(e, rnd=rnd):
            o = rnd * 2048
            e.tensor_scalar(rwork[:, o:o + 2048], pA[0:32, 0:2048], 1.0 / TWO_PI, 0.25,
                            ALU.mult, ALU.add)
            e.tensor_scalar(rwork[:, SEQ + o:SEQ + o + 2048], pA[0:32, 0:2048],
                            1.0 / TWO_PI, None, ALU.mult)
            for cc in (0, 1):
                u = rwork[:, cc * SEQ + o: cc * SEQ + o + 2048]
                w = rwork[:, 2 * SEQ: 2 * SEQ + 2048]
                e.tensor_copy(ri32[:, 0:2048], u)
                e.tensor_copy(w, ri32[:, 0:2048])
                e.tensor_sub(u, u, w)
                e.tensor_scalar(w, u, 0.5, None, ALU.is_ge)
                last = e.tensor_sub(u, u, w)
            return last
        s.comp('vector', rr)
        def mkcs(e, rnd=rnd):
            o = rnd * 2048
            e.activation(csT_bf[:, o:o + 2048], rwork[:, o:o + 2048], AF.Sin,
                         scale=s.const(TWO_PI, 32))
            return e.activation(csT_bf[:, SEQ + o:SEQ + o + 2048],
                                rwork[:, SEQ + o:SEQ + o + 2048], AF.Sin,
                                scale=s.const(TWO_PI, 32))
        s.comp('scalar', mkcs)
    cstl = None
    for t in range(32):
        s.comp('tensor', lambda e, t=t: e.transpose(
            ptpf[0:128, 0:32].bitcast(BF16)[:, 0:32], csT_bf[:, t * 128:(t + 1) * 128],
            ident[0:32, 0:32]))
        s.comp('vector', lambda e, t=t: e.tensor_copy(
            cs_tok[:, t * 64:t * 64 + 32], ptpf[0:128, 0:32].bitcast(BF16)[:, 0:32]))
        s.comp('tensor', lambda e, t=t: e.transpose(
            ptpf[0:128, 0:32].bitcast(BF16)[:, 0:32], csT_bf[:, SEQ + t * 128:SEQ + (t + 1) * 128],
            ident[0:32, 0:32]))
        cstl = s.comp('vector', lambda e, t=t: e.tensor_copy(
            cs_tok[:, t * 64 + 32:t * 64 + 64], ptpf[0:128, 0:32].bitcast(BF16)[:, 0:32]))
    ar.pop()

    ln_ld = [None] * 32
    ln_norm = [None] * 32
    ln_sq = [None] * 32
    ln_tr7 = [None] * 32
    ln_st = [None] * 32
    for t in range(32):
        sl = t % 2
        sxa = sx[:, sl * D:(sl + 1) * D]
        sta = stat[:, sl * 8:(sl + 1) * 8]
        outb = sxn_bf[:, sl * D:(sl + 1) * D]
        xtt = xnT_t[:, sl * D:(sl + 1) * D]
        ln_ld[t] = s.dma('sync', [(sxa, xf[t * 128:(t + 1) * 128, :])],
                         wait=tw0(ln_norm[t - 2] if t >= 2 else None))
        stv = s.comp('vector', lambda e, sxa=sxa, sta=sta, sl=sl: e.tensor_scalar(
            dump[:, sl * D:(sl + 1) * D], sxa, 1.0, 0.0, ALU.mult, ALU.add,
            accum_out=sta[:, 0:1]), wait=[ln_ld[t]])
        ln_sq[t] = s.comp('scalar', lambda e, sxa=sxa, sta=sta, sl=sl: e.activation(
            dumpS[:, sl * D:(sl + 1) * D], sxa, AF.Square, accum_out=sta[:, 1:2]),
            wait=[ln_ld[t]])
        def mv(e, sta=sta):
            e.tensor_scalar(sta[:, 2:3], sta[:, 0:1], 1.0 / D, None, ALU.mult)
            e.tensor_scalar(sta[:, 3:4], sta[:, 1:2], 1.0 / D, None, ALU.mult)
            e.tensor_scalar(sta[:, 4:5], sta[:, 2:3], sta[:, 2:3], None, ALU.mult)
            return e.tensor_sub(sta[:, 3:4], sta[:, 3:4], sta[:, 4:5])
        mvt = s.comp('vector', mv, wait=[ln_sq[t], stv])
        def rstd(e, sta=sta):
            e.activation(sta[:, 4:5], sta[:, 3:4], AF.Ln, bias=s.const(EPS, 128))
            return e.activation(sta[:, 4:5], sta[:, 4:5], AF.Exp, scale=s.const(-0.5, 128))
        rst = s.comp('scalar', rstd, wait=[mvt])
        def norm(e, sxa=sxa, sta=sta, outb=outb):
            e.tensor_scalar(sxa, sxa, sta[:, 2:3], sta[:, 4:5], ALU.subtract, ALU.mult)
            return e.tensor_copy(outb, sxa)
        ln_norm[t] = s.comp('vector', norm,
                            wait=tw0(rst, ln_tr7[t - 2] if t >= 2 else cstl))
        if t < 16:
            s.dma('sync', [(xn_scr[t * 128:(t + 1) * 128, :], sxa)], wait=[ln_norm[t]])
        tcp = None
        tcps = [None] * 8
        for j in range(8):
            pt = ptA if j % 2 == 0 else ptB
            trj = s.comp('tensor', lambda e, j=j, outb=outb, pt=pt: e.transpose(
                pt, outb[:, j * 128:(j + 1) * 128], ident[:]),
                wait=tw0(ln_norm[t] if j == 0 else None,
                         tcps[j - 2] if j >= 2 else
                         (prev_tcps[6 + j] if t >= 1 else cstl)))
            tcp = s.comp('vector', lambda e, j=j, xtt=xtt, pt=pt: e.tensor_copy(
                xtt[:, j * 128:(j + 1) * 128], pt),
                wait=tw0(trj, (ln_st[t - 2] if t >= 2 else None) if j == 0 else None))
            tcps[j] = tcp
        prev_tcps = tcps
        ln_tr7[t] = trj
        ln_st[t] = s.dma('sync', [(xnbfT_scr[:, :, t * 128:(t + 1) * 128],
                                   xtt.rearrange("p (j c) -> p j c", j=8))], wait=[tcp])
    ln_tcp_last = prev_tcps[7]

    zf = s.comp('vector', lambda e: e.memset(dumpS, 0.0),
                wait=[ln_sq[31], ln_norm[31]])
    s.dma('sync', [(halo_l, dumpS[:, 0:D]), (hin_l, dumpS)], wait=[zf])

    sw = ar.alloc(128, 3 * 8 * D, BF16)
    sb_row = ar.alloc(1, 3 * D, BF16)
    xnT_tb = ar.alloc(128, 2 * D, BF16)
    qtok = ar.alloc(128, 2 * D, F32)
    qrope = ar.alloc(128, 2 * D, BF16)
    qTfull = ar.alloc(128, 2 * D, BF16)
    qkvscr = ar.alloc(128, 2 * D, F32)
    zrow_bf = ar.alloc(1, SEQ, BF16)
    zmem = s.comp('vector', lambda e: e.memset(zrow_bf, 0.0))

    import dataclasses as _dc
    def sv(base, off, w=32):
        a = base[:, off:off + w]
        return _dc.replace(a, ap=[a.ap[0], [64, 16], [1, w]])

    def swv(sl):
        return sw.rearrange("p (s j d) -> p s j d", s=3, j=8)[:, sl]
    swld = {}
    swld['q'] = s.dma('gpsimd', [(swv(0), wq.rearrange("(j p) d -> p j d", p=128)),
                                 (sb_row[:, 0:D], bq)], wait=[cstl])
    swld['k'] = s.dma('gpsimd', [(swv(1), wk.rearrange("(j p) d -> p j d", p=128)),
                                 (sb_row[:, D:2 * D], bk)], wait=[cstl])
    swld['v'] = s.dma('gpsimd', [(swv(2), wv.rearrange("(j p) d -> p j d", p=128)),
                                 (sb_row[:, 2 * D:3 * D], bv)], wait=[cstl])
    for ee in range(NE):
        for cc in range(4):
            s.dma('gpsimd', [(w1b_scr[ee, cc * 256:(cc + 1) * 256, :],
                              w1[ee, cc * 256:(cc + 1) * 256, :])], wait=[])
        for cc in range(4):
            s.dma('gpsimd', [(w2b_scr[ee, cc * 1024:(cc + 1) * 1024, :],
                              w2[ee, cc * 1024:(cc + 1) * 1024, :])], wait=[])
    bar = None
    prev_tcqs = None
    for which, ws, ntile in (("q", 0, 16), ("k", 1, 32), ("v", 2, 32)):
        pv_ld = [None] * ntile
        pv_proj = [None] * ntile
        pv_cp = [None] * ntile
        pv_rope = [None] * ntile
        pv_tr7 = [None] * ntile
        pv_st = [None] * ntile
        for t in range(ntile):
            sl = t % 2
            xb = xnT_tb[:, sl * D:(sl + 1) * D]
            qt = qtok[:, sl * D:(sl + 1) * D]
            qr = qrope[:, sl * D:(sl + 1) * D]
            qT = qTfull[:, sl * D:(sl + 1) * D]
            pO = pA[:, sl * 1024:(sl + 1) * 1024]
            def back(arr, k=2):
                return arr[t - k] if t >= k else bar
            pv_ld[t] = s.dma('scalar', [(xb.rearrange("p (j c) -> p j c", j=8),
                                         xnbfT_scr[:, :, t * 128:(t + 1) * 128])],
                             wait=tw0(ln_st[t] if which == "q" else None,
                                      cstl if (which == "q" and t < 2) else None,
                                      back(pv_proj)))
            fb_proj0 = cstl if which == "q" else None
            def proj(e, xb=xb, ws=ws, pO=pO):
                for j in range(8):
                    lhsT = xb[:, j * 128:(j + 1) * 128]
                    wsl = sw[:, ws * 8 * D + j * D: ws * 8 * D + (j + 1) * D]
                    e.matmul(pO[:, 0:512], lhsT, wsl[:, 0:512],
                             start=(j == 0), stop=False)
                    e.matmul(pO[:, 512:1024], lhsT, wsl[:, 512:1024],
                             start=(j == 0), stop=False)
                e.matmul(pO[:, 0:512], ones_row[:], sb_row[0:1, ws * D:ws * D + 512],
                         start=False, stop=True)
                return e.matmul(pO[:, 512:1024], ones_row[:],
                                sb_row[0:1, ws * D + 512:(ws + 1) * D],
                                start=False, stop=True)
            pv_proj[t] = s.comp('tensor', proj,
                                wait=tw0(pv_ld[t], swld[which], back(pv_cp),
                                         fb_proj0 if t < 2 else None))
            if which == "v":
                pv_cp[t] = s.comp('vector', lambda e, qr=qr, pO=pO: e.tensor_copy(qr, pO),
                                  wait=tw0(pv_proj[t], back(pv_st)))
                pv_st[t] = s.dma('scalar', [(v_scr[t * 128:(t + 1) * 128, :], qr)],
                                 wait=[pv_cp[t]])
                continue
            if which == "q":
                pv_cp[t] = s.comp('scalar', lambda e, qt=qt, pO=pO: e.activation(
                    qt, pO, AF.Identity, scale=s.const(0.125)),
                    wait=tw0(pv_proj[t], back(pv_rope)))
            else:
                pv_cp[t] = s.comp('scalar', lambda e, qt=qt, pO=pO: e.copy(qt, pO),
                                  wait=tw0(pv_proj[t], back(pv_rope)))
            def rope_t(e, t=t, qt=qt, qr=qr, sl=sl):
                lo = sv(qt, 0)
                hi = sv(qt, 32)
                olo = sv(qr, 0)
                ohi = sv(qr, 32)
                co = cs_tok[:, t * 64:t * 64 + 32]
                si = cs_tok[:, t * 64 + 32:t * 64 + 64]
                cob = _dc.replace(co, ap=[co.ap[0], [0, 16], [1, 32]])
                sib = _dc.replace(si, ap=[si.ap[0], [0, 16], [1, 32]])
                w1_ = sv(qkvscr[:, sl * D:(sl + 1) * D], 0)
                w2_ = sv(qkvscr[:, sl * D:(sl + 1) * D], 32)
                e.tensor_tensor(w1_, lo, cob, ALU.mult)
                e.tensor_tensor(w2_, hi, sib, ALU.mult)
                e.tensor_tensor(olo, w1_, w2_, ALU.subtract)
                e.tensor_tensor(w1_, hi, cob, ALU.mult)
                e.tensor_tensor(w2_, lo, sib, ALU.mult)
                return e.tensor_tensor(ohi, w1_, w2_, ALU.add)
            pv_rope[t] = s.comp('vector', rope_t,
                                wait=tw0(pv_cp[t], back(pv_tr7)))
            tcq = None
            tcqs = [None] * 8
            for j in range(8):
                pt = ptA if j % 2 == 0 else ptB
                ptwar = (tcqs[j - 2] if j >= 2 else
                         (prev_tcqs[6 + j] if prev_tcqs is not None else ln_tcp_last))
                trj = s.comp('tensor', lambda e, j=j, qr=qr, pt=pt: e.transpose(
                    pt, qr[:, j * 128:(j + 1) * 128], ident[:]),
                    wait=tw0(pv_rope[t] if j == 0 else None, ptwar))
                tcq = s.comp('vector', lambda e, j=j, qT=qT, pt=pt: e.tensor_copy(
                    qT[:, j * 128:(j + 1) * 128], pt),
                    wait=tw0(trj, back(pv_st) if j == 0 else None))
                tcqs[j] = tcq
            prev_tcqs = tcqs
            pv_tr7[t] = trj
            dst = q_scr if which == "q" else kr_scr
            hstr = T if which == "q" else SEQ
            dape = [dst[0:64, 0, 0:128].ap[0], [2 * hstr, 8], [1, 128]]
            pv_st[t] = s.dma('scalar', [
                (_dc.replace(dst[0:64, 0, t * 128:(t + 1) * 128], ap=dape),
                 qTfull[0:64, sl * D:(sl + 1) * D].rearrange("p (j c) -> p j c", j=8)),
                (_dc.replace(dst[0:64, 1, t * 128:(t + 1) * 128], ap=dape),
                 qTfull[64:128, sl * D:(sl + 1) * D].rearrange("p (j c) -> p j c", j=8)),
            ], wait=[tcq])
        bar = pv_st[ntile - 1]
        if which == "q":
            s.dma('sync', [(q_scr[64:65, :, :].rearrange("o h t -> o (h t)"),
                            _dc.replace(zrow_bf, ap=[zrow_bf.ap[0], [0, NH], [1, T]]))],
                  wait=[zmem])
        elif which == "k":
            s.dma('sync', [(kr_scr[64:65, :, :].rearrange("o h t -> o (h t)"),
                            _dc.replace(zrow_bf, ap=[zrow_bf.ap[0], [0, NH], [1, SEQ]]))],
                  wait=[zmem])
    ar.pop()


    ar.push()
    qh = ar.alloc(65, 2 * T, BF16)
    kh = ar.alloc(65, 2 * SEQ, BF16)
    vaug = ar.alloc(128, 2 * 32 * 65, BF16)
    qsq = ar.alloc(64, SEQ, BF16)
    P_bf = ar.alloc(128, 2 * T, BF16)
    rZ = ar.alloc(1, T, F32)
    lnk = ar.alloc(1, 4, F32)
    ubr = ar.alloc(1, T, F32)
    rZb_sb = ar.alloc(64, T, F32)
    attnT = ar.alloc(64, NH * T, BF16)
    swo = ar.alloc(64, NH * D, BF16)
    sbo = ar.alloc(1, D, BF16)
    ssb = ar.alloc(128, 2 * D, F32)

    s.dma('gpsimd', [(swo.rearrange("p (h d) -> p h d", h=NH),
                      wo.rearrange("(h p) d -> p h d", p=64)),
                     (sbo, bo)])
    def hload(h, wait):
        sl = h % 2
        return s.dma('sync', [
            (qh[:, sl * T:(sl + 1) * T], q_scr[:, h, :]),
            (kh[:, sl * SEQ:(sl + 1) * SEQ], kr_scr[:, h, :]),
            (vaug.rearrange("p (s c w) -> p s c w", s=2, w=65)[:, sl, :, 0:64],
             v_scr.rearrange("(c p) d -> p c d", p=128)[:, :, h * 64:(h + 1) * 64])],
            wait=wait)
    hld = [None] * NH
    hld[0] = hload(0, None)
    hld[1] = hload(1, None)
    mmS_last = [None] * NH
    mmV_last = [None] * NH
    ksqm = [None] * NH
    atm = [None] * NH
    rzc = [None] * NH
    for h in range(NH):
        sl = h % 2
        qhs = qh[:, sl * T:(sl + 1) * T]
        khs = kh[:, sl * SEQ:(sl + 1) * SEQ]
        vas = vaug[:, sl * 32 * 65:(sl + 1) * 32 * 65]
        ms1 = s.comp('vector', lambda e, sl=sl: e.memset(
            vaug.rearrange("p (s c w) -> p s c w", s=2, w=65)[:, sl, :, 64:65], 1.0),
            wait=[hld[h]])
        ms2 = s.comp('vector', lambda e, khs=khs: e.memset(khs[64:65, :], 1.0),
                     wait=[hld[h]])
        qm = s.comp('vector', lambda e, qhs=qhs: e.tensor_mul(
            qsq[:, 0:T], qhs[0:64, :], qhs[0:64, :]),
            wait=tw0(hld[h], mmS_last[h - 1] if h else None))
        def mm_qsq(e):
            for g in range(4):
                last = e.matmul(pO2[0:1, g * 512:(g + 1) * 512], ones_col[0:64, :],
                                qsq[:, g * 512:(g + 1) * 512], start=True, stop=True)
            return last
        q2t = s.comp('tensor', mm_qsq, wait=tw0(qm, atm[h - 1] if h else None))
        ub1 = s.comp('scalar', lambda e: e.activation(ubr, pO2[0:1, 0:T], AF.Ln),
                     wait=[q2t])
        ksqm[h] = s.comp('vector', lambda e, khs=khs: e.tensor_mul(
            qsq, khs[0:64, :], khs[0:64, :]), wait=[q2t])
        def mm_ksq2(e):
            for g in range(8):
                last = e.matmul(pA[0:1, (g % 4) * 512:((g % 4) + 1) * 512],
                                ones_col[0:64, :], qsq[:, g * 512:(g + 1) * 512],
                                start=(g < 4), stop=(g >= 4))
            return last
        k2t = s.comp('tensor', mm_ksq2,
                     wait=tw0(ksqm[h], rzc[h - 1] if h else None))
        mx = s.comp('vector', lambda e: e.reduce_max(lnk[:, 0:1], pA[0:1, 0:T], axis=AX.X),
                    wait=[k2t])
        lk = s.comp('scalar', lambda e: e.activation(lnk[:, 1:2], lnk[:, 0:1], AF.Ln),
                    wait=tw0(mx, ub1))
        ua = s.comp('vector', lambda e: e.tensor_scalar(ubr, ubr, lnk[0:1, 1:2], None, ALU.add),
                    wait=[lk])
        ue = s.comp('scalar', lambda e: e.activation(ubr, ubr, AF.Exp, scale=s.const(0.5, 1)),
                    wait=[ua])
        un = s.comp('vector', lambda e, qhs=qhs: e.tensor_scalar(
            qhs[64:65, :], ubr, -1.0, None, ALU.mult), wait=tw0(ue, ms1, ms2))
        expt = {}
        mst = {}
        mvt = {}
        for c in range(32):
            for hf in range(2):
                po = pA[:, hf * 1024:(hf + 1) * 1024]
                pb = P_bf[:, (c % 2) * T + hf * 1024:(c % 2) * T + (hf + 1) * 1024]
                def mmS(e, c=c, hf=hf, khs=khs, qhs=qhs, po=po):
                    for g in range(2):
                        last = e.matmul(po[:, g * 512:(g + 1) * 512],
                                        khs[:, c * 128:(c + 1) * 128],
                                        qhs[:, hf * 1024 + g * 512:hf * 1024 + (g + 1) * 512],
                                        start=True, stop=True)
                    return last
                mst[(c, hf)] = s.comp('tensor', mmS,
                                      wait=tw0(un if c == 0 else None,
                                               rzc[h - 1] if (h and c == 0) else None,
                                               expt.get((c - 1, hf))))
                expt[(c, hf)] = s.comp('scalar', lambda e, pb=pb, po=po: e.activation(
                    pb, po, AF.Exp),
                    wait=tw0(mst[(c, hf)], mvt.get((c - 2, hf)),
                             mmV_last[h - 1] if (h and c < 2) else None))
                def mmV(e, c=c, hf=hf, vas=vas, pb=pb):
                    for g in range(2):
                        last = e.matmul(pO2[:, hf * 1024 + g * 512:hf * 1024 + (g + 1) * 512],
                                        vas[:, c * 65:(c + 1) * 65],
                                        pb[:, g * 512:(g + 1) * 512],
                                        start=(c == 0), stop=(c == 31))
                    return last
                mvt[(c, hf)] = s.comp('tensor', mmV,
                                      wait=tw0(expt[(c, hf)],
                                               atm[h - 1] if (h == 1 and c == 0) else None))
        mmS_last[h] = mst[(31, 1)]
        mmV_last[h] = mvt[(31, 1)]
        if h + 2 < NH:
            hld[h + 2] = hload(h + 2, tw0(mmS_last[h], mmV_last[h], ksqm[h], un))
        rv = s.comp('vector', lambda e: e.reciprocal(rZ, pO2[64:65, :]),
                    wait=[mmV_last[h]])
        def mm_rzb(e):
            for g in range(4):
                last = e.matmul(pA[0:64, g * 512:(g + 1) * 512], onesf[0:1, 0:64],
                                rZ[:, g * 512:(g + 1) * 512], start=True, stop=True)
            return last
        rzt = s.comp('tensor', mm_rzb, wait=tw0(rv, expt[(31, 1)]))
        rzc[h] = s.comp('vector', lambda e: e.tensor_copy(rZb_sb, pA[0:64, 0:T]),
                        wait=[rzt])
        atm[h] = s.comp('vector', lambda e, h=h: e.tensor_mul(
            attnT[:, h * T:(h + 1) * T], pO2[0:64, :], rZb_sb), wait=[rzc[h]])
    wo_cp = [None] * 16
    wo_st = [None] * 16
    for t in range(16):
        po = pA[:, (t % 2) * 1024:(t % 2 + 1) * 1024]
        def wop(e, t=t, po=po):
            for h in range(NH):
                lhsT = attnT[:, h * T + t * 128:h * T + (t + 1) * 128]
                e.matmul(po[:, 0:512], lhsT, swo[:, h * D + 0: h * D + 512],
                         start=(h == 0), stop=False)
                e.matmul(po[:, 512:1024], lhsT, swo[:, h * D + 512: (h + 1) * D],
                         start=(h == 0), stop=False)
            e.matmul(po[:, 0:512], ones_row[:], sbo[0:1, 0:512], start=False, stop=True)
            return e.matmul(po[:, 512:1024], ones_row[:], sbo[0:1, 512:1024],
                            start=False, stop=True)
        wt = s.comp('tensor', wop, wait=tw0(atm[NH - 1] if t == 0 else None,
                                            rzc[NH - 1] if t == 0 else None,
                                            wo_cp[t - 2] if t >= 2 else None))
        wo_cp[t] = s.comp('vector', lambda e, t=t, po=po: e.tensor_copy(
            ssb[:, (t % 2) * D:(t % 2 + 1) * D], po),
            wait=tw0(wt, wo_st[t - 2] if t >= 2 else None))
        wo_st[t] = s.dma('sync', [(s_scr[t * 128:(t + 1) * 128, :],
                                   ssb[:, (t % 2) * D:(t % 2 + 1) * D])], wait=[wo_cp[t]])
    ar.pop()


    ar.push()
    sco = ar.alloc(128, 16 * NE, F32)
    m1c = ar.alloc(128, 16, F32)
    ohc = ar.alloc(128, 16 * NE, F32)
    alc = ar.alloc(128, 16, F32)
    ar.push()
    centb = ar.alloc(128, NE * D, F32)
    xt32 = ar.alloc(128, D, F32)
    dmp = ar.alloc(128, D, F32)
    s.dma('sync', [(centb.rearrange("p (e d) -> p e d", e=NE),
                    centt.rearrange("e p d -> p e d"))])
    for t in range(16):
        s.dma('sync', [(xt32, xn_scr[t * 128:(t + 1) * 128, :])])
        def dots(e, t=t):
            for ee in range(NE):
                last = e.scalar_tensor_tensor(dmp, xt32, 1.0,
                                              centb[:, ee * D:(ee + 1) * D],
                                              ALU.mult, ALU.mult,
                                              accum_out=sco[:, t * NE + ee:t * NE + ee + 1])
            return last
        s.comp('vector', dots)
        def sel(e, t=t):
            e.reduce_max(m1c[:, t:t + 1], sco[:, t * NE:(t + 1) * NE], axis=AX.X)
            return e.tensor_scalar(ohc[:, t * NE:(t + 1) * NE], sco[:, t * NE:(t + 1) * NE],
                                   m1c[:, t:t + 1], None, ALU.is_ge)
        s.comp('vector', sel)
        s.comp('scalar', lambda e, t=t: e.activation(alc[:, t:t + 1], m1c[:, t:t + 1], AF.Sigmoid))
    ar.pop()

    xnT = ar.alloc(128, 8 * T, BF16)
    xsq = ar.alloc(128, T, BF16)
    RSTDF = ar.alloc(128, T, F32)
    sgam = ar.alloc(128, 8 * NE, F32)
    sbet = ar.alloc(128, 8 * NE, F32)
    sb2 = ar.alloc(1, D, BF16)
    sb1c = ar.alloc(128, 32, F32)
    nfeT = ar.alloc(128, 8 * 1024, BF16)
    w1blk = ar.alloc(128, 4096, BF16)
    w2blk = ar.alloc(128, 4096, BF16)
    rh = ar.alloc(128, 32 * 1024, BF16)
    yac = ar.alloc(128, 8 * 1024, F32)
    xtile = ar.alloc(128, D, F32)
    stile = ar.alloc(128, D, F32)
    ntile = ar.alloc(128, D, F32)
    gtile = ar.alloc(128, D, F32)

    s.dma('sync', [(xnT.rearrange("p (j t) -> p j t", j=8), xnbfT_scr[:, :, 0:T])])
    s.dma('sync', [(sgam.rearrange("p (j e) -> p j e", j=8),
                    gam.rearrange("(j p) e -> p j e", p=128)),
                   (sbet.rearrange("p (j e) -> p j e", j=8),
                    bet.rearrange("(j p) e -> p j e", p=128))])
    for j in range(8):
        s.comp('vector', lambda e, j=j: e.tensor_mul(
            xsq, xnT[:, j * T:(j + 1) * T], xnT[:, j * T:(j + 1) * T]))
        def mmsq(e, j=j):
            for g in range(4):
                last = e.matmul(pA[0:1, g * 512:(g + 1) * 512], ones_col[:],
                                xsq[:, g * 512:(g + 1) * 512],
                                start=(j == 0), stop=(j == 7))
            return last
        s.comp('tensor', mmsq)
    def rstdf_row(e):
        rsr = RSTDF[0:1, :]
        e.activation(rsr, pA[0:1, 0:T], AF.Ln, bias=s.const(EPS, 1), scale=s.const(1.0 / D, 1))
        return e.activation(rsr, rsr, AF.Exp, scale=s.const(-0.5, 1))
    s.comp('scalar', rstdf_row)
    def mm_rb(e):
        rsr = RSTDF[0:1, :]
        for g in range(4):
            last = e.matmul(pA[:, g * 512:(g + 1) * 512], onesf[0:1, :],
                            rsr[:, g * 512:(g + 1) * 512], start=True, stop=True)
        return last
    s.comp('tensor', mm_rb)
    s.comp('vector', lambda e: e.tensor_copy(RSTDF, pA[:, 0:T]))
    for j in range(8):
        s.comp('vector', lambda e, j=j: e.tensor_mul(
            xnT[:, j * T:(j + 1) * T], xnT[:, j * T:(j + 1) * T], RSTDF))

    def emit_expert_ffn(st, feT, w1src, w2src, b1pair, b2pair, sb1, sb2t,
                        actf, rh_, yac_, w1b, w2b, acc_col, feT_ready=None):
        def tw(*toks):
            toks = [t for t in toks if t is not None]
            return toks if toks else s.snap()
        def w1dma(fp, wait):
            return s.dma('gpsimd', [(
                w1b.rearrange("p (sl j c) -> p sl j c", sl=2, j=8)[:, fp % 2],
                w1src(fp))], wait=wait)
        w1d = [None] * 16
        w1d[0] = w1dma(0, tw(st['mm1_last']))
        w1d[1] = w1dma(1, tw(st['mm1_last']))
        sb1tok = s.dma('scalar', [b1pair], wait=tw(st['relu_last']))
        sb2tok = s.dma('gpsimd', [b2pair], wait=tw(st['b2_last']))
        ready = feT_ready if feT_ready is not None else st['acc_last']
        mm1 = [None] * 32
        relu = [None] * 32
        for f in range(32):
            fp, sl, po = f // 2, (f // 2) % 2, (f % 4) * 1024
            def mmw1(e, f=f, sl=sl, po=po):
                for j in range(8):
                    lb = sl * 2048 + j * 256 + (f % 2) * 128
                    e.matmul(POOL[:, po:po + 512], w1b[:, lb:lb + 128],
                             feT(j, 0), start=(j == 0), stop=(j == 7))
                    last = e.matmul(POOL[:, po + 512:po + 1024], w1b[:, lb:lb + 128],
                                    feT(j, 512), start=(j == 0), stop=(j == 7))
                return last
            if f < 4:
                wv = tw(ready, st['relu_last'], w1d[fp]) if ready is not None \
                    else s.snap() + [w1d[fp]]
            else:
                wv = [relu[f - 4], w1d[fp]]
            mm1[f] = s.comp('tensor', mmw1, wait=wv)
            if f % 2 == 1 and fp + 2 < 16:
                w1d[fp + 2] = w1dma(fp + 2, [mm1[f]])
            relu[f] = s.comp('scalar', lambda e, f=f, po=po: e.activation(
                rh_[:, f * 1024:(f + 1) * 1024], POOL[:, po:po + 1024], actf,
                bias=sb1[:, f:f + 1]), wait=[mm1[f], sb1tok])
        acc_last = None
        b2tok = None
        for oh in range(2):
            ooff = oh * 512
            def w2dma(fq, wait, ooff=ooff):
                return s.dma('gpsimd', [(
                    w2b.rearrange("p (sl j c) -> p sl j c", sl=2, j=4)[:, fq % 2],
                    w2src(fq, ooff))], wait=wait)
            w2d = [None] * 8
            pw = tw(st['mm2_last']) if oh == 0 else [mm2[7]]
            w2d[0] = w2dma(0, pw)
            w2d[1] = w2dma(1, pw)
            mm2 = [None] * 8
            for fq in range(8):
                def mmw2(e, fq=fq):
                    for jj in range(4):
                        f = fq * 4 + jj
                        wb = ((fq % 2) * 4 + jj) * 512
                        for tl in range(8):
                            last = e.matmul(
                                psum8(tl),
                                rh_[:, f * 1024 + tl * 128:f * 1024 + (tl + 1) * 128],
                                w2b[:, wb:wb + 512], start=(f == 0), stop=False)
                    return last
                wv = ([acc_last, w2d[0]] if (oh == 1 and fq == 0)
                      else [relu[31], w2d[fq]])
                mm2[fq] = s.comp('tensor', mmw2, wait=wv)
                if fq + 2 < 8:
                    w2d[fq + 2] = w2dma(fq + 2, [mm2[fq]])
            def mmb2(e, ooff=ooff):
                for tl in range(8):
                    last = e.matmul(psum8(tl), ones_row[:], sb2t[0:1, ooff:ooff + 512],
                                    start=False, stop=True)
                return last
            b2tok = s.comp('tensor', mmb2, wait=[relu[31], sb2tok])
            for tl in range(8):
                def accf(e, tl=tl, ooff=ooff):
                    return e.scalar_tensor_tensor(
                        yac_[:, tl * 1024 + ooff:tl * 1024 + ooff + 512],
                        psum8(tl), acc_col(tl),
                        yac_[:, tl * 1024 + ooff:tl * 1024 + ooff + 512],
                        ALU.mult, ALU.add)
                acc_last = s.comp('vector', accf, wait=[b2tok])
            st['mm2_last'] = mm2[7]
        st['mm1_last'] = mm1[31]
        st['relu_last'] = relu[31]
        st['b2_last'] = b2tok
        st['acc_last'] = acc_last

    st3 = {'mm1_last': None, 'mm2_last': None, 'relu_last': None,
           'b2_last': None, 'acc_last': None}
    x1_t15_tok = None
    for half in (1, 0):
        hoff = half * 1024
        s.comp('vector', lambda e: e.memset(yac, 0.0))
        for ee in range(NE):
            nfe_last = None
            for j in range(8):
                nfe_last = s.comp('vector', lambda e, ee=ee, j=j, hoff=hoff: e.tensor_scalar(
                    nfeT[:, j * 1024:(j + 1) * 1024],
                    xnT[:, j * T + hoff:j * T + hoff + 1024],
                    sgam[:, j * NE + ee:j * NE + ee + 1],
                    sbet[:, j * NE + ee:j * NE + ee + 1], ALU.mult, ALU.add))
            emit_expert_ffn(
                st3,
                feT=lambda j, off: nfeT[:, j * 1024 + off:j * 1024 + off + 512],
                w1src=lambda fp, ee=ee: w1b_scr[ee].rearrange(
                    "(j p) c -> p j c", p=128)[:, :, fp * 256:(fp + 1) * 256],
                w2src=lambda fq, ooff, ee=ee: w2b_scr[ee, fq * 512:(fq + 1) * 512,
                                                      ooff:ooff + 512].rearrange(
                    "(j p) c -> p j c", p=128),
                b1pair=(sb1c, b1[ee, :, :]),
                b2pair=(sb2, b2d[0:1, ee * D:(ee + 1) * D]),
                sb1=sb1c, sb2t=sb2, actf=AF.Relu, rh_=rh, yac_=yac,
                w1b=w1blk, w2b=w2blk,
                acc_col=lambda tl, ee=ee, half=half: ohc[:, (half * 8 + tl) * NE + ee:
                                                        (half * 8 + tl) * NE + ee + 1],
                feT_ready=nfe_last)
        for tl in range(8):
            tg = half * 8 + tl
            s.dma('scalar', [(ntile, xn_scr[tg * 128:(tg + 1) * 128, :]),
                             (stile, s_scr[tg * 128:(tg + 1) * 128, :]),
                             (xtile, xf[tg * 128:(tg + 1) * 128, :])])
            def blend(e, tl=tl, tg=tg):
                return e.scalar_tensor_tensor(gtile, yac[:, tl * 1024:(tl + 1) * 1024],
                                              alc[:, tg:tg + 1], ntile, ALU.mult, ALU.add)
            s.comp('vector', blend)
            s.comp('scalar', lambda e: e.activation(gtile, gtile, AF.Silu))
            def resid(e):
                e.tensor_mul(gtile, gtile, stile)
                return e.tensor_add(xtile, xtile, gtile)
            s.comp('vector', resid)
            stok = s.dma('scalar', [(x1_scr[tg * 128:(tg + 1) * 128, :], xtile)])
            if tg == 15:
                x1_t15_tok = stok
    ar.pop()

    s.xchg(0, x1_scr[15 * 128:16 * 128, :], halo_l,
           lambda me: xch_h[me, :, :],
           lambda me, w: (xch_f[me:me + 1, 0:2] if w is None
                          else xch_f[me:me + 1, w:w + 1]),
           par_d, wait=[x1_t15_tok, par_tok])

    ar.push()
    sx5 = ar.alloc(128, D, F32)
    dump5 = ar.alloc(128, D, F32)
    stat5 = ar.alloc(128, 8, F32)
    sxn_bf5 = ar.alloc(128, D, BF16)
    xn1T = ar.alloc(128, 8 * T, BF16)
    xnhT = ar.alloc(128, 8 * 3, BF16)
    swin = ar.alloc(128, 8 * DPROJ, BF16)
    ztile = ar.alloc(128, DIN, BF16)
    xbcpre = ar.alloc(128, 2051, F32)
    ctmp = ar.alloc(128, T, F32)
    cbf = ar.alloc(128, T, BF16)
    scw = ar.alloc(128, 18 * 4, F32)
    scb = ar.alloc(128, 18, F32)
    sdtb = ar.alloc(32, 1, F32)
    salog = ar.alloc(32, 1, F32)
    dtraw = ar.alloc(32, T, F32)
    dwork5 = ar.alloc(32, T, F32)
    dtsp = ar.alloc(32, T, F32)
    expA5 = ar.alloc(32, 1, F32)
    sxh = ar.alloc(3, D, F32)
    sxh_bf = ar.alloc(3, D, BF16)
    hstat = ar.alloc(3, 8, F32)

    s.dma('gpsimd', [(swin.rearrange("p (j c) -> p j c", j=8),
                      win.rearrange("(j p) c -> p j c", p=128))])
    for ee in range(NE):
        for cc in range(4):
            s.dma('gpsimd', [(w1nb_scr[ee, cc * 256:(cc + 1) * 256, :],
                              w1n_d[ee, cc * 256:(cc + 1) * 256, :])], wait=[])
        for cc in range(4):
            s.dma('gpsimd', [(w2nb_scr[ee, cc * 1024:(cc + 1) * 1024, :],
                              w2n_d[ee, cc * 1024:(cc + 1) * 1024, :])], wait=[])
    for cc in range(4):
        s.dma('gpsimd', [(sw1b_scr[cc * 256:(cc + 1) * 256, :],
                          sw1_d[cc * 256:(cc + 1) * 256, :])], wait=[])
    for cc in range(4):
        s.dma('gpsimd', [(sw2b_scr[cc * 1024:(cc + 1) * 1024, :],
                          sw2_d[cc * 1024:(cc + 1) * 1024, :])], wait=[])
    s.dma('sync', [(scw.rearrange("p (i k) -> p i k", i=18), cw),
                   (scb, cb), (sdtb, dtb), (salog, alog)])
    s.comp('scalar', lambda e: e.activation(expA5, salog, AF.Exp))

    for t in range(16):
        s.dma('sync', [(sx5, x1_scr[t * 128:(t + 1) * 128, :])])
        emit_ln(sx5, dump5, stat5, sxn_bf5)
        s.dma('sync', [(xn1_scr[t * 128:(t + 1) * 128, :], sx5)])
        for j in range(8):
            s.comp('tensor', lambda e, j=j: e.transpose(ptpb, sxn_bf5[:, j * 128:(j + 1) * 128], ident[:]))
            s.comp('vector', lambda e, t=t, j=j: e.tensor_copy(
                xn1T[:, j * T + t * 128: j * T + (t + 1) * 128], ptpb))

    s.dma('sync', [(sxh, halo_l[125:128, :])])
    s.comp('vector', lambda e: e.tensor_scalar(sxh, sxh, sflag[0:3, 0:1], None, ALU.mult))
    emit_ln(sxh, dump5[0:3, :], hstat, sxh_bf)
    s.comp('vector', lambda e: e.tensor_scalar(sxh_bf, sxh_bf, sflag[0:3, 0:1], None, ALU.mult))
    for j in range(8):
        s.comp('tensor', lambda e, j=j: e.transpose(ptpb[:, 0:3],
                                                    sxh_bf[:, j * 128:(j + 1) * 128],
                                                    ident[0:3, 0:3]))
        s.comp('vector', lambda e, j=j: e.tensor_copy(xnhT[:, j * 3:(j + 1) * 3], ptpb[:, 0:3]))

    for t in range(16):
        def zproj(e, t=t):
            for j in range(8):
                lhsT = xn1T[:, j * T + t * 128: j * T + (t + 1) * 128]
                for g in range(4):
                    last = e.matmul(pA[:, g * 512:(g + 1) * 512], lhsT,
                                    swin[:, j * DPROJ + g * 512: j * DPROJ + (g + 1) * 512],
                                    start=(j == 0), stop=(j == 7))
            return last
        s.comp('tensor', zproj)
        s.comp('vector', lambda e: e.tensor_copy(ztile, pA[:, 0:2048]))
        s.dma('sync', [(z_scr[t * 128:(t + 1) * 128, :], ztile)])

    for i in range(18):
        coff = DIN + i * 128
        for grp in range(2):
            def xbcmm2(e, i=i, coff=coff, grp=grp):
                for j in range(8):
                    lhsT = swin[:, j * DPROJ + coff: j * DPROJ + coff + 128]
                    e.matmul(pB[:, 0:512], lhsT,
                             xn1T[:, j * T + grp * 1024: j * T + grp * 1024 + 512],
                             start=(j == 0), stop=(j == 7))
                    last = e.matmul(pB[:, 512:1024], lhsT,
                                    xn1T[:, j * T + grp * 1024 + 512: j * T + (grp + 1) * 1024],
                                    start=(j == 0), stop=(j == 7))
                return last
            s.comp('tensor', xbcmm2)
            s.comp('vector', lambda e, grp=grp: e.tensor_copy(
                xbcpre[:, 3 + grp * 1024: 3 + (grp + 1) * 1024], pB[:, 0:1024]))
        def xbch(e, i=i, coff=coff):
            for j in range(8):
                last = e.matmul(pB[:, 0:3], swin[:, j * DPROJ + coff: j * DPROJ + coff + 128],
                                xnhT[:, j * 3:(j + 1) * 3], start=(j == 0), stop=(j == 7))
            return last
        s.comp('tensor', xbch)
        s.comp('vector', lambda e: e.tensor_copy(xbcpre[:, 0:3], pB[:, 0:3]))
        def conv2(e, i=i):
            e.tensor_scalar(ctmp, xbcpre[:, 0:T], scw[:, i * 4:i * 4 + 1], None, ALU.mult)
            e.scalar_tensor_tensor(ctmp, xbcpre[:, 1:1 + T], scw[:, i * 4 + 1:i * 4 + 2],
                                   ctmp, ALU.mult, ALU.add)
            e.scalar_tensor_tensor(ctmp, xbcpre[:, 2:2 + T], scw[:, i * 4 + 2:i * 4 + 3],
                                   ctmp, ALU.mult, ALU.add)
            return e.scalar_tensor_tensor(ctmp, xbcpre[:, 3:3 + T], scw[:, i * 4 + 3:i * 4 + 4],
                                          ctmp, ALU.mult, ALU.add)
        s.comp('vector', conv2)
        s.comp('scalar', lambda e, i=i: e.activation(cbf, ctmp, AF.Silu, bias=scb[:, i:i + 1]))
        if i < 16:
            s.dma('sync', [(xsT_scr[:, i, :], cbf)])
        elif i == 16:
            s.dma('sync', [(BT_scr, cbf)])
        else:
            s.dma('sync', [(CT_scr, cbf)])

    for grp in range(2):
        def dtmm(e, grp=grp):
            for j in range(8):
                lhsT = swin[:, j * DPROJ + DIN + CONVD: j * DPROJ + DPROJ]
                e.matmul(pB[0:32, 0:512], lhsT,
                         xn1T[:, j * T + grp * 1024: j * T + grp * 1024 + 512],
                         start=(j == 0), stop=(j == 7))
                last = e.matmul(pB[0:32, 512:1024], lhsT,
                                xn1T[:, j * T + grp * 1024 + 512: j * T + (grp + 1) * 1024],
                                start=(j == 0), stop=(j == 7))
            return last
        s.comp('tensor', dtmm)
        s.comp('vector', lambda e, grp=grp: e.tensor_copy(
            dtraw[:, grp * 1024:(grp + 1) * 1024], pB[0:32, 0:1024]))
    def sp1(e):
        e.tensor_scalar(dtraw, dtraw, sdtb, None, ALU.add)
        return e.tensor_scalar(dtsp, dtraw, 0.0, None, ALU.max)
    s.comp('vector', sp1)
    s.comp('scalar', lambda e: e.activation(dwork5, dtraw, AF.Abs))
    s.comp('scalar', lambda e: e.activation(dwork5, dwork5, AF.Exp, scale=s.const(-1.0, 32)))
    s.comp('scalar', lambda e: e.activation(dwork5, dwork5, AF.Ln, bias=s.const(1.0, 32)))
    s.comp('vector', lambda e: e.tensor_add(dtsp, dtsp, dwork5))
    s.dma('sync', [(dt_scr, dtsp)])
    def dta(e):
        e.tensor_scalar(dwork5, dtsp, expA5, -1.0, ALU.mult, ALU.mult)
        return e.tensor_tensor_scan(dwork5, dwork5, dwork5, 0.0, ALU.add, ALU.bypass)
    s.comp('vector', dta)
    s.dma('sync', [(LT_scr, dwork5)])
    ar.pop()

    ar.push()
    allonesf = ar.alloc(128, 128, F32)
    s.comp('gpsimd', lambda e: e.memset(allonesf, 1.0))
    trimask = ar.alloc(128, 128, F32)
    s.comp('gpsimd', lambda e: e.memset(trimask, 0.0))
    s.comp('gpsimd', lambda e: e.affine_select(trimask, trimask, pattern=[[1, 128]],
                                               base=0, channel_multiplier=-1,
                                               compare_op=ALU.is_ge, fill=-1e30))
    ehsel = ar.alloc(32, 32 * 128, F32)
    s.comp('gpsimd', lambda e: e.memset(ehsel, 1.0))
    s.comp('gpsimd', lambda e: e.affine_select(ehsel, ehsel,
                                               pattern=[[-1, 32], [0, 128]], base=0,
                                               channel_multiplier=1,
                                               compare_op=ALU.is_equal, fill=0.0))
    xsT6 = ar.alloc(128, 16 * T, BF16)
    sBT = ar.alloc(128, T, BF16)
    sCT = ar.alloc(128, T, BF16)
    sdt = ar.alloc(32, T, F32)
    sLT = ar.alloc(32, T, F32)
    expAb = ar.alloc(128, 32, F32)
    sdsk = ar.alloc(128, T, F32)
    h_sb = ar.alloc(128, DIN, F32)
    h_bf = ar.alloc(128, DIN, BF16)
    Y_sb = ar.alloc(128, DIN, F32)
    xs_tok = ar.alloc(128, DIN, BF16)
    Xd_bf = ar.alloc(128, DIN, BF16)
    Xe_bf = ar.alloc(128, DIN, BF16)
    B_tok = ar.alloc(128, 128, BF16)
    G_sb = ar.alloc(128, 128, BF16)
    Dm_sb = ar.alloc(128, 2 * 128, F32)
    E_sb = ar.alloc(128, 2 * 128, F32)
    MT_bf = ar.alloc(128, 2 * 128, BF16)
    aT = ar.alloc(32, 128, F32)
    negaT = ar.alloc(32, 128, F32)
    expaT = ar.alloc(32, 128, F32)
    exprT = ar.alloc(32, 128, F32)
    dt_tok = ar.alloc(128, 32, F32)
    expa_tok = ar.alloc(128, 32, F32)
    expr_tok = ar.alloc(128, 32, F32)
    dtE = ar.alloc(128, 32, F32)
    lamb = ar.alloc(128, 32, F32)
    tws = ar.alloc(128, DIN, F32)

    s.dma('sync', [(xsT6.rearrange("p (i t) -> p i t", i=16), xsT_scr),
                   (sBT, BT_scr), (sCT, CT_scr), (sdt, dt_scr), (sLT, LT_scr),
                   (sdsk, dskt)])
    s.dma('sync', [(expAb, alogt)])
    s.comp('scalar', lambda e: e.activation(expAb, expAb, AF.Exp))
    s.comp('vector', lambda e: e.memset(h_sb, 0.0))

    for c in range(16):
        ck = c * 128
        def prep_a(e, c=c, ck=ck):
            if c == 0:
                e.tensor_copy(aT, sLT[:, ck:ck + 128])
            else:
                e.tensor_scalar(aT, sLT[:, ck:ck + 128], sLT[:, ck - 1:ck],
                                None, ALU.subtract)
            e.tensor_scalar(negaT, aT, -1.0, None, ALU.mult)
            return e.tensor_scalar(exprT, sLT[:, ck:ck + 128],
                                   sLT[:, ck + 127:ck + 128], None, ALU.subtract)
        s.comp('vector', prep_a)
        def prep_exp(e):
            e.activation(expaT, aT, AF.Exp)
            return e.activation(exprT, exprT, AF.Exp, scale=s.const(-1.0, 32))
        s.comp('scalar', prep_exp)
        s.comp('tensor', lambda e, ck=ck: e.transpose(ptpf[:, 0:32], sdt[:, ck:ck + 128],
                                                      identf[0:32, 0:32]))
        s.comp('vector', lambda e: e.tensor_copy(dt_tok, ptpf[:, 0:32]))
        s.comp('tensor', lambda e: e.transpose(ptpf[:, 0:32], expaT, identf[0:32, 0:32]))
        s.comp('vector', lambda e: e.tensor_copy(expa_tok, ptpf[:, 0:32]))
        s.comp('tensor', lambda e: e.transpose(ptpf[:, 0:32], exprT, identf[0:32, 0:32]))
        s.comp('vector', lambda e: e.tensor_copy(expr_tok, ptpf[:, 0:32]))
        s.comp('vector', lambda e: e.tensor_mul(dtE, dt_tok, expAb))
        s.comp('tensor', lambda e: e.matmul(ptpf[:, 0:32], allonesf, dtE, start=True, stop=True))
        s.comp('scalar', lambda e: e.activation(lamb, ptpf[:, 0:32], AF.Exp,
                                                scale=s.const(-1.0)))
        for i in range(16):
            s.comp('tensor', lambda e, i=i, ck=ck: e.transpose(
                ptpb, xsT6[:, i * T + ck: i * T + ck + 128], ident[:]))
            s.comp('vector', lambda e, i=i: e.tensor_copy(
                xs_tok[:, i * 128:(i + 1) * 128], ptpb))
        s.comp('vector', lambda e: e.tensor_mul(dtE, dt_tok, expr_tok))
        for h in range(32):
            s.comp('vector', lambda e, h=h: e.tensor_scalar(
                Xd_bf[:, h * 64:(h + 1) * 64], xs_tok[:, h * 64:(h + 1) * 64],
                dt_tok[:, h:h + 1], None, ALU.mult))
            s.comp('vector', lambda e, h=h: e.tensor_scalar(
                Xe_bf[:, h * 64:(h + 1) * 64], xs_tok[:, h * 64:(h + 1) * 64],
                dtE[:, h:h + 1], None, ALU.mult))
        gmt = s.comp('tensor', lambda e, ck=ck: e.matmul(pB[:, 0:128], sBT[:, ck:ck + 128],
                                                   sCT[:, ck:ck + 128], start=True, stop=True))
        gcp = s.comp('vector', lambda e: e.tensor_copy(G_sb, pB[:, 0:128]))
        dtok = [None] * 32
        atok = [None] * 32
        etok = [None] * 32
        mtok = [None] * 32
        ytok = [None] * 32
        for h in range(32):
            hs = h % 2
            pD = pB[:, 0:128] if hs == 0 else pB[:, 512:640]
            Dms = Dm_sb[:, hs * 128:(hs + 1) * 128]
            Es = E_sb[:, hs * 128:(hs + 1) * 128]
            MTs = MT_bf[:, hs * 128:(hs + 1) * 128]
            def mmD(e, h=h, pD=pD):
                e.matmul(pD, ehsel[:, h * 128:(h + 1) * 128], aT,
                         start=True, stop=False)
                return e.matmul(pD, negaT, ehsel[:, h * 128:(h + 1) * 128],
                                start=False, stop=True)
            dtok[h] = s.comp('tensor', mmD,
                             wait=tw0(gcp if h == 0 else None,
                                      atok[h - 2] if h >= 2 else None))
            atok[h] = s.comp('vector', lambda e, pD=pD, Dms=Dms: e.tensor_add(
                Dms, pD, trimask),
                wait=tw0(dtok[h], etok[h - 2] if h >= 2 else None))
            etok[h] = s.comp('scalar', lambda e, Dms=Dms, Es=Es: e.activation(
                Es, Dms, AF.Exp),
                wait=tw0(atok[h], mtok[h - 2] if h >= 2 else None))
            mtok[h] = s.comp('vector', lambda e, Es=Es, MTs=MTs: e.tensor_mul(
                MTs, Es, G_sb),
                wait=tw0(etok[h], ytok[h - 2] if h >= 2 else None))
            ytok[h] = s.comp('tensor', lambda e, h=h, MTs=MTs: e.matmul(
                pA[:, h * 64:(h + 1) * 64], MTs, Xd_bf[:, h * 64:(h + 1) * 64],
                start=True, stop=True), wait=[mtok[h]])
        def ydsk(e):
            e.tensor_mul(tws, xs_tok, sdsk)
            return e.tensor_add(Y_sb, pA[:, 0:DIN], tws)
        s.comp('vector', ydsk)
        s.comp('vector', lambda e: e.tensor_copy(h_bf, h_sb))
        def mmU(e, ck=ck):
            for g in range(4):
                last = e.matmul(pA[:, g * 512:(g + 1) * 512], sCT[:, ck:ck + 128],
                                h_bf[:, g * 512:(g + 1) * 512], start=True, stop=True)
            return last
        s.comp('tensor', mmU)
        for h in range(32):
            s.comp('vector', lambda e, h=h: e.scalar_tensor_tensor(
                Y_sb[:, h * 64:(h + 1) * 64], pA[:, h * 64:(h + 1) * 64],
                expa_tok[:, h:h + 1], Y_sb[:, h * 64:(h + 1) * 64], ALU.mult, ALU.add))
        s.dma('sync', [(yl_scr[ck:ck + 128, :], Y_sb)])
        s.comp('tensor', lambda e, ck=ck: e.transpose(ptpb, sBT[:, ck:ck + 128], ident[:]))
        s.comp('vector', lambda e: e.tensor_copy(B_tok, ptpb))
        def mmH(e):
            for g in range(4):
                last = e.matmul(pA[:, g * 512:(g + 1) * 512], B_tok,
                                Xe_bf[:, g * 512:(g + 1) * 512], start=True, stop=True)
            return last
        s.comp('tensor', mmH)
        for h in range(32):
            s.comp('vector', lambda e, h=h: e.scalar_tensor_tensor(
                h_sb[:, h * 64:(h + 1) * 64], h_sb[:, h * 64:(h + 1) * 64],
                lamb[:, h:h + 1], pA[:, h * 64:(h + 1) * 64], ALU.mult, ALU.add))
    s.dma('sync', [(hend, h_sb)])
    ar.pop()

    s.xchg(1, hend, hin_l,
           lambda me: xch_s[me, :, :],
           lambda me, w: (xch_f[me:me + 1, 2:4] if w is None
                          else xch_f[me:me + 1, 2 + w:3 + w]),
           par_d)

    ar.push()
    sCT8 = ar.alloc(128, T, BF16)
    sLT8 = ar.alloc(32, T, F32)
    expg = ar.alloc(32, T, F32)
    expg_tok = ar.alloc(128, 32, F32)
    hin = ar.alloc(128, DIN, F32)
    hin_bf = ar.alloc(128, DIN, BF16)
    Yt = ar.alloc(128, DIN, F32)
    zt = ar.alloc(128, DIN, BF16)
    zs = ar.alloc(128, DIN, BF16)
    stat8 = ar.alloc(128, 8, F32)
    dump8 = ar.alloc(128, DIN, F32)
    srms = ar.alloc(128, DIN, F32)
    Ynb = ar.alloc(128, DIN, BF16)
    YnT = ar.alloc(128, 16 * 128, BF16)
    swo8 = ar.alloc(128, 16 * D, BF16)
    sglw = ar.alloc(128, 8 * DIN, BF16)
    sglb = ar.alloc(1, DIN, BF16)
    slnw = ar.alloc(128, D, F32)
    slnb = ar.alloc(128, D, F32)
    xn1t8 = ar.alloc(128, D, F32)
    y18 = ar.alloc(128, D, F32)
    y1bf = ar.alloc(128, D, BF16)
    y1T = ar.alloc(128, 8 * 128, BF16)
    sgat = ar.alloc(128, D, F32)

    s.dma('sync', [(sCT8, CT_scr), (sLT8, LT_scr), (hin, hin_l),
                   (slnw, lnw_d), (slnb, lnb_d), (srms, rmsw_d)])
    s.comp('vector', lambda e: e.tensor_scalar(hin, hin, sflag[:, 0:1], None, ALU.mult))
    s.dma('gpsimd', [(swo8.rearrange("p (j d) -> p j d", j=16),
                      wout_d.rearrange("(j p) d -> p j d", p=128)),
                     (sglw.rearrange("p (j c) -> p j c", j=8),
                      glw_d.rearrange("(j p) c -> p j c", p=128)),
                     (sglb, glb_d)])
    s.comp('vector', lambda e: e.tensor_copy(hin_bf, hin))
    s.comp('scalar', lambda e: e.activation(expg, sLT8, AF.Exp))

    for t in range(16):
        tk = t * 128
        s.dma('sync', [(Yt, yl_scr[tk:tk + 128, :]), (zt, z_scr[tk:tk + 128, :]),
                       (xn1t8, xn1_scr[tk:tk + 128, :])])
        s.comp('tensor', lambda e, tk=tk: e.transpose(ptpf[:, 0:32], expg[:, tk:tk + 128],
                                                      identf[0:32, 0:32]))
        s.comp('vector', lambda e: e.tensor_copy(expg_tok, ptpf[:, 0:32]))
        def mmU8(e, tk=tk):
            for g in range(4):
                last = e.matmul(pA[:, g * 512:(g + 1) * 512], sCT8[:, tk:tk + 128],
                                hin_bf[:, g * 512:(g + 1) * 512], start=True, stop=True)
            return last
        s.comp('tensor', mmU8)
        for h in range(32):
            s.comp('vector', lambda e, h=h: e.scalar_tensor_tensor(
                Yt[:, h * 64:(h + 1) * 64], pA[:, h * 64:(h + 1) * 64],
                expg_tok[:, h:h + 1], Yt[:, h * 64:(h + 1) * 64], ALU.mult, ALU.add))
        s.comp('scalar', lambda e: e.activation(zs, zt, AF.Silu))
        s.comp('vector', lambda e: e.tensor_mul(Yt, Yt, zs))
        s.comp('scalar', lambda e: e.activation(dump8, Yt, AF.Square, accum_out=stat8[:, 0:1]))
        def rstd8(e):
            e.activation(stat8[:, 1:2], stat8[:, 0:1], AF.Ln, bias=s.const(EPS),
                         scale=s.const(1.0 / DIN))
            return e.activation(stat8[:, 1:2], stat8[:, 1:2], AF.Exp, scale=s.const(-0.5))
        s.comp('scalar', rstd8)
        def ynorm(e):
            e.tensor_scalar(dump8, Yt, stat8[:, 1:2], None, ALU.mult)
            return e.tensor_mul(Ynb, dump8, srms)
        ynt = s.comp('vector', ynorm)
        tc8 = [None] * 16
        for j in range(16):
            pt = ptA if j % 2 == 0 else ptB
            trj = s.comp('tensor', lambda e, j=j, pt=pt: e.transpose(
                pt, Ynb[:, j * 128:(j + 1) * 128], ident[:]),
                wait=tw0(ynt if j < 2 else tc8[j - 2]))
            tc8[j] = s.comp('vector', lambda e, j=j, pt=pt: e.tensor_copy(
                YnT[:, j * 128:(j + 1) * 128], pt), wait=[trj])
        def mmWo8(e):
            for j in range(16):
                e.matmul(pB[:, 0:512], YnT[:, j * 128:(j + 1) * 128],
                         swo8[:, j * D: j * D + 512], start=(j == 0), stop=(j == 15))
                last = e.matmul(pB[:, 512:1024], YnT[:, j * 128:(j + 1) * 128],
                                swo8[:, j * D + 512: (j + 1) * D], start=(j == 0), stop=(j == 15))
            return last
        s.comp('tensor', mmWo8)
        def mk_y1(e):
            e.tensor_add(y18, pB[:, 0:1024], xn1t8)
            return e.tensor_copy(y1bf, y18)
        mky = s.comp('vector', mk_y1)
        tc8b = [None] * 8
        for j in range(8):
            pt = ptA if j % 2 == 0 else ptB
            trj = s.comp('tensor', lambda e, j=j, pt=pt: e.transpose(
                pt, y1bf[:, j * 128:(j + 1) * 128], ident[:]),
                wait=tw0(mky if j < 2 else tc8b[j - 2]))
            tc8b[j] = s.comp('vector', lambda e, j=j, pt=pt: e.tensor_copy(
                y1T[:, j * 128:(j + 1) * 128], pt), wait=[trj])
        def mmGlu(e):
            for j in range(8):
                for g in range(4):
                    e.matmul(pA[:, g * 512:(g + 1) * 512], y1T[:, j * 128:(j + 1) * 128],
                             sglw[:, j * DIN + g * 512: j * DIN + (g + 1) * 512],
                             start=(j == 0), stop=False)
            for g in range(4):
                last = e.matmul(pA[:, g * 512:(g + 1) * 512], ones_row[:],
                                sglb[0:1, g * 512:(g + 1) * 512], start=False, stop=True)
            return last
        s.comp('tensor', mmGlu)
        s.comp('scalar', lambda e: e.activation(sgat, pA[:, 1024:2048], AF.Sigmoid))
        def mk_y2(e):
            e.tensor_mul(sgat, pA[:, 0:1024], sgat)
            return e.tensor_add(y18, y18, sgat)
        s.comp('vector', mk_y2)
        def stats_v8(e):
            return e.tensor_scalar(dump8[:, 0:D], y18, 1.0, 0.0, ALU.mult, ALU.add,
                                   accum_out=stat8[:, 2:3])
        s.comp('vector', stats_v8)
        s.comp('scalar', lambda e: e.activation(dump8[:, 0:D], y18, AF.Square,
                                                accum_out=stat8[:, 3:4]))
        def mv8(e):
            e.tensor_scalar(stat8[:, 4:5], stat8[:, 2:3], 1.0 / D, None, ALU.mult)
            e.tensor_scalar(stat8[:, 5:6], stat8[:, 3:4], 1.0 / D, None, ALU.mult)
            e.tensor_scalar(stat8[:, 6:7], stat8[:, 4:5], stat8[:, 4:5], None, ALU.mult)
            return e.tensor_sub(stat8[:, 5:6], stat8[:, 5:6], stat8[:, 6:7])
        s.comp('vector', mv8)
        def rstd28(e):
            e.activation(stat8[:, 6:7], stat8[:, 5:6], AF.Ln, bias=s.const(EPS))
            return e.activation(stat8[:, 6:7], stat8[:, 6:7], AF.Exp, scale=s.const(-0.5))
        s.comp('scalar', rstd28)
        def fin8(e):
            e.tensor_scalar(y18, y18, stat8[:, 4:5], stat8[:, 6:7], ALU.subtract, ALU.mult)
            e.tensor_mul(y18, y18, slnw)
            return e.tensor_add(y18, y18, slnb)
        s.comp('vector', fin8)
        s.dma('sync', [(s1_scr[tk:tk + 128, :], y18)])
    ar.pop()

    ar.push()
    lg = ar.alloc(128, 16 * NE, F32)
    mwork = ar.alloc(128, 16 * NE, F32)
    comb = ar.alloc(128, 16 * NE, F32)
    cwork = ar.alloc(128, 8, F32)
    onecol9 = ar.alloc(128, 1, F32)

    ar.push()
    gateb = ar.alloc(128, NE * D, F32)
    sxf0 = ar.alloc(128, 2 * D, F32)
    dmp9 = ar.alloc(128, D, F32)
    s.dma('sync', [(gateb.rearrange("p (e d) -> p e d", e=NE),
                    gate_d.rearrange("e p d -> p e d"))])
    ld9 = [None] * 16
    dot9 = [None] * 16
    for t in range(16):
        sl9 = t % 2
        sx9 = sxf0[:, sl9 * D:(sl9 + 1) * D]
        ld9[t] = s.dma('sync', [(sx9, xn1_scr[t * 128:(t + 1) * 128, :])],
                       wait=([dot9[t - 2]] if t >= 2 else None))
        def dots9(e, t=t, sx9=sx9):
            for ee in range(NE):
                last = e.scalar_tensor_tensor(dmp9, sx9, 1.0,
                                              gateb[:, ee * D:(ee + 1) * D],
                                              ALU.mult, ALU.mult,
                                              accum_out=lg[:, t * NE + ee:t * NE + ee + 1])
            return last
        dot9[t] = s.comp('vector', dots9, wait=[ld9[t]])
    ar.pop()

    xn1T9 = ar.alloc(128, 8 * T, BF16)
    sb1c9 = ar.alloc(128, 32, F32)
    sb29 = ar.alloc(1, D, BF16)
    w1blk9 = ar.alloc(128, 4096, BF16)
    w2blk9 = ar.alloc(128, 4096, BF16)
    rh9 = ar.alloc(128, 32 * 1024, BF16)
    yac9 = ar.alloc(128, 8 * 1024, F32)
    s1t = ar.alloc(128, D, F32)
    x1t9 = ar.alloc(128, D, F32)
    gt9 = ar.alloc(128, D, F32)
    sxq = ar.alloc(128, 2 * D, BF16)
    sxf = ar.alloc(128, 2 * D, F32)

    ldx = [None] * 16
    cpx = [None] * 16
    prev_tc9 = None
    for t in range(16):
        sl9 = t % 2
        sxfs = sxf[:, sl9 * D:(sl9 + 1) * D]
        sxqs = sxq[:, sl9 * D:(sl9 + 1) * D]
        ldx[t] = s.dma('sync', [(sxfs, xn1_scr[t * 128:(t + 1) * 128, :])],
                       wait=([cpx[t - 2]] if t >= 2 else None))
        cpx[t] = s.comp('vector', lambda e, sxqs=sxqs, sxfs=sxfs: e.tensor_copy(sxqs, sxfs),
                        wait=[ldx[t]])
        tc9 = [None] * 8
        for j in range(8):
            pt = ptA if j % 2 == 0 else ptB
            trj = s.comp('tensor', lambda e, j=j, sxqs=sxqs, pt=pt: e.transpose(
                pt, sxqs[:, j * 128:(j + 1) * 128], ident[:]),
                wait=tw0(cpx[t] if j == 0 else None,
                         tc9[j - 2] if j >= 2 else
                         (prev_tc9[6 + j] if prev_tc9 is not None else None)))
            tc9[j] = s.comp('vector', lambda e, t=t, j=j, pt=pt: e.tensor_copy(
                xn1T9[:, j * T + t * 128: j * T + (t + 1) * 128], pt), wait=[trj])
        prev_tc9 = tc9

    s.dma('sync', [(sb1c9, sb1_d)])
    s.dma('gpsimd', [(sb29, sb2_d)])
    s.comp('vector', lambda e: e.memset(onecol9, 1.0))
    tk1 = [None] * 16
    tk2 = [None] * 16
    tk3 = [None] * 16
    for t in range(16):
        def topk(e, t=t):
            L = lg[:, t * NE:(t + 1) * NE]
            M = mwork[:, t * NE:(t + 1) * NE]
            e.reduce_max(cwork[:, 0:1], L, axis=AX.X)
            e.tensor_scalar(M, L, cwork[:, 0:1], None, ALU.is_ge)
            e.scalar_tensor_tensor(M, M, -1e30, L, ALU.mult, ALU.add)
            e.reduce_max(cwork[:, 1:2], M, axis=AX.X)
            e.tensor_scalar(cwork[:, 2:3], cwork[:, 0:1], -1.0, None, ALU.mult)
            return e.tensor_sub(cwork[:, 3:4], cwork[:, 1:2], cwork[:, 0:1])
        tk1[t] = s.comp('vector', topk,
                        wait=tw0(tk2[t - 1] if t >= 1 else None))
        def topk2(e, t=t):
            e.activation(mwork[:, t * NE:(t + 1) * NE], lg[:, t * NE:(t + 1) * NE],
                         AF.Exp, bias=cwork[:, 2:3])
            return e.activation(cwork[:, 4:5], cwork[:, 3:4], AF.Exp)
        tk2[t] = s.comp('scalar', topk2, wait=[tk1[t]])
        def topk3(e, t=t):
            L = lg[:, t * NE:(t + 1) * NE]
            e.tensor_scalar(cwork[:, 5:6], cwork[:, 4:5], 1.0, None, ALU.add)
            e.reciprocal(cwork[:, 6:7], cwork[:, 5:6])
            e.tensor_scalar(comb[:, t * NE:(t + 1) * NE], L, cwork[:, 1:2], None, ALU.is_ge)
            e.tensor_mul(comb[:, t * NE:(t + 1) * NE], comb[:, t * NE:(t + 1) * NE],
                         mwork[:, t * NE:(t + 1) * NE])
            return e.tensor_scalar(comb[:, t * NE:(t + 1) * NE], comb[:, t * NE:(t + 1) * NE],
                                   cwork[:, 6:7], None, ALU.mult)
        tk3[t] = s.comp('vector', topk3, wait=[tk2[t]])

    st9 = {'mm1_last': None, 'mm2_last': None, 'relu_last': None,
           'b2_last': None, 'acc_last': None}
    for half in range(2):
        hoff = half * 1024
        s.comp('vector', lambda e: e.memset(yac9, 0.0))
        for ee in range(NE + 1):
            shared = (ee == NE)
            w1sr = (sw1b_scr if shared else w1nb_scr[ee]).rearrange("(j p) c -> p j c", p=128)
            w2base = sw2b_scr if shared else w2nb_scr[ee]
            emit_expert_ffn(
                st9,
                feT=lambda j, off, hoff=hoff: xn1T9[:, j * T + hoff + off:
                                                    j * T + hoff + off + 512],
                w1src=lambda fp, w1sr=w1sr: w1sr[:, :, fp * 256:(fp + 1) * 256],
                w2src=lambda fq, ooff, w2base=w2base: w2base[
                    fq * 512:(fq + 1) * 512, ooff:ooff + 512].rearrange(
                    "(j p) c -> p j c", p=128),
                b1pair=(sb1c9, sb1_d if shared else b1n_d[ee, :, :]),
                b2pair=(sb29, sb2_d if shared else b2n_d[0:1, ee * D:(ee + 1) * D]),
                sb1=sb1c9, sb2t=sb29, actf=AF.Gelu, rh_=rh9, yac_=yac9,
                w1b=w1blk9, w2b=w2blk9,
                acc_col=lambda tl, ee=ee, half=half, shared=shared: (
                    onecol9[:, 0:1] if shared
                    else comb[:, ((half * 8 + tl) * NE + ee):
                              ((half * 8 + tl) * NE + ee + 1)]))
        for tl in range(8):
            tg = half * 8 + tl
            s.dma('sync', [(s1t, s1_scr[tg * 128:(tg + 1) * 128, :]),
                           (x1t9, x1_scr[tg * 128:(tg + 1) * 128, :])])
            s.comp('scalar', lambda e, tl=tl: e.activation(gt9, yac9[:, tl * 1024:(tl + 1) * 1024],
                                                           AF.Silu))
            def fin(e):
                e.tensor_mul(gt9, gt9, s1t)
                return e.tensor_add(x1t9, x1t9, gt9)
            s.comp('vector', fin)
            s.dma('sync', [(o_out[tg * 128:(tg + 1) * 128, :], x1t9)])
    ar.pop()

    return s.build()


def kernel_fused(**inputs):
    import time as _time
    nonce = (int(_time.time() * 1e6) & 0x7ffffff0) | 5
    n = NCORES
    static = _prep_static(inputs)
    static['cent_tile'] = np.ascontiguousarray(
        np.tile(inputs['L0_ffn_centroids'][:, None, :], (1, 128, 1)))
    static['gate_tile'] = np.ascontiguousarray(
        np.tile(inputs['L1_ffn_gate_w'][:, None, :], (1, 128, 1)))
    nc = build_fused()
    x = inputs['L0_attn_x']
    in_maps = []
    for c in range(8):
        b, h = c // 2, c % 2
        m = {k: static[k] for k in (
            'wq_t', 'wk_t', 'wv_t', 'bq_row', 'bk_row', 'bv_row', 'wo_t', 'bo_row',
            'cent_tile', 'gamma_t', 'beta_t', 'w1_t', 'b1_c', 'w2_t', 'b2_r',
            'w_in', 'conv_wc', 'conv_bc', 'dtb_c', 'alog_c', 'alog_tile', 'dskip_tile',
            'rmsw_tile', 'w_out', 'glu_w', 'glu_b_row', 'lnw_tile', 'lnb_tile',
            'gate_tile', 'w1n', 'b1n_c', 'w2n', 'b2n_r', 'sw1', 'sb1_c', 'sw2', 'sb2_row')}
        own = x[b, h * T:(h + 1) * T, :]
        sib = x[b, (1 - h) * T:(2 - h) * T, :]
        m['xf'] = np.ascontiguousarray(np.concatenate([own, sib], axis=0))
        own_pos = np.arange(h * T, (h + 1) * T, dtype=np.float32)
        sib_pos = np.arange((1 - h) * T, (2 - h) * T, dtype=np.float32)
        m['pos'] = np.concatenate([own_pos, sib_pos]).reshape(1, SEQ)
        m['flag'] = np.full((128, 1), float(h), np.float32)
        m['par'] = np.array([[h, nonce]], np.int32)
        in_maps.append(m)
    res = run_bass_kernel_spmd(nc, in_maps[:n], core_ids=list(range(n)))
    out = np.zeros((4, SEQ, D), dtype=np.float32)
    for c in range(n):
        b, h = c // 2, c % 2
        out[b, h * T:(h + 1) * T, :] = np.asarray(res.results[c]['out'])
    return out


def _prep_static(inputs):
    g = {}
    g['wq_t'] = np.ascontiguousarray(inputs['L0_attn_Wq'].T)
    g['wk_t'] = np.ascontiguousarray(inputs['L0_attn_Wk'].T)
    g['wv_t'] = np.ascontiguousarray(inputs['L0_attn_Wv'].T)
    g['bq_row'] = np.ascontiguousarray(inputs['L0_attn_bq'].reshape(1, D))
    g['bk_row'] = np.ascontiguousarray(inputs['L0_attn_bk'].reshape(1, D))
    g['bv_row'] = np.ascontiguousarray(inputs['L0_attn_bv'].reshape(1, D))
    g['wo_t'] = np.ascontiguousarray(inputs['L0_attn_Wo'].T)
    g['bo_row'] = np.ascontiguousarray(inputs['L0_attn_bo'].reshape(1, D))
    g['cent_t'] = np.ascontiguousarray(inputs['L0_ffn_centroids'].T)
    g['gamma_t'] = np.ascontiguousarray(inputs['L0_ffn_gamma'].T)
    g['beta_t'] = np.ascontiguousarray(inputs['L0_ffn_beta'].T)
    g['w1_t'] = np.ascontiguousarray(inputs['L0_ffn_W1'].transpose(0, 2, 1))
    g['b1_c'] = np.ascontiguousarray(inputs['L0_ffn_b1'].reshape(NE, 32, 128).transpose(0, 2, 1))
    g['w2_t'] = np.ascontiguousarray(inputs['L0_ffn_W2'].transpose(0, 2, 1))
    g['b2_r'] = np.ascontiguousarray(inputs['L0_ffn_b2'].reshape(1, NE * D))
    g['w_in'] = inputs['L1_attn_W_in']
    g['conv_wc'] = np.ascontiguousarray(inputs['L1_attn_conv_w'].reshape(18, 128, 4).transpose(1, 0, 2))
    g['conv_bc'] = np.ascontiguousarray(inputs['L1_attn_conv_b'].reshape(18, 128).T)
    g['dtb_c'] = np.ascontiguousarray(inputs['L1_attn_dt_bias'].reshape(32, 1))
    g['alog_c'] = np.ascontiguousarray(inputs['L1_attn_A_log'].reshape(32, 1))
    g['alog_tile'] = np.ascontiguousarray(np.tile(inputs['L1_attn_A_log'].reshape(1, 32), (128, 1)))
    g['dskip_tile'] = np.ascontiguousarray(np.tile(np.repeat(inputs['L1_attn_D_skip'], 64).reshape(1, T), (128, 1)))
    g['rmsw_tile'] = np.ascontiguousarray(np.tile(inputs['L1_attn_rms_w'].reshape(1, DIN), (128, 1)))
    g['w_out'] = inputs['L1_attn_W_out']
    g['glu_w'] = inputs['L1_attn_glu_w']
    g['glu_b_row'] = np.ascontiguousarray(inputs['L1_attn_glu_b'].reshape(1, DIN))
    g['lnw_tile'] = np.ascontiguousarray(np.tile(inputs['L1_attn_ln_w'].reshape(1, D), (128, 1)))
    g['lnb_tile'] = np.ascontiguousarray(np.tile(inputs['L1_attn_ln_b'].reshape(1, D), (128, 1)))
    g['gate_t'] = np.ascontiguousarray(inputs['L1_ffn_gate_w'].T)
    g['w1n'] = inputs['L1_ffn_W1']
    g['b1n_c'] = np.ascontiguousarray(inputs['L1_ffn_b1'].reshape(NE, 32, 128).transpose(0, 2, 1))
    g['w2n'] = inputs['L1_ffn_W2']
    g['b2n_r'] = np.ascontiguousarray(inputs['L1_ffn_b2'].reshape(1, NE * D))
    g['sw1'] = inputs['L1_ffn_sW1']
    g['sb1_c'] = np.ascontiguousarray(inputs['L1_ffn_sb1'].reshape(32, 128).T)
    g['sw2'] = inputs['L1_ffn_sW2']
    g['sb2_row'] = np.ascontiguousarray(inputs['L1_ffn_sb2'].reshape(1, D))
    return g


def run_l0a(inputs, static):
    nc = build_l0a()
    x = inputs['L0_attn_x']
    in_maps = []
    for c in range(8):
        b, h = c // 2, c % 2
        m = {k: static[k] for k in ('wq_t', 'wk_t', 'wv_t', 'bq_row', 'bk_row', 'bv_row')}
        m['x'] = np.ascontiguousarray(x[b, h * T:(h + 1) * T, :])
        m['pos'] = np.arange(h * T, (h + 1) * T, dtype=np.float32).reshape(1, T)
        in_maps.append(m)
    res = run_bass_kernel_spmd(nc, in_maps, core_ids=list(range(8)))
    return res.results


def kernel(**inputs):
    return kernel_fused(**inputs)


# version 14 (final)
# speedup vs baseline: 1.0439x; 1.0439x over previous
import numpy as np
import ml_dtypes
import concourse.bass as bass
import concourse.mybir as mybir
from concourse.bass_utils import run_bass_kernel_spmd

F32 = mybir.dt.float32
BF16 = mybir.dt.bfloat16
I32 = mybir.dt.int32
AF = mybir.ActivationFunctionType
ALU = mybir.AluOpType
AX = mybir.AxisListType

BF = ml_dtypes.bfloat16

D = 1024
T = 2048
NH = 16
SEQ = 4096
NE = 8
FF = 4096
DIN = 2048
DST = 128
NHM = 32
CONVD = 2304
DPROJ = 4384
EPS = 1e-5
TWO_PI = float(2 * np.pi)


class Ser:

    ENGS = ('tensor', 'vector', 'scalar', 'sync', 'gpsimd', 'gdma', 'sdma')

    def __init__(self, nc):
        self.nc = nc
        self.sem = {k: nc.alloc_semaphore(f"s_{k}") for k in self.ENGS}
        self.cnt = {k: 0 for k in self.ENGS}
        self.cnt['gpsimd'] = 1
        self.steps = []
        self.consts = {}
        self.cpool = nc.alloc_sbuf_tensor("cpool", [128, 32], F32)
        self.ncp = 0
        self.chain_sem = {}
        self.chain_cnt = {}

    def const(self, v, p=128):
        v = float(v)
        if v not in self.consts:
            assert self.ncp < 32
            self.consts[v] = self.ncp
            self.ncp += 1
        return self.cpool[0:p, self.consts[v]:self.consts[v] + 1]

    def snap(self):
        return [(k, v) for k, v in self.cnt.items() if v > 0]

    def _norm(self, wait):
        if wait is None:
            return self.snap()
        if isinstance(wait, tuple):
            return [wait]
        return [w for w in wait if w is not None]

    def _chain(self, eng_name):
        if eng_name not in self.chain_sem:
            self.chain_sem[eng_name] = self.nc.alloc_semaphore(f"ch_{eng_name}")
            self.chain_cnt[eng_name] = 0
        return self.chain_sem[eng_name]

    def comp(self, eng, fn, wait=None):
        cname = 'sync' if eng == 'sync2' else eng
        waits = self._norm(wait)
        def wrap(e, ew, fn=fn, cname=cname):
            ew(e)
            proxy = _ChainEng(self, cname, e)
            fn(proxy)
            assert proxy.last is not None
            proxy.last.then_inc(self.sem[cname], 1)
        self.steps.append((cname, wrap, waits))
        self.cnt[cname] += 1
        return (cname, self.cnt[cname])

    def dma(self, eng, pairs, wait=None):
        waits = self._norm(wait)
        stream = {'gpsimd': 'gdma', 'scalar': 'sdma'}.get(eng, 'sync')
        body = {'gpsimd': 'gpsimd', 'scalar': 'scalar'}.get(eng, 'sync')
        def fnw(e, ew, pairs=pairs, stream=stream):
            ew(e)
            for (o, i_) in pairs:
                e.dma_start(out=o, in_=i_).then_inc(self.sem[stream], 16)
        self.steps.append((body, fnw, waits))
        self.cnt[stream] += 16 * len(pairs)
        return (stream, self.cnt[stream])

    def xchg(self, idx, pay_src, recv_dst, xch_pay, xch_flag, par_in, wait=None):
        nc = self.nc
        xs = nc.alloc_semaphore(f"xs{idx}")
        ps = nc.alloc_semaphore(f"ps{idx}")
        rs = nc.alloc_semaphore(f"rs{idx}")
        waits = self._norm(wait)

        def fn(e, ew):
            ew(e)
            nonce = nc.values_load(par_in_sb[0:1, 1:2], engines=[mybir.EngineType.SP])
            nreg = e.to_reg(nonce)
            f = nc.alloc_register(mybir.EngineType.SP, f"xf{idx}")
            cnt = nc.alloc_register(mybir.EngineType.SP, f"xc{idx}")
            w0 = nc.alloc_register(mybir.EngineType.SP, f"xw0{idx}")
            w1 = nc.alloc_register(mybir.EngineType.SP, f"xw1{idx}")
            for i, arm in enumerate(e.switch_core_id(n=128)):
                if (i & 1) == 0:
                    e.dma_start(out=xch_pay(0), in_=pay_src).then_inc(xs, 16)
                    e.wait_ge(xs, 16)
                    e.dma_start(out=xch_flag(0, 0), in_=par_in[0:1, 1:2]).then_inc(xs, 16)
                    e.dma_start(out=xch_flag(0, 1), in_=par_in[0:1, 1:2]).then_inc(xs, 16)
                    e.wait_ge(xs, 48)
                else:
                    e.reg_mov(f, 1)
                    e.reg_mov(cnt, 0)
                    with e.While(f):
                        e.reg_add(cnt, cnt, 16)
                        e.dma_start(out=xflag_sb[:], in_=xch_flag(0, None)).then_inc(ps, 16)
                        e.wait_ge(ps, cnt)
                        e.reg_load(w0, xflag_sb[0:1, 0:1])
                        e.reg_load(w1, xflag_sb[0:1, 1:2])
                        e.reg_alu(w0, w0, nreg, ALU.not_equal)
                        e.reg_alu(w1, w1, nreg, ALU.not_equal)
                        e.reg_alu(f, w0, w1, ALU.bitwise_or)
                    e.dma_start(out=recv_dst, in_=xch_pay(0)).then_inc(rs, 16)
                    e.wait_ge(rs, 16)
            e.nop().then_inc(self.sem['sync'], 1)
        par_in_sb = self.xpar_sb
        xflag_sb = self.xflag_sb
        self.steps.append(('sync', fn, waits))
        self.cnt['sync'] += 1
        return ('sync', self.cnt['sync'])

    def build(self):
        nc = self.nc
        finals = [(k, v) for k, v in self.cnt.items() if v > 0]
        OWN_SKIP = set()

        def run_body(e, beng):
            seen = {k: 0 for k in self.ENGS}
            seen['gpsimd'] = 1
            if beng != 'gpsimd':
                e.wait_ge(self.sem['gpsimd'], 1)
            def ew_for(waits):
                def ew(ee, waits=waits):
                    for (k, v) in waits:
                        if k == beng and k in OWN_SKIP:
                            continue
                        if seen[k] >= v:
                            continue
                        ee.wait_ge(self.sem[k], v)
                        seen[k] = v
                return ew
            for st_eng, fn, waits in self.steps:
                if st_eng == beng:
                    fn(e, ew_for(waits))
            if beng == 'sync':
                for k, v in finals:
                    if seen[k] < v and k != 'sync':
                        e.wait_ge(self.sem[k], v)
                        seen[k] = v
                if self.cnt['sync']:
                    e.wait_ge(self.sem['sync'], self.cnt['sync'])

        with nc.Block() as blk:
            @blk.sync
            def _(e):
                run_body(e, 'sync')

            @blk.tensor
            def _(e):
                run_body(e, 'tensor')

            @blk.vector
            def _(e):
                run_body(e, 'vector')

            @blk.scalar
            def _(e):
                run_body(e, 'scalar')

            @blk.gpsimd
            def _(e):
                for v, i in list(self.consts.items()):
                    e.memset(self.cpool[:, i:i + 1], v)
                e.engine_nop().then_inc(self.sem['gpsimd'], 1)
                run_body(e, 'gpsimd')
        return nc


class _ChainEng:

    def __init__(self, ser, name, e):
        self._ser = ser
        self._name = name
        self._e = e
        self.last = None

    def __getattr__(self, attr):
        real = getattr(self._e, attr)
        if attr in ('wait_ge', 'engine_nop'):
            return real

        def wrapper(*a, **k):
            ser, name, e = self._ser, self._name, self._e
            if self.last is not None and name != 'tensor':
                ser._chain(name)
                self.last.then_inc(ser.chain_sem[name], 1)
                ser.chain_cnt[name] += 1
                e.wait_ge(ser.chain_sem[name], ser.chain_cnt[name])
            inst = real(*a, **k)
            self.last = inst
            return inst
        return wrapper


class Arena:

    def __init__(self, nc, words):
        self.t = nc.alloc_sbuf_tensor("arena", [128, words], F32)
        self.words = words
        self.off = 0
        self.marks = []
        self.peak = 0

    def alloc(self, p, n, dtype):
        bytes_per = {F32: 4, I32: 4, BF16: 2}[dtype]
        w = (n * bytes_per + 31) // 32 * 8
        assert self.off + w <= self.words, f"arena overflow {self.off + w} > {self.words}"
        base = self.off
        self.off += w
        self.peak = max(self.peak, self.off)
        view = self.t[0:p, base:base + w]
        if dtype == F32:
            return view[:, 0:n]
        return view.bitcast(dtype)[:, 0:n]

    def push(self):
        self.marks.append(self.off)

    def pop(self):
        self.off = self.marks.pop()


def mk_nc():
    nc = bass.Bass("TRN2", target_bir_lowering=False, detect_race_conditions=False)
    s = Ser(nc)
    return nc, s


def setup_common(nc, s):
    ident = nc.alloc_sbuf_tensor("ident", [128, 128], BF16)
    ones_row = nc.alloc_sbuf_tensor("ones_row", [1, 128], BF16)
    ones_col = nc.alloc_sbuf_tensor("ones_col", [128, 1], BF16)

    def fn(e):
        e.memset(ident[:], 1.0)
        e.memset(ones_row[:], 1.0)
        return e.memset(ones_col[:], 1.0)
    s.comp('gpsimd', fn)
    s.comp('gpsimd', lambda e: e.affine_select(
        ident[:], ident[:], pattern=[[1, 128]], base=0,
        channel_multiplier=-1, compare_op=ALU.is_equal, fill=0.0))
    return ident, ones_row, ones_col


def build_l0a():
    nc, s = mk_nc()
    x = nc.dram_tensor("x", [T, D], F32, kind="ExternalInput").ap()
    pos = nc.dram_tensor("pos", [1, T], F32, kind="ExternalInput").ap()
    wq = nc.dram_tensor("wq_t", [D, D], F32, kind="ExternalInput").ap()
    wk = nc.dram_tensor("wk_t", [D, D], F32, kind="ExternalInput").ap()
    wv = nc.dram_tensor("wv_t", [D, D], F32, kind="ExternalInput").ap()
    bq = nc.dram_tensor("bq_row", [1, D], F32, kind="ExternalInput").ap()
    bk = nc.dram_tensor("bk_row", [1, D], F32, kind="ExternalInput").ap()
    bv = nc.dram_tensor("bv_row", [1, D], F32, kind="ExternalInput").ap()

    o_xn = nc.dram_tensor("xn", [T, D], F32, kind="ExternalOutput").ap()
    o_xnT = nc.dram_tensor("xnT", [128, 8, T], BF16, kind="ExternalOutput").ap()
    o_q = nc.dram_tensor("qrTa", [65, NH, T], BF16, kind="ExternalOutput").ap()
    o_k = nc.dram_tensor("krTa", [65, NH, T], BF16, kind="ExternalOutput").ap()
    o_v = nc.dram_tensor("v", [T, D], BF16, kind="ExternalOutput").ap()

    ident, ones_row, ones_col = setup_common(nc, s)

    sx = nc.alloc_sbuf_tensor("sx", [128, D], F32)
    dump = nc.alloc_sbuf_tensor("dump", [128, D], F32)
    stat = nc.alloc_sbuf_tensor("stat", [128, 8], F32)
    sxn_bf = nc.alloc_sbuf_tensor("sxn_bf", [128, D], BF16)
    xnT = nc.alloc_sbuf_tensor("xnT_s", [128, 8 * T], BF16)
    sw = nc.alloc_sbuf_tensor("sw", [128, 8 * D], BF16)
    sb_row = nc.alloc_sbuf_tensor("sb_row", [1, D], BF16)
    spos = nc.alloc_sbuf_tensor("spos", [1, T], F32)
    qtok = nc.alloc_sbuf_tensor("qtok", [128, D], F32)
    qrope = nc.alloc_sbuf_tensor("qrope", [128, 16 * D], BF16)
    qT_sb = nc.alloc_sbuf_tensor("qT_sb", [128, T], BF16)
    vsb = nc.alloc_sbuf_tensor("vsb", [128, D], BF16)
    rwork = nc.alloc_sbuf_tensor("rwork", [32, 3 * T], F32)
    csT_bf = nc.alloc_sbuf_tensor("csT_bf", [32, 2 * T], BF16)
    cs_tok = nc.alloc_sbuf_tensor("cs_tok", [128, 16 * 64], BF16)
    ri32 = nc.alloc_sbuf_tensor("ri32", [32, T], I32)
    invf = nc.alloc_sbuf_tensor("invf", [1, 32], F32)
    zrow_bf = nc.alloc_sbuf_tensor("zrow_bf", [1, T], BF16)

    poolA = nc.alloc_psum_tensor("poolA", [128, 2048], F32)
    poolB = nc.alloc_psum_tensor("poolB", [128, 1024], F32)
    ptp = nc.alloc_psum_tensor("ptp", [128, 128], BF16)
    pfr = poolA[0:32, :]

    s.dma('sync', [(spos[:], pos)])

    s.comp('gpsimd', lambda e: e.iota(invf[:], pattern=[[1, 32]], base=0,
                                      channel_multiplier=0, allow_small_or_imprecise_dtypes=True))
    s.comp('vector', lambda e: e.memset(zrow_bf[:], 0.0))
    s.comp('scalar', lambda e: e.activation(invf[:], invf[:], AF.Exp,
                                            scale=s.const(-np.log(10000.0) / 32.0, 1)))

    def mm_freq(e):
        for g in range(4):
            last = e.matmul(pfr[:, g * 512:(g + 1) * 512], invf[:],
                            spos[:, g * 512:(g + 1) * 512], start=True, stop=True)
        return last
    s.comp('tensor', mm_freq)

    def rangered(e):
        e.tensor_scalar(rwork[:, 0:T], pfr[:], 1.0 / TWO_PI, 0.25, ALU.mult, ALU.add)
        e.tensor_scalar(rwork[:, T:2 * T], pfr[:], 1.0 / TWO_PI, None, ALU.mult)
        for c in (0, 1):
            u = rwork[:, c * T:(c + 1) * T]
            w = rwork[:, 2 * T:3 * T]
            e.tensor_copy(ri32[:], u)
            e.tensor_copy(w, ri32[:])
            e.tensor_sub(u, u, w)
            e.tensor_scalar(w, u, 0.5, None, ALU.is_ge)
            last = e.tensor_sub(u, u, w)
        return last
    s.comp('vector', rangered)

    def mk_cs(e):
        e.activation(csT_bf[:, 0:T], rwork[:, 0:T], AF.Sin, scale=s.const(TWO_PI, 32))
        return e.activation(csT_bf[:, T:2 * T], rwork[:, T:2 * T], AF.Sin, scale=s.const(TWO_PI, 32))
    s.comp('scalar', mk_cs)

    for t in range(16):
        s.comp('tensor', lambda e, t=t: e.transpose(
            ptp[:, 0:32], csT_bf[:, t * 128:(t + 1) * 128], ident[0:32, 0:32]))
        s.comp('vector', lambda e, t=t: e.tensor_copy(cs_tok[:, t * 64:t * 64 + 32], ptp[:, 0:32]))
        s.comp('tensor', lambda e, t=t: e.transpose(
            ptp[:, 0:32], csT_bf[:, T + t * 128:T + (t + 1) * 128], ident[0:32, 0:32]))
        s.comp('vector', lambda e, t=t: e.tensor_copy(cs_tok[:, t * 64 + 32:t * 64 + 64], ptp[:, 0:32]))

    for t in range(16):
        s.dma('sync', [(sx[:], x[t * 128:(t + 1) * 128, :])])

        def stats_v(e):
            return e.tensor_scalar(dump[:], sx[:], 1.0, 0.0, ALU.mult, ALU.add,
                                   accum_out=stat[:, 0:1])
        s.comp('vector', stats_v)

        def stats(e):
            return e.activation(dump[:], sx[:], AF.Square, accum_out=stat[:, 1:2])
        s.comp('scalar', stats)

        def mv(e):
            e.tensor_scalar(stat[:, 2:3], stat[:, 0:1], 1.0 / D, None, ALU.mult)
            e.tensor_scalar(stat[:, 3:4], stat[:, 1:2], 1.0 / D, None, ALU.mult)
            e.tensor_scalar(stat[:, 4:5], stat[:, 2:3], stat[:, 2:3], None, ALU.mult)
            return e.tensor_sub(stat[:, 3:4], stat[:, 3:4], stat[:, 4:5])
        s.comp('vector', mv)

        def rstd(e):
            e.activation(stat[:, 4:5], stat[:, 3:4], AF.Ln, bias=s.const(EPS))
            return e.activation(stat[:, 4:5], stat[:, 4:5], AF.Exp, scale=s.const(-0.5))
        s.comp('scalar', rstd)

        def norm(e):
            e.tensor_scalar(sx[:], sx[:], stat[:, 2:3], stat[:, 4:5], ALU.subtract, ALU.mult)
            return e.tensor_copy(sxn_bf[:], sx[:])
        s.comp('vector', norm)
        s.dma('sync', [(o_xn[t * 128:(t + 1) * 128, :], sx[:])])

        for j in range(8):
            s.comp('tensor', lambda e, j=j: e.transpose(ptp[:], sxn_bf[:, j * 128:(j + 1) * 128], ident[:]))
            s.comp('vector', lambda e, t=t, j=j: e.tensor_copy(
                xnT[:, j * T + t * 128: j * T + (t + 1) * 128], ptp[:]))

    s.dma('sync', [(o_xnT.rearrange("p j t -> p (j t)"), xnT[:])])

    for which, wdram, bdram, odst, scale in (("q", wq, bq, o_q, 0.125), ("k", wk, bk, o_k, 1.0)):
        s.dma('gpsimd', [(sw.rearrange("p (j d) -> p j d", j=8),
                          wdram.rearrange("(j p) d -> p j d", p=128)),
                         (sb_row[:], bdram)])
        for t in range(16):
            def proj(e, t=t):
                for j in range(8):
                    lhsT = xnT[:, j * T + t * 128: j * T + (t + 1) * 128]
                    e.matmul(poolB[:, 0:512], lhsT, sw[:, j * D + 0: j * D + 512],
                             start=(j == 0), stop=False)
                    e.matmul(poolB[:, 512:1024], lhsT, sw[:, j * D + 512: j * D + 1024],
                             start=(j == 0), stop=False)
                e.matmul(poolB[:, 0:512], ones_row[:], sb_row[:, 0:512], start=False, stop=True)
                return e.matmul(poolB[:, 512:1024], ones_row[:], sb_row[:, 512:1024],
                                start=False, stop=True)
            s.comp('tensor', proj)

            if scale == 1.0:
                s.comp('scalar', lambda e: e.copy(qtok[:], poolB[:]))
            else:
                s.comp('scalar', lambda e, sc=scale: e.activation(
                    qtok[:], poolB[:], AF.Identity, scale=s.const(sc)))

            def rope_t(e, t=t):
                def sv(base_tensor, off, w=32):
                    ap0 = base_tensor[:, 0:1]
                    import dataclasses
                    a = base_tensor[:, off:off + w]
                    return dataclasses.replace(a, ap=[a.ap[0], [64, 16], [1, w]])
                lo = sv(qtok, 0)
                hi = sv(qtok, 32)
                olo = sv(qrope, t * D + 0)
                ohi = sv(qrope, t * D + 32)
                co = cs_tok[:, t * 64:t * 64 + 32]
                si = cs_tok[:, t * 64 + 32:t * 64 + 64]
                import dataclasses
                cob = dataclasses.replace(co, ap=[co.ap[0], [0, 16], [1, 32]])
                sib = dataclasses.replace(si, ap=[si.ap[0], [0, 16], [1, 32]])
                w1 = sv(dump, 0)
                w2 = sv(dump, 32)
                e.tensor_tensor(w1, lo, cob, ALU.mult)
                e.tensor_tensor(w2, hi, sib, ALU.mult)
                e.tensor_tensor(olo, w1, w2, ALU.subtract)
                e.tensor_tensor(w1, hi, cob, ALU.mult)
                e.tensor_tensor(w2, lo, sib, ALU.mult)
                return e.tensor_tensor(ohi, w1, w2, ALU.add)
            s.comp('vector', rope_t)

        for j in range(8):
            def tph(e, j=j):
                for t in range(16):
                    last = e.transpose(ptp[:], qrope[:, t * D + j * 128:t * D + (j + 1) * 128],
                                       ident[:])
                    last.then_inc(s.flow, 0) if False else None
                return last
            pass
        for j in range(8):
            for t in range(16):
                s.comp('tensor', lambda e, j=j, t=t: e.transpose(
                    ptp[:], qrope[:, t * D + j * 128:t * D + (j + 1) * 128], ident[:]))
                s.comp('vector', lambda e, t=t: e.tensor_copy(
                    qT_sb[:, t * 128:(t + 1) * 128], ptp[:]))
            s.dma('sync', [(odst[0:64, 2 * j, :], qT_sb[0:64, :]),
                           (odst[0:64, 2 * j + 1, :], qT_sb[64:128, :]),
                           (odst[64:65, 2 * j, :], zrow_bf[:]),
                           (odst[64:65, 2 * j + 1, :], zrow_bf[:])])

    s.dma('gpsimd', [(sw.rearrange("p (j d) -> p j d", j=8),
                      wv.rearrange("(j p) d -> p j d", p=128)),
                     (sb_row[:], bv)])
    for t in range(16):
        def vproj(e, t=t):
            for j in range(8):
                lhsT = xnT[:, j * T + t * 128: j * T + (t + 1) * 128]
                e.matmul(poolB[:, 0:512], lhsT, sw[:, j * D + 0: j * D + 512],
                         start=(j == 0), stop=False)
                e.matmul(poolB[:, 512:1024], lhsT, sw[:, j * D + 512: j * D + 1024],
                         start=(j == 0), stop=False)
            e.matmul(poolB[:, 0:512], ones_row[:], sb_row[:, 0:512], start=False, stop=True)
            return e.matmul(poolB[:, 512:1024], ones_row[:], sb_row[:, 512:1024],
                            start=False, stop=True)
        s.comp('tensor', vproj)
        s.comp('vector', lambda e: e.tensor_copy(vsb[:], poolB[:]))
        s.dma('sync', [(o_v[t * 128:(t + 1) * 128, :], vsb[:])])

    return s.build()


def build_b1():
    nc, s = mk_nc()
    qr = nc.dram_tensor("qrTa", [65, NH, T], BF16, kind="ExternalInput").ap()
    kr = nc.dram_tensor("krTa_full", [65, NH, SEQ], BF16, kind="ExternalInput").ap()
    vf = nc.dram_tensor("v_full", [SEQ, D], BF16, kind="ExternalInput").ap()
    wo = nc.dram_tensor("wo_t", [D, D], F32, kind="ExternalInput").ap()
    bo = nc.dram_tensor("bo_row", [1, D], F32, kind="ExternalInput").ap()
    o_s = nc.dram_tensor("s_attn", [T, D], F32, kind="ExternalOutput").ap()

    ident, ones_row, ones_col = setup_common(nc, s)
    onesf = nc.alloc_sbuf_tensor("onesf", [1, 64], F32)
    s.comp('vector', lambda e: e.memset(onesf[:], 1.0))

    qh = nc.alloc_sbuf_tensor("qh", [65, T], BF16)
    kh = nc.alloc_sbuf_tensor("kh", [65, SEQ], BF16)
    vaug = nc.alloc_sbuf_tensor("vaug", [128, 32 * 65], BF16)
    qsq = nc.alloc_sbuf_tensor("qsq", [64, SEQ], BF16)
    P_bf = nc.alloc_sbuf_tensor("P_bf", [128, T], BF16)
    rZ = nc.alloc_sbuf_tensor("rZ", [1, T], F32)
    lnk = nc.alloc_sbuf_tensor("lnk", [1, 4], F32)
    ubr = nc.alloc_sbuf_tensor("ubr", [1, T], F32)
    rZb_sb = nc.alloc_sbuf_tensor("rZb_sb", [64, T], F32)
    attnT = nc.alloc_sbuf_tensor("attnT", [64, NH * T], BF16)
    sw = nc.alloc_sbuf_tensor("sw", [64, NH * D], BF16)
    sbo = nc.alloc_sbuf_tensor("sbo", [1, D], BF16)
    ssb = nc.alloc_sbuf_tensor("ssb", [128, D], F32)

    pS = nc.alloc_psum_tensor("pS", [128, 2048], F32)
    pO = nc.alloc_psum_tensor("pO", [65, 2048], F32)

    s.dma('gpsimd', [(sw.rearrange("p (h d) -> p h d", h=NH),
                      wo.rearrange("(h p) d -> p h d", p=64)),
                     (sbo[:], bo)])

    for h in range(NH):
        s.dma('sync', [(qh[:], qr[:, h, :]),
                       (kh[:], kr[:, h, :]),
                       (vaug.rearrange("p (c w) -> p c w", w=65)[:, :, 0:64],
                        vf.rearrange("(c p) d -> p c d", p=128)[:, :, h * 64:(h + 1) * 64])])
        s.comp('vector', lambda e: e.memset(
            vaug.rearrange("p (c w) -> p c w", w=65)[:, :, 64:65], 1.0))
        s.comp('vector', lambda e: e.memset(kh[64:65, :], 1.0))
        def sq_q(e):
            return e.tensor_mul(qsq[:, 0:T], qh[0:64, :], qh[0:64, :])
        s.comp('vector', sq_q)

        def mm_qsq(e):
            for g in range(4):
                last = e.matmul(pO[0:1, g * 512:(g + 1) * 512], ones_col[0:64, :],
                                qsq[:, g * 512:(g + 1) * 512], start=True, stop=True)
            return last
        s.comp('tensor', mm_qsq)
        s.comp('scalar', lambda e: e.activation(ubr[:], pO[0:1, 0:T], AF.Ln))

        def sq_k(e):
            return e.tensor_mul(qsq[:], kh[0:64, :], kh[0:64, :])
        s.comp('vector', sq_k)

        def mm_ksq2(e):
            for g in range(8):
                last = e.matmul(pS[0:1, (g % 4) * 512:((g % 4) + 1) * 512],
                                ones_col[0:64, :],
                                qsq[:, g * 512:(g + 1) * 512],
                                start=(g < 4), stop=(g >= 4))
            return last
        s.comp('tensor', mm_ksq2)
        def kmax2(e):
            return e.reduce_max(lnk[:, 0:1], pS[0:1, 0:T], axis=AX.X)
        s.comp('vector', kmax2)
        s.comp('scalar', lambda e: e.activation(lnk[:, 1:2], lnk[:, 0:1], AF.Ln))

        def mk_ub(e):
            return e.tensor_scalar(ubr[:], ubr[:], lnk[0:1, 1:2], None, ALU.add)
        s.comp('vector', mk_ub)
        s.comp('scalar', lambda e: e.activation(ubr[:], ubr[:], AF.Exp, scale=s.const(0.5, 1)))
        s.comp('vector', lambda e: e.tensor_scalar(qh[64:65, :], ubr[:], -1.0, None, ALU.mult))

        for c in range(32):
            def mmS(e, c=c):
                for g in range(4):
                    last = e.matmul(pS[:, g * 512:(g + 1) * 512],
                                    kh[:, c * 128:(c + 1) * 128],
                                    qh[:, g * 512:(g + 1) * 512], start=True, stop=True)
                return last
            s.comp('tensor', mmS)
            s.comp('scalar', lambda e: e.activation(P_bf[:], pS[:], AF.Exp))

            def mmV(e, c=c):
                for g in range(4):
                    last = e.matmul(pO[:, g * 512:(g + 1) * 512],
                                    vaug[:, c * 65:(c + 1) * 65],
                                    P_bf[:, g * 512:(g + 1) * 512],
                                    start=(c == 0), stop=(c == 31))
                return last
            s.comp('tensor', mmV)

        s.comp('vector', lambda e: e.reciprocal(rZ[:], pO[64:65, :]))

        def mm_rzb(e):
            for g in range(4):
                last = e.matmul(pS[0:64, g * 512:(g + 1) * 512], onesf[:],
                                rZ[:, g * 512:(g + 1) * 512], start=True, stop=True)
            return last
        s.comp('tensor', mm_rzb)
        s.comp('vector', lambda e: e.tensor_copy(rZb_sb[:], pS[0:64, 0:T]))
        s.comp('vector', lambda e, h=h: e.tensor_mul(
            attnT[:, h * T:(h + 1) * T], pO[0:64, :], rZb_sb[:]))

    for t in range(16):
        def wop(e, t=t):
            for h in range(NH):
                lhsT = attnT[:, h * T + t * 128:h * T + (t + 1) * 128]
                rhs0 = sw[:, h * D + 0: h * D + 512]
                rhs1 = sw[:, h * D + 512: h * D + 1024]
                e.matmul(pS[:, 0:512], lhsT, rhs0, start=(h == 0), stop=False)
                e.matmul(pS[:, 512:1024], lhsT, rhs1, start=(h == 0), stop=False)
            e.matmul(pS[:, 0:512], ones_row[:], sbo[:, 0:512], start=False, stop=True)
            return e.matmul(pS[:, 512:1024], ones_row[:], sbo[:, 512:1024],
                            start=False, stop=True)
        s.comp('tensor', wop)
        s.comp('vector', lambda e: e.tensor_copy(ssb[:], pS[:, 0:1024]))
        s.dma('sync', [(o_s[t * 128:(t + 1) * 128, :], ssb[:])])

    return s.build()


def run_b1(inputs, static, r0):
    nc = build_b1()
    in_maps = []
    for c in range(8):
        b = c // 2
        kr_full = np.concatenate([np.asarray(r0[2 * b]['krTa']),
                                  np.asarray(r0[2 * b + 1]['krTa'])], axis=2)
        v_full = np.concatenate([np.asarray(r0[2 * b]['v']),
                                 np.asarray(r0[2 * b + 1]['v'])], axis=0)
        m = {'qrTa': np.asarray(r0[c]['qrTa']), 'krTa_full': np.ascontiguousarray(kr_full),
             'v_full': np.ascontiguousarray(v_full),
             'wo_t': static['wo_t'], 'bo_row': static['bo_row']}
        in_maps.append(m)
    res = run_bass_kernel_spmd(nc, in_maps, core_ids=list(range(8)))
    return res.results


def build_b2():
    nc, s = mk_nc()
    xnT_d = nc.dram_tensor("xnT", [128, 8, T], BF16, kind="ExternalInput").ap()
    xn_d = nc.dram_tensor("xn", [T, D], F32, kind="ExternalInput").ap()
    x_d = nc.dram_tensor("x", [T, D], F32, kind="ExternalInput").ap()
    s_d = nc.dram_tensor("s_attn", [T, D], F32, kind="ExternalInput").ap()
    cent = nc.dram_tensor("cent_t", [D, NE], F32, kind="ExternalInput").ap()
    gam = nc.dram_tensor("gamma_t", [D, NE], F32, kind="ExternalInput").ap()
    bet = nc.dram_tensor("beta_t", [D, NE], F32, kind="ExternalInput").ap()
    w1 = nc.dram_tensor("w1_t", [NE, D, FF], F32, kind="ExternalInput").ap()
    b1 = nc.dram_tensor("b1_c", [NE, 128, 32], F32, kind="ExternalInput").ap()
    w2 = nc.dram_tensor("w2_t", [NE, FF, D], F32, kind="ExternalInput").ap()
    b2d = nc.dram_tensor("b2_r", [1, NE * D], F32, kind="ExternalInput").ap()
    o_x1 = nc.dram_tensor("x1", [T, D], F32, kind="ExternalOutput").ap()

    ident, ones_row, ones_col = setup_common(nc, s)
    identf = nc.alloc_sbuf_tensor("identf", [128, 128], F32)
    s.comp('gpsimd', lambda e: e.memset(identf[:], 1.0))
    s.comp('gpsimd', lambda e: e.affine_select(identf[:], identf[:], pattern=[[1, 128]],
                                               base=0, channel_multiplier=-1,
                                               compare_op=ALU.is_equal, fill=0.0))

    xnT = nc.alloc_sbuf_tensor("xnT_s", [128, 8 * T], BF16)
    xsq = nc.alloc_sbuf_tensor("xsq", [128, T], BF16)
    RSTDF = nc.alloc_sbuf_tensor("RSTDF", [128, T], F32)
    scent = nc.alloc_sbuf_tensor("scent", [128, 8 * NE], BF16)
    sgam = nc.alloc_sbuf_tensor("sgam", [128, 8 * NE], F32)
    sbet = nc.alloc_sbuf_tensor("sbet", [128, 8 * NE], F32)
    scoT = nc.alloc_sbuf_tensor("scoT", [8, T], F32)
    sco = nc.alloc_sbuf_tensor("sco", [128, 16 * NE], F32)
    m1c = nc.alloc_sbuf_tensor("m1c", [128, 16], F32)
    ohc = nc.alloc_sbuf_tensor("ohc", [128, 16 * NE], F32)
    alc = nc.alloc_sbuf_tensor("alc", [128, 16], F32)
    sb2 = nc.alloc_sbuf_tensor("sb2", [1, D], BF16)
    sb1c = nc.alloc_sbuf_tensor("sb1c", [128, 32], F32)
    nfeT = nc.alloc_sbuf_tensor("nfeT", [128, 8 * 1024], BF16)
    w1blk = nc.alloc_sbuf_tensor("w1blk", [128, 8 * 128], BF16)
    w2blk = nc.alloc_sbuf_tensor("w2blk", [128, 512], BF16)
    rh = nc.alloc_sbuf_tensor("rh", [128, 32 * 1024], BF16)
    yac = nc.alloc_sbuf_tensor("yac", [128, 8 * 1024], F32)
    xtile = nc.alloc_sbuf_tensor("xtile", [128, D], F32)
    stile = nc.alloc_sbuf_tensor("stile", [128, D], F32)
    ntile = nc.alloc_sbuf_tensor("ntile", [128, D], F32)
    gtile = nc.alloc_sbuf_tensor("gtile", [128, D], F32)

    pA = nc.alloc_psum_tensor("pA", [128, 2048], F32)
    pY = nc.alloc_psum_tensor("pY", [128, 2048], F32)

    s.dma('sync', [(xnT.rearrange("p (j t) -> p j t", j=8), xnT_d)])
    s.dma('gpsimd', [(scent.rearrange("p (j e) -> p j e", j=8),
                      cent.rearrange("(j p) e -> p j e", p=128))])
    s.dma('sync', [(sgam.rearrange("p (j e) -> p j e", j=8),
                    gam.rearrange("(j p) e -> p j e", p=128)),
                   (sbet.rearrange("p (j e) -> p j e", j=8),
                    bet.rearrange("(j p) e -> p j e", p=128))])

    def mm_sco(e):
        for j in range(8):
            for g in range(4):
                last = e.matmul(pA[0:8, g * 512:(g + 1) * 512],
                                scent[:, j * NE:(j + 1) * NE],
                                xnT[:, j * T + g * 512:j * T + (g + 1) * 512],
                                start=(j == 0), stop=(j == 7))
        return last
    s.comp('tensor', mm_sco)
    s.comp('vector', lambda e: e.tensor_copy(scoT[:], pA[0:8, 0:T]))
    for t in range(16):
        s.comp('tensor', lambda e, t=t: e.transpose(
            pA[:, 1024:1024 + 8], scoT[:, t * 128:(t + 1) * 128], identf[0:8, 0:8]))
        s.comp('vector', lambda e, t=t: e.tensor_copy(sco[:, t * NE:(t + 1) * NE],
                                                      pA[:, 1024:1024 + 8]))
    for t in range(16):
        def sel(e, t=t):
            e.reduce_max(m1c[:, t:t + 1], sco[:, t * NE:(t + 1) * NE], axis=AX.X)
            return e.tensor_scalar(ohc[:, t * NE:(t + 1) * NE], sco[:, t * NE:(t + 1) * NE],
                                   m1c[:, t:t + 1], None, ALU.is_ge)
        s.comp('vector', sel)
        s.comp('scalar', lambda e, t=t: e.activation(alc[:, t:t + 1], m1c[:, t:t + 1], AF.Sigmoid))

    def mk_rstdf(e):
        for j in range(8):
            e.tensor_mul(xsq[:], xnT[:, j * T:(j + 1) * T], xnT[:, j * T:(j + 1) * T])
        return e.engine_nop()
    for j in range(8):
        s.comp('vector', lambda e, j=j: e.tensor_mul(
            xsq[:], xnT[:, j * T:(j + 1) * T], xnT[:, j * T:(j + 1) * T]))
        def mmsq(e, j=j):
            for g in range(4):
                last = e.matmul(pA[0:1, g * 512:(g + 1) * 512], ones_col[:],
                                xsq[:, g * 512:(g + 1) * 512],
                                start=(j == 0), stop=(j == 7))
            return last
        s.comp('tensor', mmsq)

    def rstdf_row(e):
        rsr = RSTDF[0:1, :]
        e.activation(rsr, pA[0:1, 0:T], AF.Ln, bias=s.const(EPS, 1), scale=s.const(1.0 / D, 1))
        return e.activation(rsr, rsr, AF.Exp, scale=s.const(-0.5, 1))
    s.comp('scalar', rstdf_row)
    onesf = nc.alloc_sbuf_tensor("onesf", [1, 128], F32)
    s.comp('vector', lambda e: e.memset(onesf[:], 1.0))

    def mm_rb(e):
        rsr = RSTDF[0:1, :]
        for g in range(4):
            last = e.matmul(pA[:, g * 512:(g + 1) * 512], onesf[:],
                            rsr[:, g * 512:(g + 1) * 512], start=True, stop=True)
        return last
    s.comp('tensor', mm_rb)
    s.comp('vector', lambda e: e.tensor_copy(RSTDF[:], pA[:, 0:T]))
    for j in range(8):
        s.comp('vector', lambda e, j=j: e.tensor_mul(
            xnT[:, j * T:(j + 1) * T], xnT[:, j * T:(j + 1) * T], RSTDF[:]))

    for half in range(2):
        hoff = half * 1024
        s.comp('vector', lambda e: e.memset(yac[:], 0.0))
        for ee in range(NE):
            s.dma('sync', [(sb1c[:], b1[ee, :, :])])
            s.dma('gpsimd', [(sb2[:], b2d[0:1, ee * D:(ee + 1) * D])])
            for j in range(8):
                s.comp('vector', lambda e, ee=ee, j=j, hoff=hoff: e.tensor_scalar(
                    nfeT[:, j * 1024:(j + 1) * 1024],
                    xnT[:, j * T + hoff:j * T + hoff + 1024],
                    sgam[:, j * NE + ee:j * NE + ee + 1],
                    sbet[:, j * NE + ee:j * NE + ee + 1], ALU.mult, ALU.add))
            for f in range(32):
                s.dma('gpsimd', [(w1blk.rearrange("p (j c) -> p j c", j=8),
                                  w1[ee].rearrange("(j p) c -> p j c", p=128)[:, :, f * 128:(f + 1) * 128])])
                def mmw1(e, f=f):
                    for j in range(8):
                        e.matmul(pA[:, 0:512], w1blk[:, j * 128:(j + 1) * 128],
                                 nfeT[:, j * 1024:j * 1024 + 512],
                                 start=(j == 0), stop=(j == 7))
                        last = e.matmul(pA[:, 512:1024], w1blk[:, j * 128:(j + 1) * 128],
                                        nfeT[:, j * 1024 + 512:(j + 1) * 1024],
                                        start=(j == 0), stop=(j == 7))
                    return last
                s.comp('tensor', mmw1)
                s.comp('scalar', lambda e, f=f: e.activation(
                    rh[:, f * 1024:(f + 1) * 1024], pA[:, 0:1024], AF.Relu,
                    bias=sb1c[:, f:f + 1]))
            for oh in range(2):
                ooff = oh * 512
                for f in range(32):
                    s.dma('gpsimd', [(w2blk[:], w2[ee, f * 128:(f + 1) * 128,
                                                   ooff:ooff + 512])])
                    def mmw2(e, f=f):
                        for tl in range(8):
                            ps = pA if tl < 4 else pY
                            last = e.matmul(ps[:, (tl % 4) * 512:(tl % 4 + 1) * 512],
                                            rh[:, f * 1024 + tl * 128:f * 1024 + (tl + 1) * 128],
                                            w2blk[:], start=(f == 0), stop=False)
                        return last
                    s.comp('tensor', mmw2)

                def mmb2(e, ee=ee, ooff=ooff):
                    for tl in range(8):
                        ps = pA if tl < 4 else pY
                        last = e.matmul(ps[:, (tl % 4) * 512:(tl % 4 + 1) * 512],
                                        ones_row[:], sb2[:, ooff:ooff + 512],
                                        start=False, stop=True)
                    return last
                s.comp('tensor', mmb2)
                for tl in range(8):
                    def acc(e, ee=ee, tl=tl, ooff=ooff, half=half):
                        ps = pA if tl < 4 else pY
                        tg = half * 8 + tl
                        return e.scalar_tensor_tensor(
                            yac[:, tl * 1024 + ooff:tl * 1024 + ooff + 512],
                            ps[:, (tl % 4) * 512:(tl % 4 + 1) * 512],
                            ohc[:, tg * NE + ee:tg * NE + ee + 1],
                            yac[:, tl * 1024 + ooff:tl * 1024 + ooff + 512],
                            ALU.mult, ALU.add)
                    s.comp('vector', acc)

        for tl in range(8):
            tg = half * 8 + tl
            s.dma('sync', [(ntile[:], xn_d[tg * 128:(tg + 1) * 128, :]),
                           (stile[:], s_d[tg * 128:(tg + 1) * 128, :]),
                           (xtile[:], x_d[tg * 128:(tg + 1) * 128, :])])
            def blend(e, tl=tl, tg=tg):
                return e.scalar_tensor_tensor(gtile[:], yac[:, tl * 1024:(tl + 1) * 1024],
                                              alc[:, tg:tg + 1], ntile[:], ALU.mult, ALU.add)
            s.comp('vector', blend)
            s.comp('scalar', lambda e: e.activation(gtile[:], gtile[:], AF.Silu))
            def resid(e):
                e.tensor_mul(gtile[:], gtile[:], stile[:])
                return e.tensor_add(xtile[:], xtile[:], gtile[:])
            s.comp('vector', resid)
            s.dma('sync', [(o_x1[tg * 128:(tg + 1) * 128, :], xtile[:])])

    return s.build()


def run_b2(inputs, static, r0, r1):
    nc = build_b2()
    x = inputs['L0_attn_x']
    in_maps = []
    for c in range(8):
        b, h = c // 2, c % 2
        m = {k: static[k] for k in ('cent_t', 'gamma_t', 'beta_t', 'w1_t', 'b1_c',
                                    'w2_t', 'b2_r')}
        m['xnT'] = np.asarray(r0[c]['xnT'])
        m['xn'] = np.asarray(r0[c]['xn'])
        m['x'] = np.ascontiguousarray(x[b, h * T:(h + 1) * T, :])
        m['s_attn'] = np.asarray(r1[c]['s_attn'])
        in_maps.append(m)
    res = run_bass_kernel_spmd(nc, in_maps, core_ids=list(range(8)))
    return res.results


def build_l1a():
    nc, s = mk_nc()
    x1 = nc.dram_tensor("x1", [T, D], F32, kind="ExternalInput").ap()
    x1h = nc.dram_tensor("x1h", [3, D], F32, kind="ExternalInput").ap()
    win = nc.dram_tensor("w_in", [D, DPROJ], F32, kind="ExternalInput").ap()
    cw = nc.dram_tensor("conv_wc", [128, 18, 4], F32, kind="ExternalInput").ap()
    cb = nc.dram_tensor("conv_bc", [128, 18], F32, kind="ExternalInput").ap()
    dtb = nc.dram_tensor("dtb_c", [32, 1], F32, kind="ExternalInput").ap()
    alog = nc.dram_tensor("alog_c", [32, 1], F32, kind="ExternalInput").ap()

    o_xn1 = nc.dram_tensor("xn1", [T, D], F32, kind="ExternalOutput").ap()
    o_xn1T = nc.dram_tensor("xn1T", [128, 8, T], BF16, kind="ExternalOutput").ap()
    o_z = nc.dram_tensor("z", [T, DIN], BF16, kind="ExternalOutput").ap()
    o_xsT = nc.dram_tensor("xsT", [128, 16, T], BF16, kind="ExternalOutput").ap()
    o_BT = nc.dram_tensor("BT", [128, T], BF16, kind="ExternalOutput").ap()
    o_CT = nc.dram_tensor("CT", [128, T], BF16, kind="ExternalOutput").ap()
    o_dt = nc.dram_tensor("dtsp", [32, T], F32, kind="ExternalOutput").ap()
    o_LT = nc.dram_tensor("LT", [32, T], F32, kind="ExternalOutput").ap()

    ident, ones_row, ones_col = setup_common(nc, s)

    sx = nc.alloc_sbuf_tensor("sx", [128, D], F32)
    dump = nc.alloc_sbuf_tensor("dump", [128, D], F32)
    stat = nc.alloc_sbuf_tensor("stat", [128, 8], F32)
    sxn_bf = nc.alloc_sbuf_tensor("sxn_bf", [128, D], BF16)
    xn1T = nc.alloc_sbuf_tensor("xn1T_s", [128, 8 * T], BF16)
    xnhT = nc.alloc_sbuf_tensor("xnhT", [128, 8 * 3], BF16)
    swin = nc.alloc_sbuf_tensor("swin", [128, 8 * DPROJ], BF16)
    ztile = nc.alloc_sbuf_tensor("ztile", [128, DIN], BF16)
    xbcpre = nc.alloc_sbuf_tensor("xbcpre", [128, 2051], F32)
    ctmp = nc.alloc_sbuf_tensor("ctmp", [128, T], F32)
    cbf = nc.alloc_sbuf_tensor("cbf", [128, T], BF16)
    scw = nc.alloc_sbuf_tensor("scw", [128, 18 * 4], F32)
    scb = nc.alloc_sbuf_tensor("scb", [128, 18], F32)
    sdtb = nc.alloc_sbuf_tensor("sdtb", [32, 1], F32)
    salog = nc.alloc_sbuf_tensor("salog", [32, 1], F32)
    dtraw = nc.alloc_sbuf_tensor("dtraw", [32, T], F32)
    dwork = nc.alloc_sbuf_tensor("dwork", [32, T], F32)
    dtsp = nc.alloc_sbuf_tensor("dtsp_s", [32, T], F32)
    expA = nc.alloc_sbuf_tensor("expA", [32, 1], F32)

    pA = nc.alloc_psum_tensor("pA", [128, 2048], F32)
    pB = nc.alloc_psum_tensor("pB", [128, 1024], F32)
    ptp = nc.alloc_psum_tensor("ptp", [128, 128], BF16)

    s.dma('gpsimd', [(swin.rearrange("p (j c) -> p j c", j=8),
                      win.rearrange("(j p) c -> p j c", p=128))])
    s.dma('sync', [(scw.rearrange("p (i k) -> p i k", i=18), cw),
                   (scb[:], cb), (sdtb[:], dtb), (salog[:], alog)])
    s.comp('scalar', lambda e: e.activation(expA[:], salog[:], AF.Exp))

    for t in range(16):
        s.dma('sync', [(sx[:], x1[t * 128:(t + 1) * 128, :])])
        def stats_v(e):
            return e.tensor_scalar(dump[:], sx[:], 1.0, 0.0, ALU.mult, ALU.add,
                                   accum_out=stat[:, 0:1])
        s.comp('vector', stats_v)
        s.comp('scalar', lambda e: e.activation(dump[:], sx[:], AF.Square,
                                                accum_out=stat[:, 1:2]))
        def mv(e):
            e.tensor_scalar(stat[:, 2:3], stat[:, 0:1], 1.0 / D, None, ALU.mult)
            e.tensor_scalar(stat[:, 3:4], stat[:, 1:2], 1.0 / D, None, ALU.mult)
            e.tensor_scalar(stat[:, 4:5], stat[:, 2:3], stat[:, 2:3], None, ALU.mult)
            return e.tensor_sub(stat[:, 3:4], stat[:, 3:4], stat[:, 4:5])
        s.comp('vector', mv)
        def rstd(e):
            e.activation(stat[:, 4:5], stat[:, 3:4], AF.Ln, bias=s.const(EPS))
            return e.activation(stat[:, 4:5], stat[:, 4:5], AF.Exp, scale=s.const(-0.5))
        s.comp('scalar', rstd)
        def norm(e):
            e.tensor_scalar(sx[:], sx[:], stat[:, 2:3], stat[:, 4:5], ALU.subtract, ALU.mult)
            return e.tensor_copy(sxn_bf[:], sx[:])
        s.comp('vector', norm)
        s.dma('sync', [(o_xn1[t * 128:(t + 1) * 128, :], sx[:])])
        for j in range(8):
            s.comp('tensor', lambda e, j=j: e.transpose(ptp[:], sxn_bf[:, j * 128:(j + 1) * 128], ident[:]))
            s.comp('vector', lambda e, t=t, j=j: e.tensor_copy(
                xn1T[:, j * T + t * 128: j * T + (t + 1) * 128], ptp[:]))
    s.dma('sync', [(o_xn1T.rearrange("p j t -> p (j t)"), xn1T[:])])

    sxh = nc.alloc_sbuf_tensor("sxh", [3, D], F32)
    sxh_bf = nc.alloc_sbuf_tensor("sxh_bf", [3, D], BF16)
    hstat = nc.alloc_sbuf_tensor("hstat", [3, 8], F32)
    s.dma('sync', [(sxh[:], x1h)])
    def hstats_v(e):
        return e.tensor_scalar(dump[0:3, :], sxh[:], 1.0, 0.0, ALU.mult, ALU.add,
                               accum_out=hstat[:, 0:1])
    s.comp('vector', hstats_v)
    s.comp('scalar', lambda e: e.activation(dump[0:3, :], sxh[:], AF.Square,
                                            accum_out=hstat[:, 1:2]))
    def hmv(e):
        e.tensor_scalar(hstat[:, 2:3], hstat[:, 0:1], 1.0 / D, None, ALU.mult)
        e.tensor_scalar(hstat[:, 3:4], hstat[:, 1:2], 1.0 / D, None, ALU.mult)
        e.tensor_scalar(hstat[:, 4:5], hstat[:, 2:3], hstat[:, 2:3], None, ALU.mult)
        return e.tensor_sub(hstat[:, 3:4], hstat[:, 3:4], hstat[:, 4:5])
    s.comp('vector', hmv)
    def hrstd(e):
        e.activation(hstat[:, 4:5], hstat[:, 3:4], AF.Ln, bias=s.const(EPS, 3))
        return e.activation(hstat[:, 4:5], hstat[:, 4:5], AF.Exp, scale=s.const(-0.5, 3))
    s.comp('scalar', hrstd)
    def hnorm(e):
        e.tensor_scalar(sxh[:], sxh[:], hstat[:, 2:3], hstat[:, 4:5], ALU.subtract, ALU.mult)
        return e.tensor_copy(sxh_bf[:], sxh[:])
    s.comp('vector', hnorm)
    for j in range(8):
        s.comp('tensor', lambda e, j=j: e.transpose(ptp[:, 0:3],
                                                    sxh_bf[:, j * 128:(j + 1) * 128],
                                                    ident[0:3, 0:3]))
        s.comp('vector', lambda e, j=j: e.tensor_copy(xnhT[:, j * 3:(j + 1) * 3], ptp[:, 0:3]))

    for t in range(16):
        def zproj(e, t=t):
            for j in range(8):
                lhsT = xn1T[:, j * T + t * 128: j * T + (t + 1) * 128]
                for g in range(4):
                    last = e.matmul(pA[:, g * 512:(g + 1) * 512], lhsT,
                                    swin[:, j * DPROJ + g * 512: j * DPROJ + (g + 1) * 512],
                                    start=(j == 0), stop=(j == 7))
            return last
        s.comp('tensor', zproj)
        s.comp('vector', lambda e: e.tensor_copy(ztile[:], pA[:, 0:2048]))
        s.dma('sync', [(o_z[t * 128:(t + 1) * 128, :], ztile[:])])

    for i in range(18):
        coff = DIN + i * 128
        for grp in range(2):
            def xbcmm2(e, i=i, coff=coff, grp=grp):
                for j in range(8):
                    lhsT = swin[:, j * DPROJ + coff: j * DPROJ + coff + 128]
                    e.matmul(pB[:, 0:512], lhsT,
                             xn1T[:, j * T + grp * 1024: j * T + grp * 1024 + 512],
                             start=(j == 0), stop=(j == 7))
                    last = e.matmul(pB[:, 512:1024], lhsT,
                                    xn1T[:, j * T + grp * 1024 + 512: j * T + (grp + 1) * 1024],
                                    start=(j == 0), stop=(j == 7))
                return last
            s.comp('tensor', xbcmm2)
            s.comp('vector', lambda e, grp=grp: e.tensor_copy(
                xbcpre[:, 3 + grp * 1024: 3 + (grp + 1) * 1024], pB[:, 0:1024]))
        def xbch(e, i=i, coff=coff):
            for j in range(8):
                last = e.matmul(pB[:, 0:3], swin[:, j * DPROJ + coff: j * DPROJ + coff + 128],
                                xnhT[:, j * 3:(j + 1) * 3], start=(j == 0), stop=(j == 7))
            return last
        s.comp('tensor', xbch)
        s.comp('vector', lambda e: e.tensor_copy(xbcpre[:, 0:3], pB[:, 0:3]))
        def conv2(e, i=i):
            e.tensor_scalar(ctmp[:], xbcpre[:, 0:T], scw[:, i * 4:i * 4 + 1], None, ALU.mult)
            e.scalar_tensor_tensor(ctmp[:], xbcpre[:, 1:1 + T], scw[:, i * 4 + 1:i * 4 + 2],
                                   ctmp[:], ALU.mult, ALU.add)
            e.scalar_tensor_tensor(ctmp[:], xbcpre[:, 2:2 + T], scw[:, i * 4 + 2:i * 4 + 3],
                                   ctmp[:], ALU.mult, ALU.add)
            return e.scalar_tensor_tensor(ctmp[:], xbcpre[:, 3:3 + T], scw[:, i * 4 + 3:i * 4 + 4],
                                          ctmp[:], ALU.mult, ALU.add)
        s.comp('vector', conv2)
        s.comp('scalar', lambda e, i=i: e.activation(cbf[:], ctmp[:], AF.Silu,
                                                     bias=scb[:, i:i + 1]))
        if i < 16:
            s.dma('sync', [(o_xsT[:, i, :], cbf[:])])
        elif i == 16:
            s.dma('sync', [(o_BT, cbf[:])])
        else:
            s.dma('sync', [(o_CT, cbf[:])])

    for grp in range(2):
        def dtmm(e, grp=grp):
            for j in range(8):
                lhsT = swin[:, j * DPROJ + DIN + CONVD: j * DPROJ + DPROJ]
                e.matmul(pB[0:32, 0:512], lhsT,
                         xn1T[:, j * T + grp * 1024: j * T + grp * 1024 + 512],
                         start=(j == 0), stop=(j == 7))
                last = e.matmul(pB[0:32, 512:1024], lhsT,
                                xn1T[:, j * T + grp * 1024 + 512: j * T + (grp + 1) * 1024],
                                start=(j == 0), stop=(j == 7))
            return last
        s.comp('tensor', dtmm)
        s.comp('vector', lambda e, grp=grp: e.tensor_copy(
            dtraw[:, grp * 1024:(grp + 1) * 1024], pB[0:32, 0:1024]))
    def sp1(e):
        e.tensor_scalar(dtraw[:], dtraw[:], sdtb[:], None, ALU.add)
        return e.tensor_scalar(dtsp[:], dtraw[:], 0.0, None, ALU.max)
    s.comp('vector', sp1)
    s.comp('scalar', lambda e: e.activation(dwork[:], dtraw[:], AF.Abs))
    s.comp('scalar', lambda e: e.activation(dwork[:], dwork[:], AF.Exp, scale=s.const(-1.0, 32)))
    s.comp('scalar', lambda e: e.activation(dwork[:], dwork[:], AF.Ln, bias=s.const(1.0, 32)))
    s.comp('vector', lambda e: e.tensor_add(dtsp[:], dtsp[:], dwork[:]))
    s.dma('sync', [(o_dt, dtsp[:])])
    def dta(e):
        e.tensor_scalar(dwork[:], dtsp[:], expA[:], -1.0, ALU.mult, ALU.mult)
        return e.tensor_tensor_scan(dwork[:], dwork[:], dwork[:], 0.0, ALU.add, ALU.bypass)
    s.comp('vector', dta)
    s.dma('sync', [(o_LT, dwork[:])])
    return s.build()


def run_l1a(inputs, static, r2):
    nc = build_l1a()
    in_maps = []
    for c in range(8):
        h = c % 2
        m = {k: static[k] for k in ('w_in', 'conv_wc', 'conv_bc', 'dtb_c', 'alog_c')}
        m['x1'] = np.asarray(r2[c]['x1'])
        if h == 0:
            m['x1h'] = np.zeros((3, D), np.float32)
        else:
            m['x1h'] = np.ascontiguousarray(np.asarray(r2[c - 1]['x1'])[-3:, :])
        in_maps.append(m)
    res = run_bass_kernel_spmd(nc, in_maps, core_ids=list(range(8)))
    return res.results


def build_l1b():
    nc, s = mk_nc()
    xsT_d = nc.dram_tensor("xsT", [128, 16, T], BF16, kind="ExternalInput").ap()
    BT_d = nc.dram_tensor("BT", [128, T], BF16, kind="ExternalInput").ap()
    CT_d = nc.dram_tensor("CT", [128, T], BF16, kind="ExternalInput").ap()
    dt_d = nc.dram_tensor("dtsp", [32, T], F32, kind="ExternalInput").ap()
    LT_d = nc.dram_tensor("LT", [32, T], F32, kind="ExternalInput").ap()
    alogt = nc.dram_tensor("alog_tile", [128, 32], F32, kind="ExternalInput").ap()
    dskt = nc.dram_tensor("dskip_tile", [128, T], F32, kind="ExternalInput").ap()
    o_y = nc.dram_tensor("ylocal", [T, DIN], F32, kind="ExternalOutput").ap()
    o_h = nc.dram_tensor("h_end", [128, DIN], F32, kind="ExternalOutput").ap()

    ident, ones_row, ones_col = setup_common(nc, s)
    identf = nc.alloc_sbuf_tensor("identf", [128, 128], F32)
    s.comp('gpsimd', lambda e: e.memset(identf[:], 1.0))
    s.comp('gpsimd', lambda e: e.affine_select(identf[:], identf[:], pattern=[[1, 128]],
                                               base=0, channel_multiplier=-1,
                                               compare_op=ALU.is_equal, fill=0.0))
    allonesf = nc.alloc_sbuf_tensor("allonesf", [128, 128], F32)
    s.comp('gpsimd', lambda e: e.memset(allonesf[:], 1.0))
    trimask = nc.alloc_sbuf_tensor("trimask", [128, 128], F32)
    s.comp('gpsimd', lambda e: e.memset(trimask[:], 0.0))
    s.comp('gpsimd', lambda e: e.affine_select(trimask[:], trimask[:], pattern=[[1, 128]],
                                               base=0, channel_multiplier=-1,
                                               compare_op=ALU.is_ge, fill=-1e30))
    ehsel = nc.alloc_sbuf_tensor("ehsel", [32, 32 * 128], F32)
    s.comp('gpsimd', lambda e: e.memset(ehsel[:], 1.0))
    s.comp('gpsimd', lambda e: e.affine_select(ehsel[:], ehsel[:],
                                               pattern=[[-1, 32], [0, 128]], base=0,
                                               channel_multiplier=1,
                                               compare_op=ALU.is_equal, fill=0.0))

    xsT = nc.alloc_sbuf_tensor("xsT_s", [128, 16 * T], BF16)
    sBT = nc.alloc_sbuf_tensor("sBT", [128, T], BF16)
    sCT = nc.alloc_sbuf_tensor("sCT", [128, T], BF16)
    sdt = nc.alloc_sbuf_tensor("sdt", [32, T], F32)
    sLT = nc.alloc_sbuf_tensor("sLT", [32, T], F32)
    expAb = nc.alloc_sbuf_tensor("expAb", [128, 32], F32)
    sdsk = nc.alloc_sbuf_tensor("sdsk", [128, T], F32)
    h_sb = nc.alloc_sbuf_tensor("h_sb", [128, DIN], F32)
    h_bf = nc.alloc_sbuf_tensor("h_bf", [128, DIN], BF16)
    Y_sb = nc.alloc_sbuf_tensor("Y_sb", [128, DIN], F32)
    xs_tok = nc.alloc_sbuf_tensor("xs_tok", [128, DIN], BF16)
    Xd_bf = nc.alloc_sbuf_tensor("Xd_bf", [128, DIN], BF16)
    Xe_bf = nc.alloc_sbuf_tensor("Xe_bf", [128, DIN], BF16)
    B_tok = nc.alloc_sbuf_tensor("B_tok", [128, 128], BF16)
    G_sb = nc.alloc_sbuf_tensor("G_sb", [128, 128], BF16)
    Dm_sb = nc.alloc_sbuf_tensor("Dm_sb", [128, 128], F32)
    E_sb = nc.alloc_sbuf_tensor("E_sb", [128, 128], F32)
    MT_bf = nc.alloc_sbuf_tensor("MT_bf", [128, 128], BF16)
    aT = nc.alloc_sbuf_tensor("aT", [32, 128], F32)
    negaT = nc.alloc_sbuf_tensor("negaT", [32, 128], F32)
    expaT = nc.alloc_sbuf_tensor("expaT", [32, 128], F32)
    exprT = nc.alloc_sbuf_tensor("exprT", [32, 128], F32)
    dt_tok = nc.alloc_sbuf_tensor("dt_tok", [128, 32], F32)
    expa_tok = nc.alloc_sbuf_tensor("expa_tok", [128, 32], F32)
    expr_tok = nc.alloc_sbuf_tensor("expr_tok", [128, 32], F32)
    dtE = nc.alloc_sbuf_tensor("dtE", [128, 32], F32)
    lamb = nc.alloc_sbuf_tensor("lamb", [128, 32], F32)
    tws = nc.alloc_sbuf_tensor("tws", [128, DIN], F32)

    pBig = nc.alloc_psum_tensor("pBig", [128, 2048], F32)
    pG = nc.alloc_psum_tensor("pG", [128, 128], F32)
    pD = nc.alloc_psum_tensor("pD", [128, 128], F32)
    ptpb = nc.alloc_psum_tensor("ptpb", [128, 128], BF16)
    ptpf = nc.alloc_psum_tensor("ptpf", [128, 128], F32)

    s.dma('sync', [(xsT.rearrange("p (i t) -> p i t", i=16), xsT_d),
                   (sBT[:], BT_d), (sCT[:], CT_d), (sdt[:], dt_d), (sLT[:], LT_d),
                   (sdsk[:], dskt)])
    s.dma('sync', [(expAb[:], alogt)])
    s.comp('scalar', lambda e: e.activation(expAb[:], expAb[:], AF.Exp))
    s.comp('vector', lambda e: e.memset(h_sb[:], 0.0))

    for c in range(16):
        ck = c * 128
        def prep_a(e, c=c, ck=ck):
            if c == 0:
                e.tensor_copy(aT[:], sLT[:, ck:ck + 128])
            else:
                e.tensor_scalar(aT[:], sLT[:, ck:ck + 128], sLT[:, ck - 1:ck],
                                None, ALU.subtract)
            e.tensor_scalar(negaT[:], aT[:], -1.0, None, ALU.mult)
            return e.tensor_scalar(exprT[:], sLT[:, ck:ck + 128],
                                   sLT[:, ck + 127:ck + 128], None, ALU.subtract)
        s.comp('vector', prep_a)

        def prep_exp(e):
            e.activation(expaT[:], aT[:], AF.Exp)
            return e.activation(exprT[:], exprT[:], AF.Exp, scale=s.const(-1.0, 32))
        s.comp('scalar', prep_exp)

        for srcT, dstT in ((sdt, dt_tok), (expaT, expa_tok), (exprT, expr_tok)):
            if srcT is sdt:
                s.comp('tensor', lambda e, ck=ck: e.transpose(
                    ptpf[:, 0:32], sdt[:, ck:ck + 128], identf[0:32, 0:32]))
            elif srcT is expaT:
                s.comp('tensor', lambda e: e.transpose(ptpf[:, 0:32], expaT[:], identf[0:32, 0:32]))
            else:
                s.comp('tensor', lambda e: e.transpose(ptpf[:, 0:32], exprT[:], identf[0:32, 0:32]))
            s.comp('vector', lambda e, dstT=dstT: e.tensor_copy(dstT[:], ptpf[:, 0:32]))

        def lam1(e):
            return e.tensor_mul(dtE[:], dt_tok[:], expAb[:])
        s.comp('vector', lam1)
        s.comp('tensor', lambda e: e.matmul(ptpf[:, 0:32], allonesf[:], dtE[:],
                                            start=True, stop=True))
        s.comp('scalar', lambda e: e.activation(lamb[:], ptpf[:, 0:32], AF.Exp,
                                                scale=s.const(-1.0)))

        for i in range(16):
            s.comp('tensor', lambda e, i=i, ck=ck: e.transpose(
                ptpb[:], xsT[:, i * T + ck: i * T + ck + 128], ident[:]))
            s.comp('vector', lambda e, i=i: e.tensor_copy(
                xs_tok[:, i * 128:(i + 1) * 128], ptpb[:]))

        def mk_dtE2(e):
            return e.tensor_mul(dtE[:], dt_tok[:], expr_tok[:])
        s.comp('vector', mk_dtE2)
        for h in range(32):
            s.comp('vector', lambda e, h=h: e.tensor_scalar(
                Xd_bf[:, h * 64:(h + 1) * 64], xs_tok[:, h * 64:(h + 1) * 64],
                dt_tok[:, h:h + 1], None, ALU.mult))
            s.comp('vector', lambda e, h=h: e.tensor_scalar(
                Xe_bf[:, h * 64:(h + 1) * 64], xs_tok[:, h * 64:(h + 1) * 64],
                dtE[:, h:h + 1], None, ALU.mult))

        s.comp('tensor', lambda e, ck=ck: e.matmul(pG[:], sBT[:, ck:ck + 128],
                                                   sCT[:, ck:ck + 128], start=True, stop=True))
        s.comp('vector', lambda e: e.tensor_copy(G_sb[:], pG[:]))

        for h in range(32):
            def mmD(e, h=h):
                e.matmul(pD[:], ehsel[:, h * 128:(h + 1) * 128], aT[:], start=True, stop=False)
                return e.matmul(pD[:], negaT[:], ehsel[:, h * 128:(h + 1) * 128],
                                start=False, stop=True)
            s.comp('tensor', mmD)
            s.comp('vector', lambda e: e.tensor_add(Dm_sb[:], pD[:], trimask[:]))
            s.comp('scalar', lambda e: e.activation(E_sb[:], Dm_sb[:], AF.Exp))
            s.comp('vector', lambda e: e.tensor_mul(MT_bf[:], E_sb[:], G_sb[:]))
            s.comp('tensor', lambda e, h=h: e.matmul(
                pBig[:, h * 64:(h + 1) * 64], MT_bf[:], Xd_bf[:, h * 64:(h + 1) * 64],
                start=True, stop=True))

        def ydsk(e, ck=ck):
            e.tensor_mul(tws[:], xs_tok[:], sdsk[:, 0:DIN] if False else sdsk[:, 0:DIN])
            return e.tensor_add(Y_sb[:], pBig[:, 0:DIN], tws[:])
        s.comp('vector', ydsk)

        s.comp('vector', lambda e: e.tensor_copy(h_bf[:], h_sb[:]))
        def mmU(e, ck=ck):
            for g in range(4):
                last = e.matmul(pBig[:, g * 512:(g + 1) * 512], sCT[:, ck:ck + 128],
                                h_bf[:, g * 512:(g + 1) * 512], start=True, stop=True)
            return last
        s.comp('tensor', mmU)
        for h in range(32):
            s.comp('vector', lambda e, h=h: e.scalar_tensor_tensor(
                Y_sb[:, h * 64:(h + 1) * 64], pBig[:, h * 64:(h + 1) * 64],
                expa_tok[:, h:h + 1], Y_sb[:, h * 64:(h + 1) * 64], ALU.mult, ALU.add))
        s.dma('sync', [(o_y[ck:ck + 128, :], Y_sb[:])])

        s.comp('tensor', lambda e, ck=ck: e.transpose(ptpb[:], sBT[:, ck:ck + 128], ident[:]))
        s.comp('vector', lambda e: e.tensor_copy(B_tok[:], ptpb[:]))
        def mmH(e):
            for g in range(4):
                last = e.matmul(pBig[:, g * 512:(g + 1) * 512], B_tok[:],
                                Xe_bf[:, g * 512:(g + 1) * 512], start=True, stop=True)
            return last
        s.comp('tensor', mmH)
        for h in range(32):
            s.comp('vector', lambda e, h=h: e.scalar_tensor_tensor(
                h_sb[:, h * 64:(h + 1) * 64], h_sb[:, h * 64:(h + 1) * 64],
                lamb[:, h:h + 1], pBig[:, h * 64:(h + 1) * 64], ALU.mult, ALU.add))

    s.dma('sync', [(o_h, h_sb[:])])
    return s.build()


def run_l1b(inputs, static, r3):
    nc = build_l1b()
    in_maps = []
    for c in range(8):
        m = {'xsT': np.asarray(r3[c]['xsT']), 'BT': np.asarray(r3[c]['BT']),
             'CT': np.asarray(r3[c]['CT']), 'dtsp': np.asarray(r3[c]['dtsp']),
             'LT': np.asarray(r3[c]['LT']),
             'alog_tile': static['alog_tile'], 'dskip_tile': static['dskip_tile']}
        in_maps.append(m)
    res = run_bass_kernel_spmd(nc, in_maps, core_ids=list(range(8)))
    return res.results


def build_l1c():
    nc, s = mk_nc()
    yl_d = nc.dram_tensor("ylocal", [T, DIN], F32, kind="ExternalInput").ap()
    hin_d = nc.dram_tensor("h_in", [128, DIN], F32, kind="ExternalInput").ap()
    CT_d = nc.dram_tensor("CT", [128, T], BF16, kind="ExternalInput").ap()
    LT_d = nc.dram_tensor("LT", [32, T], F32, kind="ExternalInput").ap()
    z_d = nc.dram_tensor("z", [T, DIN], BF16, kind="ExternalInput").ap()
    xn1_d = nc.dram_tensor("xn1", [T, D], F32, kind="ExternalInput").ap()
    rmsw_d = nc.dram_tensor("rmsw_tile", [128, DIN], F32, kind="ExternalInput").ap()
    wout_d = nc.dram_tensor("w_out", [DIN, D], F32, kind="ExternalInput").ap()
    glw_d = nc.dram_tensor("glu_w", [D, DIN], F32, kind="ExternalInput").ap()
    glb_d = nc.dram_tensor("glu_b_row", [1, DIN], F32, kind="ExternalInput").ap()
    lnw_d = nc.dram_tensor("lnw_tile", [128, D], F32, kind="ExternalInput").ap()
    lnb_d = nc.dram_tensor("lnb_tile", [128, D], F32, kind="ExternalInput").ap()
    o_s1 = nc.dram_tensor("s1", [T, D], F32, kind="ExternalOutput").ap()

    ident, ones_row, ones_col = setup_common(nc, s)
    identf = nc.alloc_sbuf_tensor("identf", [128, 128], F32)
    s.comp('gpsimd', lambda e: e.memset(identf[:], 1.0))
    s.comp('gpsimd', lambda e: e.affine_select(identf[:], identf[:], pattern=[[1, 128]],
                                               base=0, channel_multiplier=-1,
                                               compare_op=ALU.is_equal, fill=0.0))

    sCT = nc.alloc_sbuf_tensor("sCT", [128, T], BF16)
    sLT = nc.alloc_sbuf_tensor("sLT", [32, T], F32)
    expg = nc.alloc_sbuf_tensor("expg", [32, T], F32)
    expg_tok = nc.alloc_sbuf_tensor("expg_tok", [128, 32], F32)
    hin = nc.alloc_sbuf_tensor("hin", [128, DIN], F32)
    hin_bf = nc.alloc_sbuf_tensor("hin_bf", [128, DIN], BF16)
    Yt = nc.alloc_sbuf_tensor("Yt", [128, DIN], F32)
    zt = nc.alloc_sbuf_tensor("zt", [128, DIN], BF16)
    zs = nc.alloc_sbuf_tensor("zs", [128, DIN], BF16)
    stat = nc.alloc_sbuf_tensor("stat", [128, 8], F32)
    dump = nc.alloc_sbuf_tensor("dump", [128, DIN], F32)
    srms = nc.alloc_sbuf_tensor("srms", [128, DIN], F32)
    Ynb = nc.alloc_sbuf_tensor("Ynb", [128, DIN], BF16)
    YnT = nc.alloc_sbuf_tensor("YnT", [128, 16 * 128], BF16)
    swo = nc.alloc_sbuf_tensor("swo", [128, 16 * D], BF16)
    sglw = nc.alloc_sbuf_tensor("sglw", [128, 8 * DIN], BF16)
    sglb = nc.alloc_sbuf_tensor("sglb", [1, DIN], BF16)
    slnw = nc.alloc_sbuf_tensor("slnw", [128, D], F32)
    slnb = nc.alloc_sbuf_tensor("slnb", [128, D], F32)
    xn1t = nc.alloc_sbuf_tensor("xn1t", [128, D], F32)
    y1 = nc.alloc_sbuf_tensor("y1", [128, D], F32)
    y1bf = nc.alloc_sbuf_tensor("y1bf", [128, D], BF16)
    y1T = nc.alloc_sbuf_tensor("y1T", [128, 8 * 128], BF16)
    sgat = nc.alloc_sbuf_tensor("sgat", [128, D], F32)

    pBig = nc.alloc_psum_tensor("pBig", [128, 2048], F32)
    pW = nc.alloc_psum_tensor("pW", [128, 1024], F32)
    ptpb = nc.alloc_psum_tensor("ptpb", [128, 128], BF16)
    ptpf = nc.alloc_psum_tensor("ptpf", [128, 128], F32)

    s.dma('sync', [(sCT[:], CT_d), (sLT[:], LT_d), (hin[:], hin_d),
                   (slnw[:], lnw_d), (slnb[:], lnb_d), (srms[:], rmsw_d)])
    s.dma('gpsimd', [(swo.rearrange("p (j d) -> p j d", j=16),
                      wout_d.rearrange("(j p) d -> p j d", p=128)),
                     (sglw.rearrange("p (j c) -> p j c", j=8),
                      glw_d.rearrange("(j p) c -> p j c", p=128)),
                     (sglb[:], glb_d)])
    s.comp('vector', lambda e: e.tensor_copy(hin_bf[:], hin[:]))
    s.comp('scalar', lambda e: e.activation(expg[:], sLT[:], AF.Exp))

    for t in range(16):
        tk = t * 128
        s.dma('sync', [(Yt[:], yl_d[tk:tk + 128, :]), (zt[:], z_d[tk:tk + 128, :]),
                       (xn1t[:], xn1_d[tk:tk + 128, :])])
        s.comp('tensor', lambda e, tk=tk: e.transpose(ptpf[:, 0:32], expg[:, tk:tk + 128],
                                                      identf[0:32, 0:32]))
        s.comp('vector', lambda e: e.tensor_copy(expg_tok[:], ptpf[:, 0:32]))
        def mmU(e, tk=tk):
            for g in range(4):
                last = e.matmul(pBig[:, g * 512:(g + 1) * 512], sCT[:, tk:tk + 128],
                                hin_bf[:, g * 512:(g + 1) * 512], start=True, stop=True)
            return last
        s.comp('tensor', mmU)
        for h in range(32):
            s.comp('vector', lambda e, h=h: e.scalar_tensor_tensor(
                Yt[:, h * 64:(h + 1) * 64], pBig[:, h * 64:(h + 1) * 64],
                expg_tok[:, h:h + 1], Yt[:, h * 64:(h + 1) * 64], ALU.mult, ALU.add))
        s.comp('scalar', lambda e: e.activation(zs[:], zt[:], AF.Silu))
        s.comp('vector', lambda e: e.tensor_mul(Yt[:], Yt[:], zs[:]))
        s.comp('scalar', lambda e: e.activation(dump[:], Yt[:], AF.Square,
                                                accum_out=stat[:, 0:1]))
        def rstd(e):
            e.activation(stat[:, 1:2], stat[:, 0:1], AF.Ln, bias=s.const(EPS),
                         scale=s.const(1.0 / DIN))
            return e.activation(stat[:, 1:2], stat[:, 1:2], AF.Exp, scale=s.const(-0.5))
        s.comp('scalar', rstd)
        def ynorm(e):
            e.tensor_scalar(dump[:], Yt[:], stat[:, 1:2], None, ALU.mult)
            return e.tensor_mul(Ynb[:], dump[:], srms[:])
        s.comp('vector', ynorm)
        for j in range(16):
            s.comp('tensor', lambda e, j=j: e.transpose(ptpb[:], Ynb[:, j * 128:(j + 1) * 128],
                                                        ident[:]))
            s.comp('vector', lambda e, j=j: e.tensor_copy(YnT[:, j * 128:(j + 1) * 128], ptpb[:]))
        def mmWo(e):
            for j in range(16):
                e.matmul(pW[:, 0:512], YnT[:, j * 128:(j + 1) * 128],
                         swo[:, j * D: j * D + 512], start=(j == 0), stop=(j == 15))
                last = e.matmul(pW[:, 512:1024], YnT[:, j * 128:(j + 1) * 128],
                                swo[:, j * D + 512: (j + 1) * D], start=(j == 0), stop=(j == 15))
            return last
        s.comp('tensor', mmWo)
        def mk_y1(e):
            e.tensor_add(y1[:], pW[:, 0:1024], xn1t[:])
            return e.tensor_copy(y1bf[:], y1[:])
        s.comp('vector', mk_y1)
        for j in range(8):
            s.comp('tensor', lambda e, j=j: e.transpose(ptpb[:], y1bf[:, j * 128:(j + 1) * 128],
                                                        ident[:]))
            s.comp('vector', lambda e, j=j: e.tensor_copy(y1T[:, j * 128:(j + 1) * 128], ptpb[:]))
        def mmGlu(e):
            for j in range(8):
                for g in range(4):
                    e.matmul(pBig[:, g * 512:(g + 1) * 512], y1T[:, j * 128:(j + 1) * 128],
                             sglw[:, j * DIN + g * 512: j * DIN + (g + 1) * 512],
                             start=(j == 0), stop=False)
            for g in range(4):
                last = e.matmul(pBig[:, g * 512:(g + 1) * 512], ones_row[:],
                                sglb[:, g * 512:(g + 1) * 512], start=False, stop=True)
            return last
        s.comp('tensor', mmGlu)
        s.comp('scalar', lambda e: e.activation(sgat[:], pBig[:, 1024:2048], AF.Sigmoid))
        def mk_y2(e):
            e.tensor_mul(sgat[:], pBig[:, 0:1024], sgat[:])
            return e.tensor_add(y1[:], y1[:], sgat[:])
        s.comp('vector', mk_y2)
        def stats_v(e):
            return e.tensor_scalar(dump[:, 0:D], y1[:], 1.0, 0.0, ALU.mult, ALU.add,
                                   accum_out=stat[:, 2:3])
        s.comp('vector', stats_v)
        s.comp('scalar', lambda e: e.activation(dump[:, 0:D], y1[:], AF.Square,
                                                accum_out=stat[:, 3:4]))
        def mv(e):
            e.tensor_scalar(stat[:, 4:5], stat[:, 2:3], 1.0 / D, None, ALU.mult)
            e.tensor_scalar(stat[:, 5:6], stat[:, 3:4], 1.0 / D, None, ALU.mult)
            e.tensor_scalar(stat[:, 6:7], stat[:, 4:5], stat[:, 4:5], None, ALU.mult)
            return e.tensor_sub(stat[:, 5:6], stat[:, 5:6], stat[:, 6:7])
        s.comp('vector', mv)
        def rstd2(e):
            e.activation(stat[:, 6:7], stat[:, 5:6], AF.Ln, bias=s.const(EPS))
            return e.activation(stat[:, 6:7], stat[:, 6:7], AF.Exp, scale=s.const(-0.5))
        s.comp('scalar', rstd2)
        def fin(e):
            e.tensor_scalar(y1[:], y1[:], stat[:, 4:5], stat[:, 6:7], ALU.subtract, ALU.mult)
            e.tensor_mul(y1[:], y1[:], slnw[:])
            return e.tensor_add(y1[:], y1[:], slnb[:])
        s.comp('vector', fin)
        s.dma('sync', [(o_s1[tk:tk + 128, :], y1[:])])

    return s.build()


def run_l1c(inputs, static, r3, r4):
    nc = build_l1c()
    in_maps = []
    for c in range(8):
        h = c % 2
        m = {k: static[k] for k in ('rmsw_tile', 'w_out', 'glu_w', 'glu_b_row',
                                    'lnw_tile', 'lnb_tile')}
        m['ylocal'] = np.asarray(r4[c]['ylocal'])
        m['h_in'] = (np.zeros((128, DIN), np.float32) if h == 0
                     else np.asarray(r4[c - 1]['h_end']))
        m['CT'] = np.asarray(r3[c]['CT'])
        m['LT'] = np.asarray(r3[c]['LT'])
        m['z'] = np.asarray(r3[c]['z'])
        m['xn1'] = np.asarray(r3[c]['xn1'])
        in_maps.append(m)
    res = run_bass_kernel_spmd(nc, in_maps, core_ids=list(range(8)))
    return res.results


def build_l1d():
    nc, s = mk_nc()
    xn1T_d = nc.dram_tensor("xn1T", [128, 8, T], BF16, kind="ExternalInput").ap()
    s1_d = nc.dram_tensor("s1", [T, D], F32, kind="ExternalInput").ap()
    x1_d = nc.dram_tensor("x1", [T, D], F32, kind="ExternalInput").ap()
    gate_d = nc.dram_tensor("gate_t", [D, NE], F32, kind="ExternalInput").ap()
    w1_d = nc.dram_tensor("w1n", [NE, D, FF], F32, kind="ExternalInput").ap()
    b1_d = nc.dram_tensor("b1n_c", [NE, 128, 32], F32, kind="ExternalInput").ap()
    w2_d = nc.dram_tensor("w2n", [NE, FF, D], F32, kind="ExternalInput").ap()
    b2_d = nc.dram_tensor("b2n_r", [1, NE * D], F32, kind="ExternalInput").ap()
    sw1_d = nc.dram_tensor("sw1", [D, FF], F32, kind="ExternalInput").ap()
    sb1_d = nc.dram_tensor("sb1_c", [128, 32], F32, kind="ExternalInput").ap()
    sw2_d = nc.dram_tensor("sw2", [FF, D], F32, kind="ExternalInput").ap()
    sb2_d = nc.dram_tensor("sb2_row", [1, D], F32, kind="ExternalInput").ap()
    o_out = nc.dram_tensor("out", [T, D], F32, kind="ExternalOutput").ap()

    ident, ones_row, ones_col = setup_common(nc, s)
    identf = nc.alloc_sbuf_tensor("identf", [128, 128], F32)
    s.comp('gpsimd', lambda e: e.memset(identf[:], 1.0))
    s.comp('gpsimd', lambda e: e.affine_select(identf[:], identf[:], pattern=[[1, 128]],
                                               base=0, channel_multiplier=-1,
                                               compare_op=ALU.is_equal, fill=0.0))

    xn1T = nc.alloc_sbuf_tensor("xn1T_s", [128, 8 * T], BF16)
    sgt = nc.alloc_sbuf_tensor("sgt", [128, 8 * NE], BF16)
    lgT = nc.alloc_sbuf_tensor("lgT", [8, T], F32)
    lg = nc.alloc_sbuf_tensor("lg", [128, 16 * NE], F32)
    mwork = nc.alloc_sbuf_tensor("mwork", [128, 16 * NE], F32)
    comb = nc.alloc_sbuf_tensor("comb", [128, 16 * NE], F32)
    cwork = nc.alloc_sbuf_tensor("cwork", [128, 8], F32)
    onecol = nc.alloc_sbuf_tensor("onecol", [128, 1], F32)
    sb1c = nc.alloc_sbuf_tensor("sb1c", [128, 32], F32)
    sb2 = nc.alloc_sbuf_tensor("sb2", [1, D], BF16)
    w1blk = nc.alloc_sbuf_tensor("w1blk", [128, 8 * 128], BF16)
    w2blk = nc.alloc_sbuf_tensor("w2blk", [128, 512], BF16)
    rh = nc.alloc_sbuf_tensor("rh", [128, 32 * 1024], BF16)
    yac = nc.alloc_sbuf_tensor("yac", [128, 8 * 1024], F32)
    s1t = nc.alloc_sbuf_tensor("s1t", [128, D], F32)
    x1t = nc.alloc_sbuf_tensor("x1t", [128, D], F32)
    gt = nc.alloc_sbuf_tensor("gt", [128, D], F32)

    pA = nc.alloc_psum_tensor("pA", [128, 2048], F32)
    pY = nc.alloc_psum_tensor("pY", [128, 2048], F32)

    s.dma('sync', [(xn1T.rearrange("p (j t) -> p j t", j=8), xn1T_d),
                   (sb1c[:], sb1_d)])
    s.dma('gpsimd', [(sgt.rearrange("p (j e) -> p j e", j=8),
                      gate_d.rearrange("(j p) e -> p j e", p=128)),
                     (sb2[:], sb2_d)])
    s.comp('vector', lambda e: e.memset(onecol[:], 1.0))

    def mm_lg(e):
        for j in range(8):
            for g in range(4):
                last = e.matmul(pA[0:8, g * 512:(g + 1) * 512],
                                sgt[:, j * NE:(j + 1) * NE],
                                xn1T[:, j * T + g * 512:j * T + (g + 1) * 512],
                                start=(j == 0), stop=(j == 7))
        return last
    s.comp('tensor', mm_lg)
    s.comp('vector', lambda e: e.tensor_copy(lgT[:], pA[0:8, 0:T]))
    for t in range(16):
        s.comp('tensor', lambda e, t=t: e.transpose(
            pA[:, 1024:1024 + 8], lgT[:, t * 128:(t + 1) * 128], identf[0:8, 0:8]))
        s.comp('vector', lambda e, t=t: e.tensor_copy(lg[:, t * NE:(t + 1) * NE],
                                                      pA[:, 1024:1024 + 8]))
    tk1 = [None] * 16
    tk2 = [None] * 16
    tk3 = [None] * 16
    for t in range(16):
        def topk(e, t=t):
            L = lg[:, t * NE:(t + 1) * NE]
            M = mwork[:, t * NE:(t + 1) * NE]
            e.reduce_max(cwork[:, 0:1], L, axis=AX.X)
            e.tensor_scalar(M, L, cwork[:, 0:1], None, ALU.is_ge)
            e.scalar_tensor_tensor(M, M, -1e30, L, ALU.mult, ALU.add)
            e.reduce_max(cwork[:, 1:2], M, axis=AX.X)
            e.tensor_scalar(cwork[:, 2:3], cwork[:, 0:1], -1.0, None, ALU.mult)
            return e.tensor_sub(cwork[:, 3:4], cwork[:, 1:2], cwork[:, 0:1])
        tk1[t] = s.comp('vector', topk,
                        wait=tw0(tk2[t - 1] if t >= 1 else None))
        def topk2(e, t=t):
            e.activation(mwork[:, t * NE:(t + 1) * NE], lg[:, t * NE:(t + 1) * NE],
                         AF.Exp, bias=cwork[:, 2:3])
            return e.activation(cwork[:, 4:5], cwork[:, 3:4], AF.Exp)
        tk2[t] = s.comp('scalar', topk2, wait=[tk1[t]])
        def topk3(e, t=t):
            L = lg[:, t * NE:(t + 1) * NE]
            e.tensor_scalar(cwork[:, 5:6], cwork[:, 4:5], 1.0, None, ALU.add)
            e.reciprocal(cwork[:, 6:7], cwork[:, 5:6])
            e.tensor_scalar(comb[:, t * NE:(t + 1) * NE], L, cwork[:, 1:2], None, ALU.is_ge)
            e.tensor_mul(comb[:, t * NE:(t + 1) * NE], comb[:, t * NE:(t + 1) * NE],
                         mwork[:, t * NE:(t + 1) * NE])
            return e.tensor_scalar(comb[:, t * NE:(t + 1) * NE], comb[:, t * NE:(t + 1) * NE],
                                   cwork[:, 6:7], None, ALU.mult)
        tk3[t] = s.comp('vector', topk3, wait=[tk2[t]])

    for half in range(2):
        hoff = half * 1024
        s.comp('vector', lambda e: e.memset(yac[:], 0.0))
        for ee in range(NE + 1):
            shared = (ee == NE)
            if not shared:
                s.dma('sync', [(sb1c[:], b1_d[ee, :, :])])
                s.dma('gpsimd', [(sb2[:], b2_d[0:1, ee * D:(ee + 1) * D])])
            else:
                s.dma('sync', [(sb1c[:], sb1_d)])
                s.dma('gpsimd', [(sb2[:], sb2_d)])
            for f in range(32):
                if not shared:
                    s.dma('gpsimd', [(w1blk.rearrange("p (j c) -> p j c", j=8),
                                      w1_d[ee].rearrange("(j p) c -> p j c", p=128)[:, :, f * 128:(f + 1) * 128])])
                else:
                    s.dma('gpsimd', [(w1blk.rearrange("p (j c) -> p j c", j=8),
                                      sw1_d.rearrange("(j p) c -> p j c", p=128)[:, :, f * 128:(f + 1) * 128])])
                def mmw1(e, f=f, hoff=hoff):
                    for j in range(8):
                        e.matmul(pA[:, 0:512], w1blk[:, j * 128:(j + 1) * 128],
                                 xn1T[:, j * T + hoff:j * T + hoff + 512],
                                 start=(j == 0), stop=(j == 7))
                        last = e.matmul(pA[:, 512:1024], w1blk[:, j * 128:(j + 1) * 128],
                                        xn1T[:, j * T + hoff + 512:j * T + hoff + 1024],
                                        start=(j == 0), stop=(j == 7))
                    return last
                s.comp('tensor', mmw1)
                s.comp('scalar', lambda e, f=f: e.activation(
                    rh[:, f * 1024:(f + 1) * 1024], pA[:, 0:1024], AF.Gelu,
                    bias=sb1c[:, f:f + 1]))
            for oh in range(2):
                ooff = oh * 512
                for f in range(32):
                    if not shared:
                        s.dma('gpsimd', [(w2blk[:], w2_d[ee, f * 128:(f + 1) * 128,
                                                         ooff:ooff + 512])])
                    else:
                        s.dma('gpsimd', [(w2blk[:], sw2_d[f * 128:(f + 1) * 128,
                                                          ooff:ooff + 512])])
                    def mmw2(e, f=f):
                        for tl in range(8):
                            ps = pA if tl < 4 else pY
                            last = e.matmul(ps[:, (tl % 4) * 512:(tl % 4 + 1) * 512],
                                            rh[:, f * 1024 + tl * 128:f * 1024 + (tl + 1) * 128],
                                            w2blk[:], start=(f == 0), stop=False)
                        return last
                    s.comp('tensor', mmw2)
                def mmb2(e, ooff=ooff):
                    for tl in range(8):
                        ps = pA if tl < 4 else pY
                        last = e.matmul(ps[:, (tl % 4) * 512:(tl % 4 + 1) * 512],
                                        ones_row[:], sb2[:, ooff:ooff + 512],
                                        start=False, stop=True)
                    return last
                s.comp('tensor', mmb2)
                for tl in range(8):
                    def acc(e, ee=ee, tl=tl, ooff=ooff, half=half, shared=shared):
                        ps = pA if tl < 4 else pY
                        tg = half * 8 + tl
                        ccol = onecol[:, 0:1] if shared else comb[:, tg * NE + ee:tg * NE + ee + 1]
                        return e.scalar_tensor_tensor(
                            yac[:, tl * 1024 + ooff:tl * 1024 + ooff + 512],
                            ps[:, (tl % 4) * 512:(tl % 4 + 1) * 512],
                            ccol,
                            yac[:, tl * 1024 + ooff:tl * 1024 + ooff + 512],
                            ALU.mult, ALU.add)
                    s.comp('vector', acc)

        for tl in range(8):
            tg = half * 8 + tl
            s.dma('sync', [(s1t[:], s1_d[tg * 128:(tg + 1) * 128, :]),
                           (x1t[:], x1_d[tg * 128:(tg + 1) * 128, :])])
            s.comp('scalar', lambda e, tl=tl: e.activation(gt[:], yac[:, tl * 1024:(tl + 1) * 1024],
                                                           AF.Silu))
            def fin(e):
                e.tensor_mul(gt[:], gt[:], s1t[:])
                return e.tensor_add(x1t[:], x1t[:], gt[:])
            s.comp('vector', fin)
            s.dma('sync', [(o_out[tg * 128:(tg + 1) * 128, :], x1t[:])])

    return s.build()


def run_l1d(inputs, static, r2, r3, r5):
    nc = build_l1d()
    in_maps = []
    for c in range(8):
        m = {k: static[k] for k in ('gate_t', 'w1n', 'b1n_c', 'w2n', 'b2n_r',
                                    'sw1', 'sb1_c', 'sw2', 'sb2_row')}
        m['xn1T'] = np.asarray(r3[c]['xn1T'])
        m['s1'] = np.asarray(r5[c]['s1'])
        m['x1'] = np.asarray(r2[c]['x1'])
        in_maps.append(m)
    res = run_bass_kernel_spmd(nc, in_maps, core_ids=list(range(8)))
    return res.results


NCORES = 8


def build_fused(dbg=()):
    nc, s = mk_nc()
    ar = Arena(nc, 48 * 1024)

    def tw0(*toks):
        return [t for t in toks if t is not None]

    xf = nc.dram_tensor("xf", [SEQ, D], F32, kind="ExternalInput").ap()
    pos = nc.dram_tensor("pos", [1, SEQ], F32, kind="ExternalInput").ap()
    flag = nc.dram_tensor("flag", [128, 1], F32, kind="ExternalInput").ap()
    wq = nc.dram_tensor("wq_t", [D, D], F32, kind="ExternalInput").ap()
    wk = nc.dram_tensor("wk_t", [D, D], F32, kind="ExternalInput").ap()
    wv = nc.dram_tensor("wv_t", [D, D], F32, kind="ExternalInput").ap()
    bq = nc.dram_tensor("bq_row", [1, D], F32, kind="ExternalInput").ap()
    bk = nc.dram_tensor("bk_row", [1, D], F32, kind="ExternalInput").ap()
    bv = nc.dram_tensor("bv_row", [1, D], F32, kind="ExternalInput").ap()
    wo = nc.dram_tensor("wo_t", [D, D], F32, kind="ExternalInput").ap()
    bo = nc.dram_tensor("bo_row", [1, D], F32, kind="ExternalInput").ap()
    centt = nc.dram_tensor("cent_tile", [NE, 128, D], F32, kind="ExternalInput").ap()
    gam = nc.dram_tensor("gamma_t", [D, NE], F32, kind="ExternalInput").ap()
    bet = nc.dram_tensor("beta_t", [D, NE], F32, kind="ExternalInput").ap()
    w1 = nc.dram_tensor("w1_t", [NE, D, FF], F32, kind="ExternalInput").ap()
    b1 = nc.dram_tensor("b1_c", [NE, 128, 32], F32, kind="ExternalInput").ap()
    w2 = nc.dram_tensor("w2_t", [NE, FF, D], F32, kind="ExternalInput").ap()
    b2d = nc.dram_tensor("b2_r", [1, NE * D], F32, kind="ExternalInput").ap()
    win = nc.dram_tensor("w_in", [D, DPROJ], F32, kind="ExternalInput").ap()
    cw = nc.dram_tensor("conv_wc", [128, 18, 4], F32, kind="ExternalInput").ap()
    cb = nc.dram_tensor("conv_bc", [128, 18], F32, kind="ExternalInput").ap()
    dtb = nc.dram_tensor("dtb_c", [32, 1], F32, kind="ExternalInput").ap()
    alog = nc.dram_tensor("alog_c", [32, 1], F32, kind="ExternalInput").ap()
    alogt = nc.dram_tensor("alog_tile", [128, 32], F32, kind="ExternalInput").ap()
    dskt = nc.dram_tensor("dskip_tile", [128, T], F32, kind="ExternalInput").ap()
    rmsw_d = nc.dram_tensor("rmsw_tile", [128, DIN], F32, kind="ExternalInput").ap()
    wout_d = nc.dram_tensor("w_out", [DIN, D], F32, kind="ExternalInput").ap()
    glw_d = nc.dram_tensor("glu_w", [D, DIN], F32, kind="ExternalInput").ap()
    glb_d = nc.dram_tensor("glu_b_row", [1, DIN], F32, kind="ExternalInput").ap()
    lnw_d = nc.dram_tensor("lnw_tile", [128, D], F32, kind="ExternalInput").ap()
    lnb_d = nc.dram_tensor("lnb_tile", [128, D], F32, kind="ExternalInput").ap()
    gate_d = nc.dram_tensor("gate_tile", [NE, 128, D], F32, kind="ExternalInput").ap()
    w1n_d = nc.dram_tensor("w1n", [NE, D, FF], F32, kind="ExternalInput").ap()
    b1n_d = nc.dram_tensor("b1n_c", [NE, 128, 32], F32, kind="ExternalInput").ap()
    w2n_d = nc.dram_tensor("w2n", [NE, FF, D], F32, kind="ExternalInput").ap()
    b2n_d = nc.dram_tensor("b2n_r", [1, NE * D], F32, kind="ExternalInput").ap()
    sw1_d = nc.dram_tensor("sw1", [D, FF], F32, kind="ExternalInput").ap()
    sb1_d = nc.dram_tensor("sb1_c", [128, 32], F32, kind="ExternalInput").ap()
    sw2_d = nc.dram_tensor("sw2", [FF, D], F32, kind="ExternalInput").ap()
    sb2_d = nc.dram_tensor("sb2_row", [1, D], F32, kind="ExternalInput").ap()

    o_out = nc.dram_tensor("out", [T, D], F32, kind="ExternalOutput").ap()

    def scr(name, shape, dtype):
        kind = "ExternalOutput" if name in dbg else "Internal"
        return nc.dram_tensor(name, shape, dtype, kind=kind).ap()
    xn_scr = scr("xn", [T, D], F32)
    xnbfT_scr = scr("xnT", [128, 8, SEQ], BF16)
    kr_scr = scr("krTa", [65, NH, SEQ], BF16)
    v_scr = scr("v", [SEQ, D], BF16)
    q_scr = scr("qrTa", [65, NH, T], BF16)
    s_scr = scr("s_attn", [T, D], F32)
    x1_scr = scr("x1", [T, D], F32)
    z_scr = scr("z", [T, DIN], BF16)
    xn1_scr = scr("xn1", [T, D], F32)
    xsT_scr = scr("xsT", [128, 16, T], BF16)
    BT_scr = scr("BT", [128, T], BF16)
    CT_scr = scr("CT", [128, T], BF16)
    dt_scr = scr("dtsp", [32, T], F32)
    LT_scr = scr("LT", [32, T], F32)
    yl_scr = scr("ylocal", [T, DIN], F32)
    s1_scr = scr("s1", [T, D], F32)
    hend = nc.dram_tensor("hend", [128, DIN], F32, kind="Internal").ap()
    xch_h = nc.dram_tensor("xch_h", [2, 128, D], F32, kind="Internal",
                           addr_space="Shared").ap()
    xch_s = nc.dram_tensor("xch_s", [2, 128, DIN], F32, kind="Internal",
                           addr_space="Shared").ap()
    xch_f = nc.dram_tensor("xch_f", [2, 8], I32, kind="Internal",
                           addr_space="Shared").ap()
    halo_l = nc.dram_tensor("halo_l", [128, D], F32, kind="Internal").ap()
    hin_l = nc.dram_tensor("hin_l", [128, DIN], F32, kind="Internal").ap()
    par_d = nc.dram_tensor("par", [1, 2], I32, kind="ExternalInput").ap()

    ident, ones_row, ones_col = setup_common(nc, s)
    identf = nc.alloc_sbuf_tensor("identf", [128, 128], F32)
    s.comp('gpsimd', lambda e: e.memset(identf[:], 1.0))
    s.comp('gpsimd', lambda e: e.affine_select(identf[:], identf[:], pattern=[[1, 128]],
                                               base=0, channel_multiplier=-1,
                                               compare_op=ALU.is_equal, fill=0.0))
    onesf = nc.alloc_sbuf_tensor("onesf", [1, 128], F32)
    s.comp('vector', lambda e: e.memset(onesf[:], 1.0))
    sflag = nc.alloc_sbuf_tensor("sflag", [128, 1], F32)
    s.dma('sync', [(sflag[:], flag)])
    s.xpar_sb = nc.alloc_sbuf_tensor("xpar_sb", [1, 2], I32)
    s.xflag_sb = nc.alloc_sbuf_tensor("xflag_sb", [1, 2], I32)
    par_tok = s.dma('sync', [(s.xpar_sb[:], par_d)])

    POOL = nc.alloc_psum_tensor("POOL", [128, 4096], F32)
    pA = POOL[:, 0:2048]
    pB = POOL[:, 2048:3072]
    ptpf = POOL[:, 3072:3200]
    ptpb = POOL[:, 3200:3264].bitcast(BF16)
    pO2 = POOL[0:65, 2048:4096]

    def psum8(tl):
        return POOL[:, tl * 512:(tl + 1) * 512]

    def emit_ln(sxa, dmp, sta, out_bf):
        def stats_v(e):
            return e.tensor_scalar(dmp, sxa, 1.0, 0.0, ALU.mult, ALU.add,
                                   accum_out=sta[:, 0:1])
        s.comp('vector', stats_v)
        s.comp('scalar', lambda e: e.activation(dmp, sxa, AF.Square, accum_out=sta[:, 1:2]))
        def mv(e):
            e.tensor_scalar(sta[:, 2:3], sta[:, 0:1], 1.0 / D, None, ALU.mult)
            e.tensor_scalar(sta[:, 3:4], sta[:, 1:2], 1.0 / D, None, ALU.mult)
            e.tensor_scalar(sta[:, 4:5], sta[:, 2:3], sta[:, 2:3], None, ALU.mult)
            return e.tensor_sub(sta[:, 3:4], sta[:, 3:4], sta[:, 4:5])
        s.comp('vector', mv)
        p = sxa.shape[0]
        def rstd(e):
            e.activation(sta[:, 4:5], sta[:, 3:4], AF.Ln, bias=s.const(EPS, p))
            return e.activation(sta[:, 4:5], sta[:, 4:5], AF.Exp, scale=s.const(-0.5, p))
        s.comp('scalar', rstd)
        def norm(e):
            e.tensor_scalar(sxa, sxa, sta[:, 2:3], sta[:, 4:5], ALU.subtract, ALU.mult)
            return e.tensor_copy(out_bf, sxa)
        return s.comp('vector', norm)

    ar.push()
    sx = ar.alloc(128, 2 * D, F32)
    dump = ar.alloc(128, DIN, F32)
    dumpS = ar.alloc(128, DIN, F32)
    stat = ar.alloc(128, 16, F32)
    sxn_bf = ar.alloc(128, 2 * D, BF16)
    xnT_t = ar.alloc(128, 2 * D, BF16)
    cs_tok = ar.alloc(128, 32 * 64, BF16)
    ptA = ptpb
    ptB = POOL[:, 3584:3648].bitcast(BF16)

    ar.push()
    spos = ar.alloc(1, SEQ, F32)
    rwork = ar.alloc(32, 3 * SEQ, F32)
    csT_bf = ar.alloc(32, 2 * SEQ, BF16)
    ri32 = ar.alloc(32, SEQ, I32)
    invf = ar.alloc(1, 32, F32)
    s.dma('sync', [(spos, pos)])
    s.comp('gpsimd', lambda e: e.iota(invf, pattern=[[1, 32]], base=0,
                                      channel_multiplier=0, allow_small_or_imprecise_dtypes=True))
    s.comp('scalar', lambda e: e.activation(invf, invf, AF.Exp,
                                            scale=s.const(-np.log(10000.0) / 32.0, 1)))
    for rnd in range(2):
        def mm_f(e, rnd=rnd):
            for g in range(4):
                last = e.matmul(pA[0:32, g * 512:(g + 1) * 512], invf,
                                spos[:, rnd * 2048 + g * 512: rnd * 2048 + (g + 1) * 512],
                                start=True, stop=True)
            return last
        s.comp('tensor', mm_f)
        def rr(e, rnd=rnd):
            o = rnd * 2048
            e.tensor_scalar(rwork[:, o:o + 2048], pA[0:32, 0:2048], 1.0 / TWO_PI, 0.25,
                            ALU.mult, ALU.add)
            e.tensor_scalar(rwork[:, SEQ + o:SEQ + o + 2048], pA[0:32, 0:2048],
                            1.0 / TWO_PI, None, ALU.mult)
            for cc in (0, 1):
                u = rwork[:, cc * SEQ + o: cc * SEQ + o + 2048]
                w = rwork[:, 2 * SEQ: 2 * SEQ + 2048]
                e.tensor_copy(ri32[:, 0:2048], u)
                e.tensor_copy(w, ri32[:, 0:2048])
                e.tensor_sub(u, u, w)
                e.tensor_scalar(w, u, 0.5, None, ALU.is_ge)
                last = e.tensor_sub(u, u, w)
            return last
        s.comp('vector', rr)
        def mkcs(e, rnd=rnd):
            o = rnd * 2048
            e.activation(csT_bf[:, o:o + 2048], rwork[:, o:o + 2048], AF.Sin,
                         scale=s.const(TWO_PI, 32))
            return e.activation(csT_bf[:, SEQ + o:SEQ + o + 2048],
                                rwork[:, SEQ + o:SEQ + o + 2048], AF.Sin,
                                scale=s.const(TWO_PI, 32))
        s.comp('scalar', mkcs)
    cstl = None
    for t in range(32):
        s.comp('tensor', lambda e, t=t: e.transpose(
            ptpf[0:128, 0:32].bitcast(BF16)[:, 0:32], csT_bf[:, t * 128:(t + 1) * 128],
            ident[0:32, 0:32]))
        s.comp('vector', lambda e, t=t: e.tensor_copy(
            cs_tok[:, t * 64:t * 64 + 32], ptpf[0:128, 0:32].bitcast(BF16)[:, 0:32]))
        s.comp('tensor', lambda e, t=t: e.transpose(
            ptpf[0:128, 0:32].bitcast(BF16)[:, 0:32], csT_bf[:, SEQ + t * 128:SEQ + (t + 1) * 128],
            ident[0:32, 0:32]))
        cstl = s.comp('vector', lambda e, t=t: e.tensor_copy(
            cs_tok[:, t * 64 + 32:t * 64 + 64], ptpf[0:128, 0:32].bitcast(BF16)[:, 0:32]))
    ar.pop()

    ln_ld = [None] * 32
    ln_norm = [None] * 32
    ln_sq = [None] * 32
    ln_tr7 = [None] * 32
    ln_st = [None] * 32
    for t in range(32):
        sl = t % 2
        sxa = sx[:, sl * D:(sl + 1) * D]
        sta = stat[:, sl * 8:(sl + 1) * 8]
        outb = sxn_bf[:, sl * D:(sl + 1) * D]
        xtt = xnT_t[:, sl * D:(sl + 1) * D]
        ln_ld[t] = s.dma('sync', [(sxa, xf[t * 128:(t + 1) * 128, :])],
                         wait=tw0(ln_norm[t - 2] if t >= 2 else None))
        stv = s.comp('vector', lambda e, sxa=sxa, sta=sta, sl=sl: e.tensor_scalar(
            dump[:, sl * D:(sl + 1) * D], sxa, 1.0, 0.0, ALU.mult, ALU.add,
            accum_out=sta[:, 0:1]), wait=[ln_ld[t]])
        ln_sq[t] = s.comp('scalar', lambda e, sxa=sxa, sta=sta, sl=sl: e.activation(
            dumpS[:, sl * D:(sl + 1) * D], sxa, AF.Square, accum_out=sta[:, 1:2]),
            wait=[ln_ld[t]])
        def mv(e, sta=sta):
            e.tensor_scalar(sta[:, 2:3], sta[:, 0:1], 1.0 / D, None, ALU.mult)
            e.tensor_scalar(sta[:, 3:4], sta[:, 1:2], 1.0 / D, None, ALU.mult)
            e.tensor_scalar(sta[:, 4:5], sta[:, 2:3], sta[:, 2:3], None, ALU.mult)
            return e.tensor_sub(sta[:, 3:4], sta[:, 3:4], sta[:, 4:5])
        mvt = s.comp('vector', mv, wait=[ln_sq[t], stv])
        def rstd(e, sta=sta):
            e.activation(sta[:, 4:5], sta[:, 3:4], AF.Ln, bias=s.const(EPS, 128))
            return e.activation(sta[:, 4:5], sta[:, 4:5], AF.Exp, scale=s.const(-0.5, 128))
        rst = s.comp('scalar', rstd, wait=[mvt])
        def norm(e, sxa=sxa, sta=sta, outb=outb):
            e.tensor_scalar(sxa, sxa, sta[:, 2:3], sta[:, 4:5], ALU.subtract, ALU.mult)
            return e.tensor_copy(outb, sxa)
        ln_norm[t] = s.comp('vector', norm,
                            wait=tw0(rst, ln_tr7[t - 2] if t >= 2 else cstl))
        if t < 16:
            s.dma('sync', [(xn_scr[t * 128:(t + 1) * 128, :], sxa)], wait=[ln_norm[t]])
        tcp = None
        tcps = [None] * 8
        for j in range(8):
            pt = ptA if j % 2 == 0 else ptB
            trj = s.comp('tensor', lambda e, j=j, outb=outb, pt=pt: e.transpose(
                pt, outb[:, j * 128:(j + 1) * 128], ident[:]),
                wait=tw0(ln_norm[t] if j == 0 else None,
                         tcps[j - 2] if j >= 2 else
                         (prev_tcps[6 + j] if t >= 1 else cstl)))
            tcp = s.comp('vector', lambda e, j=j, xtt=xtt, pt=pt: e.tensor_copy(
                xtt[:, j * 128:(j + 1) * 128], pt),
                wait=tw0(trj, (ln_st[t - 2] if t >= 2 else None) if j == 0 else None))
            tcps[j] = tcp
        prev_tcps = tcps
        ln_tr7[t] = trj
        ln_st[t] = s.dma('sync', [(xnbfT_scr[:, :, t * 128:(t + 1) * 128],
                                   xtt.rearrange("p (j c) -> p j c", j=8))], wait=[tcp])
    ln_tcp_last = prev_tcps[7]

    zf = s.comp('vector', lambda e: e.memset(dumpS, 0.0),
                wait=[ln_sq[31], ln_norm[31]])
    s.dma('sync', [(halo_l, dumpS[:, 0:D]), (hin_l, dumpS)], wait=[zf])

    sw = ar.alloc(128, 2 * 8 * D, BF16)
    sb_row = ar.alloc(1, 2 * D, BF16)
    xnT_tb = ar.alloc(128, 2 * D, BF16)
    qtok = ar.alloc(128, 2 * D, F32)
    qrope = ar.alloc(128, 2 * D, BF16)
    qTfull = ar.alloc(128, 2 * D, BF16)
    qkvscr = ar.alloc(128, 2 * D, F32)
    zrow_bf = ar.alloc(1, SEQ, BF16)
    zmem = s.comp('vector', lambda e: e.memset(zrow_bf, 0.0))

    import dataclasses as _dc
    def sv(base, off, w=32):
        a = base[:, off:off + w]
        return _dc.replace(a, ap=[a.ap[0], [64, 16], [1, w]])

    def swv(sl):
        return sw.rearrange("p (s j d) -> p s j d", s=2, j=8)[:, sl]
    swld = {}
    swld['q'] = s.dma('gpsimd', [(swv(0), wq.rearrange("(j p) d -> p j d", p=128)),
                                 (sb_row[:, 0:D], bq)], wait=[cstl])
    swld['k'] = s.dma('gpsimd', [(swv(1), wk.rearrange("(j p) d -> p j d", p=128)),
                                 (sb_row[:, D:2 * D], bk)], wait=[cstl])
    bar = None
    prev_tcqs = None
    for which, ws, ntile in (("q", 0, 16), ("k", 1, 32), ("v", 0, 32)):
        pv_ld = [None] * ntile
        pv_proj = [None] * ntile
        pv_cp = [None] * ntile
        pv_rope = [None] * ntile
        pv_tr7 = [None] * ntile
        pv_st = [None] * ntile
        if which == "v":
            swld['v'] = s.dma('gpsimd', [(swv(0), wv.rearrange("(j p) d -> p j d", p=128)),
                                         (sb_row[:, 0:D], bv)], wait=[bar])
        for t in range(ntile):
            sl = t % 2
            xb = xnT_tb[:, sl * D:(sl + 1) * D]
            qt = qtok[:, sl * D:(sl + 1) * D]
            qr = qrope[:, sl * D:(sl + 1) * D]
            qT = qTfull[:, sl * D:(sl + 1) * D]
            pO = pA[:, sl * 1024:(sl + 1) * 1024]
            def back(arr, k=2):
                return arr[t - k] if t >= k else bar
            pv_ld[t] = s.dma('scalar', [(xb.rearrange("p (j c) -> p j c", j=8),
                                         xnbfT_scr[:, :, t * 128:(t + 1) * 128])],
                             wait=tw0(ln_st[t] if which == "q" else None,
                                      cstl if (which == "q" and t < 2) else None,
                                      back(pv_proj)))
            fb_proj0 = cstl if which == "q" else None
            def proj(e, xb=xb, ws=ws, pO=pO):
                for j in range(8):
                    lhsT = xb[:, j * 128:(j + 1) * 128]
                    wsl = sw[:, ws * 8 * D + j * D: ws * 8 * D + (j + 1) * D]
                    e.matmul(pO[:, 0:512], lhsT, wsl[:, 0:512],
                             start=(j == 0), stop=False)
                    e.matmul(pO[:, 512:1024], lhsT, wsl[:, 512:1024],
                             start=(j == 0), stop=False)
                e.matmul(pO[:, 0:512], ones_row[:], sb_row[0:1, ws * D:ws * D + 512],
                         start=False, stop=True)
                return e.matmul(pO[:, 512:1024], ones_row[:],
                                sb_row[0:1, ws * D + 512:(ws + 1) * D],
                                start=False, stop=True)
            pv_proj[t] = s.comp('tensor', proj,
                                wait=tw0(pv_ld[t], swld[which], back(pv_cp),
                                         fb_proj0 if t < 2 else None))
            if which == "v":
                pv_cp[t] = s.comp('vector', lambda e, qr=qr, pO=pO: e.tensor_copy(qr, pO),
                                  wait=tw0(pv_proj[t], back(pv_st)))
                pv_st[t] = s.dma('scalar', [(v_scr[t * 128:(t + 1) * 128, :], qr)],
                                 wait=[pv_cp[t]])
                continue
            if which == "q":
                pv_cp[t] = s.comp('scalar', lambda e, qt=qt, pO=pO: e.activation(
                    qt, pO, AF.Identity, scale=s.const(0.125)),
                    wait=tw0(pv_proj[t], back(pv_rope)))
            else:
                pv_cp[t] = s.comp('scalar', lambda e, qt=qt, pO=pO: e.copy(qt, pO),
                                  wait=tw0(pv_proj[t], back(pv_rope)))
            def rope_t(e, t=t, qt=qt, qr=qr, sl=sl):
                lo = sv(qt, 0)
                hi = sv(qt, 32)
                olo = sv(qr, 0)
                ohi = sv(qr, 32)
                co = cs_tok[:, t * 64:t * 64 + 32]
                si = cs_tok[:, t * 64 + 32:t * 64 + 64]
                cob = _dc.replace(co, ap=[co.ap[0], [0, 16], [1, 32]])
                sib = _dc.replace(si, ap=[si.ap[0], [0, 16], [1, 32]])
                w1_ = sv(qkvscr[:, sl * D:(sl + 1) * D], 0)
                w2_ = sv(qkvscr[:, sl * D:(sl + 1) * D], 32)
                e.tensor_tensor(w1_, lo, cob, ALU.mult)
                e.tensor_tensor(w2_, hi, sib, ALU.mult)
                e.tensor_tensor(olo, w1_, w2_, ALU.subtract)
                e.tensor_tensor(w1_, hi, cob, ALU.mult)
                e.tensor_tensor(w2_, lo, sib, ALU.mult)
                return e.tensor_tensor(ohi, w1_, w2_, ALU.add)
            pv_rope[t] = s.comp('vector', rope_t,
                                wait=tw0(pv_cp[t], back(pv_tr7)))
            tcq = None
            tcqs = [None] * 8
            for j in range(8):
                pt = ptA if j % 2 == 0 else ptB
                ptwar = (tcqs[j - 2] if j >= 2 else
                         (prev_tcqs[6 + j] if prev_tcqs is not None else ln_tcp_last))
                trj = s.comp('tensor', lambda e, j=j, qr=qr, pt=pt: e.transpose(
                    pt, qr[:, j * 128:(j + 1) * 128], ident[:]),
                    wait=tw0(pv_rope[t] if j == 0 else None, ptwar))
                tcq = s.comp('vector', lambda e, j=j, qT=qT, pt=pt: e.tensor_copy(
                    qT[:, j * 128:(j + 1) * 128], pt),
                    wait=tw0(trj, back(pv_st) if j == 0 else None))
                tcqs[j] = tcq
            prev_tcqs = tcqs
            pv_tr7[t] = trj
            dst = q_scr if which == "q" else kr_scr
            hstr = T if which == "q" else SEQ
            dape = [dst[0:64, 0, 0:128].ap[0], [2 * hstr, 8], [1, 128]]
            pv_st[t] = s.dma('scalar', [
                (_dc.replace(dst[0:64, 0, t * 128:(t + 1) * 128], ap=dape),
                 qTfull[0:64, sl * D:(sl + 1) * D].rearrange("p (j c) -> p j c", j=8)),
                (_dc.replace(dst[0:64, 1, t * 128:(t + 1) * 128], ap=dape),
                 qTfull[64:128, sl * D:(sl + 1) * D].rearrange("p (j c) -> p j c", j=8)),
            ], wait=[tcq])
        bar = pv_st[ntile - 1]
        if which == "q":
            s.dma('sync', [(q_scr[64:65, :, :].rearrange("o h t -> o (h t)"),
                            _dc.replace(zrow_bf, ap=[zrow_bf.ap[0], [0, NH], [1, T]]))],
                  wait=[zmem])
        elif which == "k":
            s.dma('sync', [(kr_scr[64:65, :, :].rearrange("o h t -> o (h t)"),
                            _dc.replace(zrow_bf, ap=[zrow_bf.ap[0], [0, NH], [1, SEQ]]))],
                  wait=[zmem])
    ar.pop()


    ar.push()
    qh = ar.alloc(65, 2 * T, BF16)
    kh = ar.alloc(65, 2 * SEQ, BF16)
    vaug = ar.alloc(128, 2 * 32 * 65, BF16)
    qsq = ar.alloc(64, SEQ, BF16)
    P_bf = ar.alloc(128, T, BF16)
    rZ = ar.alloc(1, T, F32)
    lnk = ar.alloc(1, 4, F32)
    ubr = ar.alloc(1, T, F32)
    rZb_sb = ar.alloc(64, T, F32)
    attnT = ar.alloc(64, NH * T, BF16)
    swo = ar.alloc(64, NH * D, BF16)
    sbo = ar.alloc(1, D, BF16)
    ssb = ar.alloc(128, 2 * D, F32)

    s.dma('gpsimd', [(swo.rearrange("p (h d) -> p h d", h=NH),
                      wo.rearrange("(h p) d -> p h d", p=64)),
                     (sbo, bo)])
    def hload(h, wait):
        sl = h % 2
        return s.dma('sync', [
            (qh[:, sl * T:(sl + 1) * T], q_scr[:, h, :]),
            (kh[:, sl * SEQ:(sl + 1) * SEQ], kr_scr[:, h, :]),
            (vaug.rearrange("p (s c w) -> p s c w", s=2, w=65)[:, sl, :, 0:64],
             v_scr.rearrange("(c p) d -> p c d", p=128)[:, :, h * 64:(h + 1) * 64])],
            wait=wait)
    hld = [None] * NH
    hld[0] = hload(0, None)
    hld[1] = hload(1, None)
    mmS_last = [None] * NH
    mmV_last = [None] * NH
    ksqm = [None] * NH
    atm = [None] * NH
    rzc = [None] * NH
    for h in range(NH):
        sl = h % 2
        qhs = qh[:, sl * T:(sl + 1) * T]
        khs = kh[:, sl * SEQ:(sl + 1) * SEQ]
        vas = vaug[:, sl * 32 * 65:(sl + 1) * 32 * 65]
        ms1 = s.comp('vector', lambda e, sl=sl: e.memset(
            vaug.rearrange("p (s c w) -> p s c w", s=2, w=65)[:, sl, :, 64:65], 1.0),
            wait=[hld[h]])
        ms2 = s.comp('vector', lambda e, khs=khs: e.memset(khs[64:65, :], 1.0),
                     wait=[hld[h]])
        qm = s.comp('vector', lambda e, qhs=qhs: e.tensor_mul(
            qsq[:, 0:T], qhs[0:64, :], qhs[0:64, :]),
            wait=tw0(hld[h], mmS_last[h - 1] if h else None))
        def mm_qsq(e):
            for g in range(4):
                last = e.matmul(pO2[0:1, g * 512:(g + 1) * 512], ones_col[0:64, :],
                                qsq[:, g * 512:(g + 1) * 512], start=True, stop=True)
            return last
        q2t = s.comp('tensor', mm_qsq, wait=tw0(qm, atm[h - 1] if h else None))
        ub1 = s.comp('scalar', lambda e: e.activation(ubr, pO2[0:1, 0:T], AF.Ln),
                     wait=[q2t])
        ksqm[h] = s.comp('vector', lambda e, khs=khs: e.tensor_mul(
            qsq, khs[0:64, :], khs[0:64, :]), wait=[q2t])
        def mm_ksq2(e):
            for g in range(8):
                last = e.matmul(pA[0:1, (g % 4) * 512:((g % 4) + 1) * 512],
                                ones_col[0:64, :], qsq[:, g * 512:(g + 1) * 512],
                                start=(g < 4), stop=(g >= 4))
            return last
        k2t = s.comp('tensor', mm_ksq2,
                     wait=tw0(ksqm[h], rzc[h - 1] if h else None))
        mx = s.comp('vector', lambda e: e.reduce_max(lnk[:, 0:1], pA[0:1, 0:T], axis=AX.X),
                    wait=[k2t])
        lk = s.comp('scalar', lambda e: e.activation(lnk[:, 1:2], lnk[:, 0:1], AF.Ln),
                    wait=tw0(mx, ub1))
        ua = s.comp('vector', lambda e: e.tensor_scalar(ubr, ubr, lnk[0:1, 1:2], None, ALU.add),
                    wait=[lk])
        ue = s.comp('scalar', lambda e: e.activation(ubr, ubr, AF.Exp, scale=s.const(0.5, 1)),
                    wait=[ua])
        un = s.comp('vector', lambda e, qhs=qhs: e.tensor_scalar(
            qhs[64:65, :], ubr, -1.0, None, ALU.mult), wait=tw0(ue, ms1, ms2))
        expt = {}
        mst = {}
        mvt = {}
        for c in range(32):
            for hf in range(2):
                po = pA[:, hf * 1024:(hf + 1) * 1024]
                def mmS(e, c=c, hf=hf, khs=khs, qhs=qhs, po=po):
                    for g in range(2):
                        last = e.matmul(po[:, g * 512:(g + 1) * 512],
                                        khs[:, c * 128:(c + 1) * 128],
                                        qhs[:, hf * 1024 + g * 512:hf * 1024 + (g + 1) * 512],
                                        start=True, stop=True)
                    return last
                mst[(c, hf)] = s.comp('tensor', mmS,
                                      wait=tw0(un if c == 0 else None,
                                               rzc[h - 1] if (h and c == 0) else None,
                                               expt.get((c - 1, hf))))
                expt[(c, hf)] = s.comp('scalar', lambda e, hf=hf, po=po: e.activation(
                    P_bf[:, hf * 1024:(hf + 1) * 1024], po, AF.Exp),
                    wait=tw0(mst[(c, hf)], mvt.get((c - 1, hf)),
                             mmV_last[h - 1] if (h and c == 0) else None))
                def mmV(e, c=c, hf=hf, vas=vas):
                    for g in range(2):
                        last = e.matmul(pO2[:, hf * 1024 + g * 512:hf * 1024 + (g + 1) * 512],
                                        vas[:, c * 65:(c + 1) * 65],
                                        P_bf[:, hf * 1024 + g * 512:hf * 1024 + (g + 1) * 512],
                                        start=(c == 0), stop=(c == 31))
                    return last
                mvt[(c, hf)] = s.comp('tensor', mmV,
                                      wait=tw0(expt[(c, hf)],
                                               atm[h - 1] if (h == 1 and c == 0) else None))
        mmS_last[h] = mst[(31, 1)]
        mmV_last[h] = mvt[(31, 1)]
        if h + 2 < NH:
            hld[h + 2] = hload(h + 2, tw0(mmS_last[h], mmV_last[h], ksqm[h], un))
        rv = s.comp('vector', lambda e: e.reciprocal(rZ, pO2[64:65, :]),
                    wait=[mmV_last[h]])
        def mm_rzb(e):
            for g in range(4):
                last = e.matmul(pA[0:64, g * 512:(g + 1) * 512], onesf[0:1, 0:64],
                                rZ[:, g * 512:(g + 1) * 512], start=True, stop=True)
            return last
        rzt = s.comp('tensor', mm_rzb, wait=tw0(rv, expt[(31, 1)]))
        rzc[h] = s.comp('vector', lambda e: e.tensor_copy(rZb_sb, pA[0:64, 0:T]),
                        wait=[rzt])
        atm[h] = s.comp('vector', lambda e, h=h: e.tensor_mul(
            attnT[:, h * T:(h + 1) * T], pO2[0:64, :], rZb_sb), wait=[rzc[h]])
    wo_cp = [None] * 16
    wo_st = [None] * 16
    for t in range(16):
        po = pA[:, (t % 2) * 1024:(t % 2 + 1) * 1024]
        def wop(e, t=t, po=po):
            for h in range(NH):
                lhsT = attnT[:, h * T + t * 128:h * T + (t + 1) * 128]
                e.matmul(po[:, 0:512], lhsT, swo[:, h * D + 0: h * D + 512],
                         start=(h == 0), stop=False)
                e.matmul(po[:, 512:1024], lhsT, swo[:, h * D + 512: (h + 1) * D],
                         start=(h == 0), stop=False)
            e.matmul(po[:, 0:512], ones_row[:], sbo[0:1, 0:512], start=False, stop=True)
            return e.matmul(po[:, 512:1024], ones_row[:], sbo[0:1, 512:1024],
                            start=False, stop=True)
        wt = s.comp('tensor', wop, wait=tw0(atm[NH - 1] if t == 0 else None,
                                            rzc[NH - 1] if t == 0 else None,
                                            wo_cp[t - 2] if t >= 2 else None))
        wo_cp[t] = s.comp('vector', lambda e, t=t, po=po: e.tensor_copy(
            ssb[:, (t % 2) * D:(t % 2 + 1) * D], po),
            wait=tw0(wt, wo_st[t - 2] if t >= 2 else None))
        wo_st[t] = s.dma('sync', [(s_scr[t * 128:(t + 1) * 128, :],
                                   ssb[:, (t % 2) * D:(t % 2 + 1) * D])], wait=[wo_cp[t]])
    ar.pop()


    ar.push()
    sco = ar.alloc(128, 16 * NE, F32)
    m1c = ar.alloc(128, 16, F32)
    ohc = ar.alloc(128, 16 * NE, F32)
    alc = ar.alloc(128, 16, F32)
    ar.push()
    centb = ar.alloc(128, NE * D, F32)
    xt32 = ar.alloc(128, D, F32)
    dmp = ar.alloc(128, D, F32)
    s.dma('sync', [(centb.rearrange("p (e d) -> p e d", e=NE),
                    centt.rearrange("e p d -> p e d"))])
    for t in range(16):
        s.dma('sync', [(xt32, xn_scr[t * 128:(t + 1) * 128, :])])
        def dots(e, t=t):
            for ee in range(NE):
                last = e.scalar_tensor_tensor(dmp, xt32, 1.0,
                                              centb[:, ee * D:(ee + 1) * D],
                                              ALU.mult, ALU.mult,
                                              accum_out=sco[:, t * NE + ee:t * NE + ee + 1])
            return last
        s.comp('vector', dots)
        def sel(e, t=t):
            e.reduce_max(m1c[:, t:t + 1], sco[:, t * NE:(t + 1) * NE], axis=AX.X)
            return e.tensor_scalar(ohc[:, t * NE:(t + 1) * NE], sco[:, t * NE:(t + 1) * NE],
                                   m1c[:, t:t + 1], None, ALU.is_ge)
        s.comp('vector', sel)
        s.comp('scalar', lambda e, t=t: e.activation(alc[:, t:t + 1], m1c[:, t:t + 1], AF.Sigmoid))
    ar.pop()

    xnT = ar.alloc(128, 8 * T, BF16)
    xsq = ar.alloc(128, T, BF16)
    RSTDF = ar.alloc(128, T, F32)
    sgam = ar.alloc(128, 8 * NE, F32)
    sbet = ar.alloc(128, 8 * NE, F32)
    sb2 = ar.alloc(1, D, BF16)
    sb1c = ar.alloc(128, 32, F32)
    nfeT = ar.alloc(128, 8 * 1024, BF16)
    w1blk = ar.alloc(128, 4096, BF16)
    w2blk = ar.alloc(128, 4096, BF16)
    rh = ar.alloc(128, 32 * 1024, BF16)
    yac = ar.alloc(128, 8 * 1024, F32)
    xtile = ar.alloc(128, D, F32)
    stile = ar.alloc(128, D, F32)
    ntile = ar.alloc(128, D, F32)
    gtile = ar.alloc(128, D, F32)

    s.dma('sync', [(xnT.rearrange("p (j t) -> p j t", j=8), xnbfT_scr[:, :, 0:T])])
    s.dma('sync', [(sgam.rearrange("p (j e) -> p j e", j=8),
                    gam.rearrange("(j p) e -> p j e", p=128)),
                   (sbet.rearrange("p (j e) -> p j e", j=8),
                    bet.rearrange("(j p) e -> p j e", p=128))])
    for j in range(8):
        s.comp('vector', lambda e, j=j: e.tensor_mul(
            xsq, xnT[:, j * T:(j + 1) * T], xnT[:, j * T:(j + 1) * T]))
        def mmsq(e, j=j):
            for g in range(4):
                last = e.matmul(pA[0:1, g * 512:(g + 1) * 512], ones_col[:],
                                xsq[:, g * 512:(g + 1) * 512],
                                start=(j == 0), stop=(j == 7))
            return last
        s.comp('tensor', mmsq)
    def rstdf_row(e):
        rsr = RSTDF[0:1, :]
        e.activation(rsr, pA[0:1, 0:T], AF.Ln, bias=s.const(EPS, 1), scale=s.const(1.0 / D, 1))
        return e.activation(rsr, rsr, AF.Exp, scale=s.const(-0.5, 1))
    s.comp('scalar', rstdf_row)
    def mm_rb(e):
        rsr = RSTDF[0:1, :]
        for g in range(4):
            last = e.matmul(pA[:, g * 512:(g + 1) * 512], onesf[0:1, :],
                            rsr[:, g * 512:(g + 1) * 512], start=True, stop=True)
        return last
    s.comp('tensor', mm_rb)
    s.comp('vector', lambda e: e.tensor_copy(RSTDF, pA[:, 0:T]))
    for j in range(8):
        s.comp('vector', lambda e, j=j: e.tensor_mul(
            xnT[:, j * T:(j + 1) * T], xnT[:, j * T:(j + 1) * T], RSTDF))

    def emit_expert_ffn(st, feT, w1src, w2src, b1pair, b2pair, sb1, sb2t,
                        actf, rh_, yac_, w1b, w2b, acc_col, feT_ready=None):
        def tw(*toks):
            toks = [t for t in toks if t is not None]
            return toks if toks else s.snap()
        def w1dma(fp, wait):
            return s.dma('gpsimd', [(
                w1b.rearrange("p (sl j c) -> p sl j c", sl=2, j=8)[:, fp % 2],
                w1src(fp))], wait=wait)
        w1d = [None] * 16
        w1d[0] = w1dma(0, tw(st['mm1_last']))
        w1d[1] = w1dma(1, tw(st['mm1_last']))
        sb1tok = s.dma('scalar', [b1pair], wait=tw(st['relu_last']))
        sb2tok = s.dma('gpsimd', [b2pair], wait=tw(st['b2_last']))
        ready = feT_ready if feT_ready is not None else st['acc_last']
        mm1 = [None] * 32
        relu = [None] * 32
        for f in range(32):
            fp, sl, po = f // 2, (f // 2) % 2, (f % 4) * 1024
            def mmw1(e, f=f, sl=sl, po=po):
                for j in range(8):
                    lb = sl * 2048 + j * 256 + (f % 2) * 128
                    e.matmul(POOL[:, po:po + 512], w1b[:, lb:lb + 128],
                             feT(j, 0), start=(j == 0), stop=(j == 7))
                    last = e.matmul(POOL[:, po + 512:po + 1024], w1b[:, lb:lb + 128],
                                    feT(j, 512), start=(j == 0), stop=(j == 7))
                return last
            if f < 4:
                wv = tw(ready, st['relu_last'], w1d[fp]) if ready is not None \
                    else s.snap() + [w1d[fp]]
            else:
                wv = [relu[f - 4], w1d[fp]]
            mm1[f] = s.comp('tensor', mmw1, wait=wv)
            if f % 2 == 1 and fp + 2 < 16:
                w1d[fp + 2] = w1dma(fp + 2, [mm1[f]])
            relu[f] = s.comp('scalar', lambda e, f=f, po=po: e.activation(
                rh_[:, f * 1024:(f + 1) * 1024], POOL[:, po:po + 1024], actf,
                bias=sb1[:, f:f + 1]), wait=[mm1[f], sb1tok])
        acc_last = None
        b2tok = None
        for oh in range(2):
            ooff = oh * 512
            def w2dma(fq, wait, ooff=ooff):
                return s.dma('gpsimd', [(
                    w2b[fq % 2].rearrange("p (j c) -> p j c", j=8),
                    w2src(fq, ooff))], wait=wait)
            w2d = [None] * 4
            pw = tw(st['mm2_last']) if oh == 0 else [mm2[3]]
            w2d[0] = w2dma(0, pw)
            w2d[1] = w2dma(1, pw)
            mm2 = [None] * 4
            for fq in range(4):
                def mmw2(e, fq=fq):
                    for jj in range(8):
                        f = fq * 8 + jj
                        for tl in range(8):
                            last = e.matmul(
                                psum8(tl),
                                rh_[:, f * 1024 + tl * 128:f * 1024 + (tl + 1) * 128],
                                w2b[fq % 2][:, jj * 512:(jj + 1) * 512],
                                start=(f == 0), stop=False)
                    return last
                wv = ([acc_last, w2d[0]] if (oh == 1 and fq == 0)
                      else [relu[31], w2d[fq]])
                mm2[fq] = s.comp('tensor', mmw2, wait=wv)
                if fq + 2 < 4:
                    w2d[fq + 2] = w2dma(fq + 2, [mm2[fq]])
            def mmb2(e, ooff=ooff):
                for tl in range(8):
                    last = e.matmul(psum8(tl), ones_row[:], sb2t[0:1, ooff:ooff + 512],
                                    start=False, stop=True)
                return last
            b2tok = s.comp('tensor', mmb2, wait=[relu[31], sb2tok])
            for tl in range(8):
                def accf(e, tl=tl, ooff=ooff):
                    return e.scalar_tensor_tensor(
                        yac_[:, tl * 1024 + ooff:tl * 1024 + ooff + 512],
                        psum8(tl), acc_col(tl),
                        yac_[:, tl * 1024 + ooff:tl * 1024 + ooff + 512],
                        ALU.mult, ALU.add)
                acc_last = s.comp('vector', accf, wait=[b2tok])
            st['mm2_last'] = mm2[3]
        st['mm1_last'] = mm1[31]
        st['relu_last'] = relu[31]
        st['b2_last'] = b2tok
        st['acc_last'] = acc_last

    st3 = {'mm1_last': None, 'mm2_last': None, 'relu_last': None,
           'b2_last': None, 'acc_last': None}
    x1_t15_tok = None
    for half in (1, 0):
        hoff = half * 1024
        s.comp('vector', lambda e: e.memset(yac, 0.0))
        for ee in range(NE):
            nfe_last = None
            for j in range(8):
                nfe_last = s.comp('vector', lambda e, ee=ee, j=j, hoff=hoff: e.tensor_scalar(
                    nfeT[:, j * 1024:(j + 1) * 1024],
                    xnT[:, j * T + hoff:j * T + hoff + 1024],
                    sgam[:, j * NE + ee:j * NE + ee + 1],
                    sbet[:, j * NE + ee:j * NE + ee + 1], ALU.mult, ALU.add))
            emit_expert_ffn(
                st3,
                feT=lambda j, off: nfeT[:, j * 1024 + off:j * 1024 + off + 512],
                w1src=lambda fp, ee=ee: w1[ee].rearrange(
                    "(j p) c -> p j c", p=128)[:, :, fp * 256:(fp + 1) * 256],
                w2src=lambda fq, ooff, ee=ee: w2[ee, fq * 1024:(fq + 1) * 1024,
                                                 ooff:ooff + 512].rearrange(
                    "(j p) c -> p j c", p=128),
                b1pair=(sb1c, b1[ee, :, :]),
                b2pair=(sb2, b2d[0:1, ee * D:(ee + 1) * D]),
                sb1=sb1c, sb2t=sb2, actf=AF.Relu, rh_=rh, yac_=yac,
                w1b=w1blk, w2b=[w2blk, RSTDF.bitcast(BF16)[:, 0:4096]],
                acc_col=lambda tl, ee=ee, half=half: ohc[:, (half * 8 + tl) * NE + ee:
                                                        (half * 8 + tl) * NE + ee + 1],
                feT_ready=nfe_last)
        for tl in range(8):
            tg = half * 8 + tl
            s.dma('scalar', [(ntile, xn_scr[tg * 128:(tg + 1) * 128, :]),
                             (stile, s_scr[tg * 128:(tg + 1) * 128, :]),
                             (xtile, xf[tg * 128:(tg + 1) * 128, :])])
            def blend(e, tl=tl, tg=tg):
                return e.scalar_tensor_tensor(gtile, yac[:, tl * 1024:(tl + 1) * 1024],
                                              alc[:, tg:tg + 1], ntile, ALU.mult, ALU.add)
            s.comp('vector', blend)
            s.comp('scalar', lambda e: e.activation(gtile, gtile, AF.Silu))
            def resid(e):
                e.tensor_mul(gtile, gtile, stile)
                return e.tensor_add(xtile, xtile, gtile)
            s.comp('vector', resid)
            stok = s.dma('scalar', [(x1_scr[tg * 128:(tg + 1) * 128, :], xtile)])
            if tg == 15:
                x1_t15_tok = stok
    ar.pop()

    s.xchg(0, x1_scr[15 * 128:16 * 128, :], halo_l,
           lambda me: xch_h[me, :, :],
           lambda me, w: (xch_f[me:me + 1, 0:2] if w is None
                          else xch_f[me:me + 1, w:w + 1]),
           par_d, wait=[x1_t15_tok, par_tok])

    ar.push()
    sx5 = ar.alloc(128, D, F32)
    dump5 = ar.alloc(128, D, F32)
    stat5 = ar.alloc(128, 8, F32)
    sxn_bf5 = ar.alloc(128, D, BF16)
    xn1T = ar.alloc(128, 8 * T, BF16)
    xnhT = ar.alloc(128, 8 * 3, BF16)
    swin = ar.alloc(128, 8 * DPROJ, BF16)
    ztile = ar.alloc(128, DIN, BF16)
    xbcpre = ar.alloc(128, 2051, F32)
    ctmp = ar.alloc(128, T, F32)
    cbf = ar.alloc(128, T, BF16)
    scw = ar.alloc(128, 18 * 4, F32)
    scb = ar.alloc(128, 18, F32)
    sdtb = ar.alloc(32, 1, F32)
    salog = ar.alloc(32, 1, F32)
    dtraw = ar.alloc(32, T, F32)
    dwork5 = ar.alloc(32, T, F32)
    dtsp = ar.alloc(32, T, F32)
    expA5 = ar.alloc(32, 1, F32)
    sxh = ar.alloc(3, D, F32)
    sxh_bf = ar.alloc(3, D, BF16)
    hstat = ar.alloc(3, 8, F32)

    s.dma('gpsimd', [(swin.rearrange("p (j c) -> p j c", j=8),
                      win.rearrange("(j p) c -> p j c", p=128))])
    s.dma('sync', [(scw.rearrange("p (i k) -> p i k", i=18), cw),
                   (scb, cb), (sdtb, dtb), (salog, alog)])
    s.comp('scalar', lambda e: e.activation(expA5, salog, AF.Exp))

    for t in range(16):
        s.dma('sync', [(sx5, x1_scr[t * 128:(t + 1) * 128, :])])
        emit_ln(sx5, dump5, stat5, sxn_bf5)
        s.dma('sync', [(xn1_scr[t * 128:(t + 1) * 128, :], sx5)])
        for j in range(8):
            s.comp('tensor', lambda e, j=j: e.transpose(ptpb, sxn_bf5[:, j * 128:(j + 1) * 128], ident[:]))
            s.comp('vector', lambda e, t=t, j=j: e.tensor_copy(
                xn1T[:, j * T + t * 128: j * T + (t + 1) * 128], ptpb))

    s.dma('sync', [(sxh, halo_l[125:128, :])])
    s.comp('vector', lambda e: e.tensor_scalar(sxh, sxh, sflag[0:3, 0:1], None, ALU.mult))
    emit_ln(sxh, dump5[0:3, :], hstat, sxh_bf)
    s.comp('vector', lambda e: e.tensor_scalar(sxh_bf, sxh_bf, sflag[0:3, 0:1], None, ALU.mult))
    for j in range(8):
        s.comp('tensor', lambda e, j=j: e.transpose(ptpb[:, 0:3],
                                                    sxh_bf[:, j * 128:(j + 1) * 128],
                                                    ident[0:3, 0:3]))
        s.comp('vector', lambda e, j=j: e.tensor_copy(xnhT[:, j * 3:(j + 1) * 3], ptpb[:, 0:3]))

    for t in range(16):
        def zproj(e, t=t):
            for j in range(8):
                lhsT = xn1T[:, j * T + t * 128: j * T + (t + 1) * 128]
                for g in range(4):
                    last = e.matmul(pA[:, g * 512:(g + 1) * 512], lhsT,
                                    swin[:, j * DPROJ + g * 512: j * DPROJ + (g + 1) * 512],
                                    start=(j == 0), stop=(j == 7))
            return last
        s.comp('tensor', zproj)
        s.comp('vector', lambda e: e.tensor_copy(ztile, pA[:, 0:2048]))
        s.dma('sync', [(z_scr[t * 128:(t + 1) * 128, :], ztile)])

    for i in range(18):
        coff = DIN + i * 128
        for grp in range(2):
            def xbcmm2(e, i=i, coff=coff, grp=grp):
                for j in range(8):
                    lhsT = swin[:, j * DPROJ + coff: j * DPROJ + coff + 128]
                    e.matmul(pB[:, 0:512], lhsT,
                             xn1T[:, j * T + grp * 1024: j * T + grp * 1024 + 512],
                             start=(j == 0), stop=(j == 7))
                    last = e.matmul(pB[:, 512:1024], lhsT,
                                    xn1T[:, j * T + grp * 1024 + 512: j * T + (grp + 1) * 1024],
                                    start=(j == 0), stop=(j == 7))
                return last
            s.comp('tensor', xbcmm2)
            s.comp('vector', lambda e, grp=grp: e.tensor_copy(
                xbcpre[:, 3 + grp * 1024: 3 + (grp + 1) * 1024], pB[:, 0:1024]))
        def xbch(e, i=i, coff=coff):
            for j in range(8):
                last = e.matmul(pB[:, 0:3], swin[:, j * DPROJ + coff: j * DPROJ + coff + 128],
                                xnhT[:, j * 3:(j + 1) * 3], start=(j == 0), stop=(j == 7))
            return last
        s.comp('tensor', xbch)
        s.comp('vector', lambda e: e.tensor_copy(xbcpre[:, 0:3], pB[:, 0:3]))
        def conv2(e, i=i):
            e.tensor_scalar(ctmp, xbcpre[:, 0:T], scw[:, i * 4:i * 4 + 1], None, ALU.mult)
            e.scalar_tensor_tensor(ctmp, xbcpre[:, 1:1 + T], scw[:, i * 4 + 1:i * 4 + 2],
                                   ctmp, ALU.mult, ALU.add)
            e.scalar_tensor_tensor(ctmp, xbcpre[:, 2:2 + T], scw[:, i * 4 + 2:i * 4 + 3],
                                   ctmp, ALU.mult, ALU.add)
            return e.scalar_tensor_tensor(ctmp, xbcpre[:, 3:3 + T], scw[:, i * 4 + 3:i * 4 + 4],
                                          ctmp, ALU.mult, ALU.add)
        s.comp('vector', conv2)
        s.comp('scalar', lambda e, i=i: e.activation(cbf, ctmp, AF.Silu, bias=scb[:, i:i + 1]))
        if i < 16:
            s.dma('sync', [(xsT_scr[:, i, :], cbf)])
        elif i == 16:
            s.dma('sync', [(BT_scr, cbf)])
        else:
            s.dma('sync', [(CT_scr, cbf)])

    for grp in range(2):
        def dtmm(e, grp=grp):
            for j in range(8):
                lhsT = swin[:, j * DPROJ + DIN + CONVD: j * DPROJ + DPROJ]
                e.matmul(pB[0:32, 0:512], lhsT,
                         xn1T[:, j * T + grp * 1024: j * T + grp * 1024 + 512],
                         start=(j == 0), stop=(j == 7))
                last = e.matmul(pB[0:32, 512:1024], lhsT,
                                xn1T[:, j * T + grp * 1024 + 512: j * T + (grp + 1) * 1024],
                                start=(j == 0), stop=(j == 7))
            return last
        s.comp('tensor', dtmm)
        s.comp('vector', lambda e, grp=grp: e.tensor_copy(
            dtraw[:, grp * 1024:(grp + 1) * 1024], pB[0:32, 0:1024]))
    def sp1(e):
        e.tensor_scalar(dtraw, dtraw, sdtb, None, ALU.add)
        return e.tensor_scalar(dtsp, dtraw, 0.0, None, ALU.max)
    s.comp('vector', sp1)
    s.comp('scalar', lambda e: e.activation(dwork5, dtraw, AF.Abs))
    s.comp('scalar', lambda e: e.activation(dwork5, dwork5, AF.Exp, scale=s.const(-1.0, 32)))
    s.comp('scalar', lambda e: e.activation(dwork5, dwork5, AF.Ln, bias=s.const(1.0, 32)))
    s.comp('vector', lambda e: e.tensor_add(dtsp, dtsp, dwork5))
    s.dma('sync', [(dt_scr, dtsp)])
    def dta(e):
        e.tensor_scalar(dwork5, dtsp, expA5, -1.0, ALU.mult, ALU.mult)
        return e.tensor_tensor_scan(dwork5, dwork5, dwork5, 0.0, ALU.add, ALU.bypass)
    s.comp('vector', dta)
    s.dma('sync', [(LT_scr, dwork5)])
    ar.pop()

    ar.push()
    allonesf = ar.alloc(128, 128, F32)
    s.comp('gpsimd', lambda e: e.memset(allonesf, 1.0))
    trimask = ar.alloc(128, 128, F32)
    s.comp('gpsimd', lambda e: e.memset(trimask, 0.0))
    s.comp('gpsimd', lambda e: e.affine_select(trimask, trimask, pattern=[[1, 128]],
                                               base=0, channel_multiplier=-1,
                                               compare_op=ALU.is_ge, fill=-1e30))
    ehsel = ar.alloc(32, 32 * 128, F32)
    s.comp('gpsimd', lambda e: e.memset(ehsel, 1.0))
    s.comp('gpsimd', lambda e: e.affine_select(ehsel, ehsel,
                                               pattern=[[-1, 32], [0, 128]], base=0,
                                               channel_multiplier=1,
                                               compare_op=ALU.is_equal, fill=0.0))
    xsT6 = ar.alloc(128, 16 * T, BF16)
    sBT = ar.alloc(128, T, BF16)
    sCT = ar.alloc(128, T, BF16)
    sdt = ar.alloc(32, T, F32)
    sLT = ar.alloc(32, T, F32)
    expAb = ar.alloc(128, 32, F32)
    sdsk = ar.alloc(128, T, F32)
    h_sb = ar.alloc(128, DIN, F32)
    h_bf = ar.alloc(128, DIN, BF16)
    Y_sb = ar.alloc(128, DIN, F32)
    xs_tok = ar.alloc(128, DIN, BF16)
    Xd_bf = ar.alloc(128, DIN, BF16)
    Xe_bf = ar.alloc(128, DIN, BF16)
    B_tok = ar.alloc(128, 128, BF16)
    G_sb = ar.alloc(128, 128, BF16)
    Dm_sb = ar.alloc(128, 2 * 128, F32)
    E_sb = ar.alloc(128, 2 * 128, F32)
    MT_bf = ar.alloc(128, 2 * 128, BF16)
    aT = ar.alloc(32, 128, F32)
    negaT = ar.alloc(32, 128, F32)
    expaT = ar.alloc(32, 128, F32)
    exprT = ar.alloc(32, 128, F32)
    dt_tok = ar.alloc(128, 32, F32)
    expa_tok = ar.alloc(128, 32, F32)
    expr_tok = ar.alloc(128, 32, F32)
    dtE = ar.alloc(128, 32, F32)
    lamb = ar.alloc(128, 32, F32)
    tws = ar.alloc(128, DIN, F32)

    s.dma('sync', [(xsT6.rearrange("p (i t) -> p i t", i=16), xsT_scr),
                   (sBT, BT_scr), (sCT, CT_scr), (sdt, dt_scr), (sLT, LT_scr),
                   (sdsk, dskt)])
    s.dma('sync', [(expAb, alogt)])
    s.comp('scalar', lambda e: e.activation(expAb, expAb, AF.Exp))
    s.comp('vector', lambda e: e.memset(h_sb, 0.0))

    for c in range(16):
        ck = c * 128
        def prep_a(e, c=c, ck=ck):
            if c == 0:
                e.tensor_copy(aT, sLT[:, ck:ck + 128])
            else:
                e.tensor_scalar(aT, sLT[:, ck:ck + 128], sLT[:, ck - 1:ck],
                                None, ALU.subtract)
            e.tensor_scalar(negaT, aT, -1.0, None, ALU.mult)
            return e.tensor_scalar(exprT, sLT[:, ck:ck + 128],
                                   sLT[:, ck + 127:ck + 128], None, ALU.subtract)
        s.comp('vector', prep_a)
        def prep_exp(e):
            e.activation(expaT, aT, AF.Exp)
            return e.activation(exprT, exprT, AF.Exp, scale=s.const(-1.0, 32))
        s.comp('scalar', prep_exp)
        s.comp('tensor', lambda e, ck=ck: e.transpose(ptpf[:, 0:32], sdt[:, ck:ck + 128],
                                                      identf[0:32, 0:32]))
        s.comp('vector', lambda e: e.tensor_copy(dt_tok, ptpf[:, 0:32]))
        s.comp('tensor', lambda e: e.transpose(ptpf[:, 0:32], expaT, identf[0:32, 0:32]))
        s.comp('vector', lambda e: e.tensor_copy(expa_tok, ptpf[:, 0:32]))
        s.comp('tensor', lambda e: e.transpose(ptpf[:, 0:32], exprT, identf[0:32, 0:32]))
        s.comp('vector', lambda e: e.tensor_copy(expr_tok, ptpf[:, 0:32]))
        s.comp('vector', lambda e: e.tensor_mul(dtE, dt_tok, expAb))
        s.comp('tensor', lambda e: e.matmul(ptpf[:, 0:32], allonesf, dtE, start=True, stop=True))
        s.comp('scalar', lambda e: e.activation(lamb, ptpf[:, 0:32], AF.Exp,
                                                scale=s.const(-1.0)))
        for i in range(16):
            s.comp('tensor', lambda e, i=i, ck=ck: e.transpose(
                ptpb, xsT6[:, i * T + ck: i * T + ck + 128], ident[:]))
            s.comp('vector', lambda e, i=i: e.tensor_copy(
                xs_tok[:, i * 128:(i + 1) * 128], ptpb))
        s.comp('vector', lambda e: e.tensor_mul(dtE, dt_tok, expr_tok))
        for h in range(32):
            s.comp('vector', lambda e, h=h: e.tensor_scalar(
                Xd_bf[:, h * 64:(h + 1) * 64], xs_tok[:, h * 64:(h + 1) * 64],
                dt_tok[:, h:h + 1], None, ALU.mult))
            s.comp('vector', lambda e, h=h: e.tensor_scalar(
                Xe_bf[:, h * 64:(h + 1) * 64], xs_tok[:, h * 64:(h + 1) * 64],
                dtE[:, h:h + 1], None, ALU.mult))
        gmt = s.comp('tensor', lambda e, ck=ck: e.matmul(pB[:, 0:128], sBT[:, ck:ck + 128],
                                                   sCT[:, ck:ck + 128], start=True, stop=True))
        gcp = s.comp('vector', lambda e: e.tensor_copy(G_sb, pB[:, 0:128]))
        dtok = [None] * 32
        atok = [None] * 32
        etok = [None] * 32
        mtok = [None] * 32
        ytok = [None] * 32
        for h in range(32):
            hs = h % 2
            pD = pB[:, 0:128] if hs == 0 else pB[:, 512:640]
            Dms = Dm_sb[:, hs * 128:(hs + 1) * 128]
            Es = E_sb[:, hs * 128:(hs + 1) * 128]
            MTs = MT_bf[:, hs * 128:(hs + 1) * 128]
            def mmD(e, h=h, pD=pD):
                e.matmul(pD, ehsel[:, h * 128:(h + 1) * 128], aT,
                         start=True, stop=False)
                return e.matmul(pD, negaT, ehsel[:, h * 128:(h + 1) * 128],
                                start=False, stop=True)
            dtok[h] = s.comp('tensor', mmD,
                             wait=tw0(gcp if h == 0 else None,
                                      atok[h - 2] if h >= 2 else None))
            atok[h] = s.comp('vector', lambda e, pD=pD, Dms=Dms: e.tensor_add(
                Dms, pD, trimask),
                wait=tw0(dtok[h], etok[h - 2] if h >= 2 else None))
            etok[h] = s.comp('scalar', lambda e, Dms=Dms, Es=Es: e.activation(
                Es, Dms, AF.Exp),
                wait=tw0(atok[h], mtok[h - 2] if h >= 2 else None))
            mtok[h] = s.comp('vector', lambda e, Es=Es, MTs=MTs: e.tensor_mul(
                MTs, Es, G_sb),
                wait=tw0(etok[h], ytok[h - 2] if h >= 2 else None))
            ytok[h] = s.comp('tensor', lambda e, h=h, MTs=MTs: e.matmul(
                pA[:, h * 64:(h + 1) * 64], MTs, Xd_bf[:, h * 64:(h + 1) * 64],
                start=True, stop=True), wait=[mtok[h]])
        def ydsk(e):
            e.tensor_mul(tws, xs_tok, sdsk)
            return e.tensor_add(Y_sb, pA[:, 0:DIN], tws)
        s.comp('vector', ydsk)
        s.comp('vector', lambda e: e.tensor_copy(h_bf, h_sb))
        def mmU(e, ck=ck):
            for g in range(4):
                last = e.matmul(pA[:, g * 512:(g + 1) * 512], sCT[:, ck:ck + 128],
                                h_bf[:, g * 512:(g + 1) * 512], start=True, stop=True)
            return last
        s.comp('tensor', mmU)
        for h in range(32):
            s.comp('vector', lambda e, h=h: e.scalar_tensor_tensor(
                Y_sb[:, h * 64:(h + 1) * 64], pA[:, h * 64:(h + 1) * 64],
                expa_tok[:, h:h + 1], Y_sb[:, h * 64:(h + 1) * 64], ALU.mult, ALU.add))
        s.dma('sync', [(yl_scr[ck:ck + 128, :], Y_sb)])
        s.comp('tensor', lambda e, ck=ck: e.transpose(ptpb, sBT[:, ck:ck + 128], ident[:]))
        s.comp('vector', lambda e: e.tensor_copy(B_tok, ptpb))
        def mmH(e):
            for g in range(4):
                last = e.matmul(pA[:, g * 512:(g + 1) * 512], B_tok,
                                Xe_bf[:, g * 512:(g + 1) * 512], start=True, stop=True)
            return last
        s.comp('tensor', mmH)
        for h in range(32):
            s.comp('vector', lambda e, h=h: e.scalar_tensor_tensor(
                h_sb[:, h * 64:(h + 1) * 64], h_sb[:, h * 64:(h + 1) * 64],
                lamb[:, h:h + 1], pA[:, h * 64:(h + 1) * 64], ALU.mult, ALU.add))
    s.dma('sync', [(hend, h_sb)])
    ar.pop()

    s.xchg(1, hend, hin_l,
           lambda me: xch_s[me, :, :],
           lambda me, w: (xch_f[me:me + 1, 2:4] if w is None
                          else xch_f[me:me + 1, 2 + w:3 + w]),
           par_d)

    ar.push()
    sCT8 = ar.alloc(128, T, BF16)
    sLT8 = ar.alloc(32, T, F32)
    expg = ar.alloc(32, T, F32)
    expg_tok = ar.alloc(128, 32, F32)
    hin = ar.alloc(128, DIN, F32)
    hin_bf = ar.alloc(128, DIN, BF16)
    Yt = ar.alloc(128, DIN, F32)
    zt = ar.alloc(128, DIN, BF16)
    zs = ar.alloc(128, DIN, BF16)
    stat8 = ar.alloc(128, 8, F32)
    dump8 = ar.alloc(128, DIN, F32)
    srms = ar.alloc(128, DIN, F32)
    Ynb = ar.alloc(128, DIN, BF16)
    YnT = ar.alloc(128, 16 * 128, BF16)
    swo8 = ar.alloc(128, 16 * D, BF16)
    sglw = ar.alloc(128, 8 * DIN, BF16)
    sglb = ar.alloc(1, DIN, BF16)
    slnw = ar.alloc(128, D, F32)
    slnb = ar.alloc(128, D, F32)
    xn1t8 = ar.alloc(128, D, F32)
    y18 = ar.alloc(128, D, F32)
    y1bf = ar.alloc(128, D, BF16)
    y1T = ar.alloc(128, 8 * 128, BF16)
    sgat = ar.alloc(128, D, F32)

    s.dma('sync', [(sCT8, CT_scr), (sLT8, LT_scr), (hin, hin_l),
                   (slnw, lnw_d), (slnb, lnb_d), (srms, rmsw_d)])
    s.comp('vector', lambda e: e.tensor_scalar(hin, hin, sflag[:, 0:1], None, ALU.mult))
    s.dma('gpsimd', [(swo8.rearrange("p (j d) -> p j d", j=16),
                      wout_d.rearrange("(j p) d -> p j d", p=128)),
                     (sglw.rearrange("p (j c) -> p j c", j=8),
                      glw_d.rearrange("(j p) c -> p j c", p=128)),
                     (sglb, glb_d)])
    s.comp('vector', lambda e: e.tensor_copy(hin_bf, hin))
    s.comp('scalar', lambda e: e.activation(expg, sLT8, AF.Exp))

    for t in range(16):
        tk = t * 128
        s.dma('sync', [(Yt, yl_scr[tk:tk + 128, :]), (zt, z_scr[tk:tk + 128, :]),
                       (xn1t8, xn1_scr[tk:tk + 128, :])])
        s.comp('tensor', lambda e, tk=tk: e.transpose(ptpf[:, 0:32], expg[:, tk:tk + 128],
                                                      identf[0:32, 0:32]))
        s.comp('vector', lambda e: e.tensor_copy(expg_tok, ptpf[:, 0:32]))
        def mmU8(e, tk=tk):
            for g in range(4):
                last = e.matmul(pA[:, g * 512:(g + 1) * 512], sCT8[:, tk:tk + 128],
                                hin_bf[:, g * 512:(g + 1) * 512], start=True, stop=True)
            return last
        s.comp('tensor', mmU8)
        for h in range(32):
            s.comp('vector', lambda e, h=h: e.scalar_tensor_tensor(
                Yt[:, h * 64:(h + 1) * 64], pA[:, h * 64:(h + 1) * 64],
                expg_tok[:, h:h + 1], Yt[:, h * 64:(h + 1) * 64], ALU.mult, ALU.add))
        s.comp('scalar', lambda e: e.activation(zs, zt, AF.Silu))
        s.comp('vector', lambda e: e.tensor_mul(Yt, Yt, zs))
        s.comp('scalar', lambda e: e.activation(dump8, Yt, AF.Square, accum_out=stat8[:, 0:1]))
        def rstd8(e):
            e.activation(stat8[:, 1:2], stat8[:, 0:1], AF.Ln, bias=s.const(EPS),
                         scale=s.const(1.0 / DIN))
            return e.activation(stat8[:, 1:2], stat8[:, 1:2], AF.Exp, scale=s.const(-0.5))
        s.comp('scalar', rstd8)
        def ynorm(e):
            e.tensor_scalar(dump8, Yt, stat8[:, 1:2], None, ALU.mult)
            return e.tensor_mul(Ynb, dump8, srms)
        ynt = s.comp('vector', ynorm)
        tc8 = [None] * 16
        for j in range(16):
            pt = ptA if j % 2 == 0 else ptB
            trj = s.comp('tensor', lambda e, j=j, pt=pt: e.transpose(
                pt, Ynb[:, j * 128:(j + 1) * 128], ident[:]),
                wait=tw0(ynt if j < 2 else tc8[j - 2]))
            tc8[j] = s.comp('vector', lambda e, j=j, pt=pt: e.tensor_copy(
                YnT[:, j * 128:(j + 1) * 128], pt), wait=[trj])
        def mmWo8(e):
            for j in range(16):
                e.matmul(pB[:, 0:512], YnT[:, j * 128:(j + 1) * 128],
                         swo8[:, j * D: j * D + 512], start=(j == 0), stop=(j == 15))
                last = e.matmul(pB[:, 512:1024], YnT[:, j * 128:(j + 1) * 128],
                                swo8[:, j * D + 512: (j + 1) * D], start=(j == 0), stop=(j == 15))
            return last
        s.comp('tensor', mmWo8)
        def mk_y1(e):
            e.tensor_add(y18, pB[:, 0:1024], xn1t8)
            return e.tensor_copy(y1bf, y18)
        mky = s.comp('vector', mk_y1)
        tc8b = [None] * 8
        for j in range(8):
            pt = ptA if j % 2 == 0 else ptB
            trj = s.comp('tensor', lambda e, j=j, pt=pt: e.transpose(
                pt, y1bf[:, j * 128:(j + 1) * 128], ident[:]),
                wait=tw0(mky if j < 2 else tc8b[j - 2]))
            tc8b[j] = s.comp('vector', lambda e, j=j, pt=pt: e.tensor_copy(
                y1T[:, j * 128:(j + 1) * 128], pt), wait=[trj])
        def mmGlu(e):
            for j in range(8):
                for g in range(4):
                    e.matmul(pA[:, g * 512:(g + 1) * 512], y1T[:, j * 128:(j + 1) * 128],
                             sglw[:, j * DIN + g * 512: j * DIN + (g + 1) * 512],
                             start=(j == 0), stop=False)
            for g in range(4):
                last = e.matmul(pA[:, g * 512:(g + 1) * 512], ones_row[:],
                                sglb[0:1, g * 512:(g + 1) * 512], start=False, stop=True)
            return last
        s.comp('tensor', mmGlu)
        s.comp('scalar', lambda e: e.activation(sgat, pA[:, 1024:2048], AF.Sigmoid))
        def mk_y2(e):
            e.tensor_mul(sgat, pA[:, 0:1024], sgat)
            return e.tensor_add(y18, y18, sgat)
        s.comp('vector', mk_y2)
        def stats_v8(e):
            return e.tensor_scalar(dump8[:, 0:D], y18, 1.0, 0.0, ALU.mult, ALU.add,
                                   accum_out=stat8[:, 2:3])
        s.comp('vector', stats_v8)
        s.comp('scalar', lambda e: e.activation(dump8[:, 0:D], y18, AF.Square,
                                                accum_out=stat8[:, 3:4]))
        def mv8(e):
            e.tensor_scalar(stat8[:, 4:5], stat8[:, 2:3], 1.0 / D, None, ALU.mult)
            e.tensor_scalar(stat8[:, 5:6], stat8[:, 3:4], 1.0 / D, None, ALU.mult)
            e.tensor_scalar(stat8[:, 6:7], stat8[:, 4:5], stat8[:, 4:5], None, ALU.mult)
            return e.tensor_sub(stat8[:, 5:6], stat8[:, 5:6], stat8[:, 6:7])
        s.comp('vector', mv8)
        def rstd28(e):
            e.activation(stat8[:, 6:7], stat8[:, 5:6], AF.Ln, bias=s.const(EPS))
            return e.activation(stat8[:, 6:7], stat8[:, 6:7], AF.Exp, scale=s.const(-0.5))
        s.comp('scalar', rstd28)
        def fin8(e):
            e.tensor_scalar(y18, y18, stat8[:, 4:5], stat8[:, 6:7], ALU.subtract, ALU.mult)
            e.tensor_mul(y18, y18, slnw)
            return e.tensor_add(y18, y18, slnb)
        s.comp('vector', fin8)
        s.dma('sync', [(s1_scr[tk:tk + 128, :], y18)])
    ar.pop()

    ar.push()
    lg = ar.alloc(128, 16 * NE, F32)
    mwork = ar.alloc(128, 16 * NE, F32)
    comb = ar.alloc(128, 16 * NE, F32)
    cwork = ar.alloc(128, 8, F32)
    onecol9 = ar.alloc(128, 1, F32)

    ar.push()
    gateb = ar.alloc(128, NE * D, F32)
    sxf0 = ar.alloc(128, 2 * D, F32)
    dmp9 = ar.alloc(128, D, F32)
    s.dma('sync', [(gateb.rearrange("p (e d) -> p e d", e=NE),
                    gate_d.rearrange("e p d -> p e d"))])
    ld9 = [None] * 16
    dot9 = [None] * 16
    for t in range(16):
        sl9 = t % 2
        sx9 = sxf0[:, sl9 * D:(sl9 + 1) * D]
        ld9[t] = s.dma('sync', [(sx9, xn1_scr[t * 128:(t + 1) * 128, :])],
                       wait=([dot9[t - 2]] if t >= 2 else None))
        def dots9(e, t=t, sx9=sx9):
            for ee in range(NE):
                last = e.scalar_tensor_tensor(dmp9, sx9, 1.0,
                                              gateb[:, ee * D:(ee + 1) * D],
                                              ALU.mult, ALU.mult,
                                              accum_out=lg[:, t * NE + ee:t * NE + ee + 1])
            return last
        dot9[t] = s.comp('vector', dots9, wait=[ld9[t]])
    ar.pop()

    xn1T9 = ar.alloc(128, 8 * T, BF16)
    sb1c9 = ar.alloc(128, 32, F32)
    sb29 = ar.alloc(1, D, BF16)
    w1blk9 = ar.alloc(128, 4096, BF16)
    w2blk9 = ar.alloc(128, 4096, BF16)
    w2blk9b = ar.alloc(128, 4096, BF16)
    rh9 = ar.alloc(128, 32 * 1024, BF16)
    yac9 = ar.alloc(128, 8 * 1024, F32)
    s1t = ar.alloc(128, D, F32)
    x1t9 = ar.alloc(128, D, F32)
    gt9 = ar.alloc(128, D, F32)
    sxq = ar.alloc(128, 2 * D, BF16)
    sxf = ar.alloc(128, 2 * D, F32)

    ldx = [None] * 16
    cpx = [None] * 16
    prev_tc9 = None
    for t in range(16):
        sl9 = t % 2
        sxfs = sxf[:, sl9 * D:(sl9 + 1) * D]
        sxqs = sxq[:, sl9 * D:(sl9 + 1) * D]
        ldx[t] = s.dma('sync', [(sxfs, xn1_scr[t * 128:(t + 1) * 128, :])],
                       wait=([cpx[t - 2]] if t >= 2 else None))
        cpx[t] = s.comp('vector', lambda e, sxqs=sxqs, sxfs=sxfs: e.tensor_copy(sxqs, sxfs),
                        wait=[ldx[t]])
        tc9 = [None] * 8
        for j in range(8):
            pt = ptA if j % 2 == 0 else ptB
            trj = s.comp('tensor', lambda e, j=j, sxqs=sxqs, pt=pt: e.transpose(
                pt, sxqs[:, j * 128:(j + 1) * 128], ident[:]),
                wait=tw0(cpx[t] if j == 0 else None,
                         tc9[j - 2] if j >= 2 else
                         (prev_tc9[6 + j] if prev_tc9 is not None else None)))
            tc9[j] = s.comp('vector', lambda e, t=t, j=j, pt=pt: e.tensor_copy(
                xn1T9[:, j * T + t * 128: j * T + (t + 1) * 128], pt), wait=[trj])
        prev_tc9 = tc9

    s.dma('sync', [(sb1c9, sb1_d)])
    s.dma('gpsimd', [(sb29, sb2_d)])
    s.comp('vector', lambda e: e.memset(onecol9, 1.0))
    tk1 = [None] * 16
    tk2 = [None] * 16
    tk3 = [None] * 16
    for t in range(16):
        def topk(e, t=t):
            L = lg[:, t * NE:(t + 1) * NE]
            M = mwork[:, t * NE:(t + 1) * NE]
            e.reduce_max(cwork[:, 0:1], L, axis=AX.X)
            e.tensor_scalar(M, L, cwork[:, 0:1], None, ALU.is_ge)
            e.scalar_tensor_tensor(M, M, -1e30, L, ALU.mult, ALU.add)
            e.reduce_max(cwork[:, 1:2], M, axis=AX.X)
            e.tensor_scalar(cwork[:, 2:3], cwork[:, 0:1], -1.0, None, ALU.mult)
            return e.tensor_sub(cwork[:, 3:4], cwork[:, 1:2], cwork[:, 0:1])
        tk1[t] = s.comp('vector', topk,
                        wait=tw0(tk2[t - 1] if t >= 1 else None))
        def topk2(e, t=t):
            e.activation(mwork[:, t * NE:(t + 1) * NE], lg[:, t * NE:(t + 1) * NE],
                         AF.Exp, bias=cwork[:, 2:3])
            return e.activation(cwork[:, 4:5], cwork[:, 3:4], AF.Exp)
        tk2[t] = s.comp('scalar', topk2, wait=[tk1[t]])
        def topk3(e, t=t):
            L = lg[:, t * NE:(t + 1) * NE]
            e.tensor_scalar(cwork[:, 5:6], cwork[:, 4:5], 1.0, None, ALU.add)
            e.reciprocal(cwork[:, 6:7], cwork[:, 5:6])
            e.tensor_scalar(comb[:, t * NE:(t + 1) * NE], L, cwork[:, 1:2], None, ALU.is_ge)
            e.tensor_mul(comb[:, t * NE:(t + 1) * NE], comb[:, t * NE:(t + 1) * NE],
                         mwork[:, t * NE:(t + 1) * NE])
            return e.tensor_scalar(comb[:, t * NE:(t + 1) * NE], comb[:, t * NE:(t + 1) * NE],
                                   cwork[:, 6:7], None, ALU.mult)
        tk3[t] = s.comp('vector', topk3, wait=[tk2[t]])

    st9 = {'mm1_last': None, 'mm2_last': None, 'relu_last': None,
           'b2_last': None, 'acc_last': None}
    for half in range(2):
        hoff = half * 1024
        s.comp('vector', lambda e: e.memset(yac9, 0.0))
        for ee in range(NE + 1):
            shared = (ee == NE)
            w1sr = (sw1_d if shared else w1n_d[ee]).rearrange("(j p) c -> p j c", p=128)
            w2base = sw2_d if shared else w2n_d[ee]
            emit_expert_ffn(
                st9,
                feT=lambda j, off, hoff=hoff: xn1T9[:, j * T + hoff + off:
                                                    j * T + hoff + off + 512],
                w1src=lambda fp, w1sr=w1sr: w1sr[:, :, fp * 256:(fp + 1) * 256],
                w2src=lambda fq, ooff, w2base=w2base: w2base[
                    fq * 1024:(fq + 1) * 1024, ooff:ooff + 512].rearrange(
                    "(j p) c -> p j c", p=128),
                b1pair=(sb1c9, sb1_d if shared else b1n_d[ee, :, :]),
                b2pair=(sb29, sb2_d if shared else b2n_d[0:1, ee * D:(ee + 1) * D]),
                sb1=sb1c9, sb2t=sb29, actf=AF.Gelu, rh_=rh9, yac_=yac9,
                w1b=w1blk9, w2b=[w2blk9, w2blk9b],
                acc_col=lambda tl, ee=ee, half=half, shared=shared: (
                    onecol9[:, 0:1] if shared
                    else comb[:, ((half * 8 + tl) * NE + ee):
                              ((half * 8 + tl) * NE + ee + 1)]))
        for tl in range(8):
            tg = half * 8 + tl
            s.dma('sync', [(s1t, s1_scr[tg * 128:(tg + 1) * 128, :]),
                           (x1t9, x1_scr[tg * 128:(tg + 1) * 128, :])])
            s.comp('scalar', lambda e, tl=tl: e.activation(gt9, yac9[:, tl * 1024:(tl + 1) * 1024],
                                                           AF.Silu))
            def fin(e):
                e.tensor_mul(gt9, gt9, s1t)
                return e.tensor_add(x1t9, x1t9, gt9)
            s.comp('vector', fin)
            s.dma('sync', [(o_out[tg * 128:(tg + 1) * 128, :], x1t9)])
    ar.pop()

    return s.build()


def kernel_fused(**inputs):
    import time as _time
    nonce = (int(_time.time() * 1e6) & 0x7ffffff0) | 5
    n = NCORES
    static = _prep_static(inputs)
    static['cent_tile'] = np.ascontiguousarray(
        np.tile(inputs['L0_ffn_centroids'][:, None, :], (1, 128, 1)))
    static['gate_tile'] = np.ascontiguousarray(
        np.tile(inputs['L1_ffn_gate_w'][:, None, :], (1, 128, 1)))
    nc = build_fused()
    x = inputs['L0_attn_x']
    in_maps = []
    for c in range(8):
        b, h = c // 2, c % 2
        m = {k: static[k] for k in (
            'wq_t', 'wk_t', 'wv_t', 'bq_row', 'bk_row', 'bv_row', 'wo_t', 'bo_row',
            'cent_tile', 'gamma_t', 'beta_t', 'w1_t', 'b1_c', 'w2_t', 'b2_r',
            'w_in', 'conv_wc', 'conv_bc', 'dtb_c', 'alog_c', 'alog_tile', 'dskip_tile',
            'rmsw_tile', 'w_out', 'glu_w', 'glu_b_row', 'lnw_tile', 'lnb_tile',
            'gate_tile', 'w1n', 'b1n_c', 'w2n', 'b2n_r', 'sw1', 'sb1_c', 'sw2', 'sb2_row')}
        own = x[b, h * T:(h + 1) * T, :]
        sib = x[b, (1 - h) * T:(2 - h) * T, :]
        m['xf'] = np.ascontiguousarray(np.concatenate([own, sib], axis=0))
        own_pos = np.arange(h * T, (h + 1) * T, dtype=np.float32)
        sib_pos = np.arange((1 - h) * T, (2 - h) * T, dtype=np.float32)
        m['pos'] = np.concatenate([own_pos, sib_pos]).reshape(1, SEQ)
        m['flag'] = np.full((128, 1), float(h), np.float32)
        m['par'] = np.array([[h, nonce]], np.int32)
        in_maps.append(m)
    res = run_bass_kernel_spmd(nc, in_maps[:n], core_ids=list(range(n)))
    out = np.zeros((4, SEQ, D), dtype=np.float32)
    for c in range(n):
        b, h = c // 2, c % 2
        out[b, h * T:(h + 1) * T, :] = np.asarray(res.results[c]['out'])
    return out


def _prep_static(inputs):
    g = {}
    g['wq_t'] = np.ascontiguousarray(inputs['L0_attn_Wq'].T)
    g['wk_t'] = np.ascontiguousarray(inputs['L0_attn_Wk'].T)
    g['wv_t'] = np.ascontiguousarray(inputs['L0_attn_Wv'].T)
    g['bq_row'] = np.ascontiguousarray(inputs['L0_attn_bq'].reshape(1, D))
    g['bk_row'] = np.ascontiguousarray(inputs['L0_attn_bk'].reshape(1, D))
    g['bv_row'] = np.ascontiguousarray(inputs['L0_attn_bv'].reshape(1, D))
    g['wo_t'] = np.ascontiguousarray(inputs['L0_attn_Wo'].T)
    g['bo_row'] = np.ascontiguousarray(inputs['L0_attn_bo'].reshape(1, D))
    g['cent_t'] = np.ascontiguousarray(inputs['L0_ffn_centroids'].T)
    g['gamma_t'] = np.ascontiguousarray(inputs['L0_ffn_gamma'].T)
    g['beta_t'] = np.ascontiguousarray(inputs['L0_ffn_beta'].T)
    g['w1_t'] = np.ascontiguousarray(inputs['L0_ffn_W1'].transpose(0, 2, 1))
    g['b1_c'] = np.ascontiguousarray(inputs['L0_ffn_b1'].reshape(NE, 32, 128).transpose(0, 2, 1))
    g['w2_t'] = np.ascontiguousarray(inputs['L0_ffn_W2'].transpose(0, 2, 1))
    g['b2_r'] = np.ascontiguousarray(inputs['L0_ffn_b2'].reshape(1, NE * D))
    g['w_in'] = inputs['L1_attn_W_in']
    g['conv_wc'] = np.ascontiguousarray(inputs['L1_attn_conv_w'].reshape(18, 128, 4).transpose(1, 0, 2))
    g['conv_bc'] = np.ascontiguousarray(inputs['L1_attn_conv_b'].reshape(18, 128).T)
    g['dtb_c'] = np.ascontiguousarray(inputs['L1_attn_dt_bias'].reshape(32, 1))
    g['alog_c'] = np.ascontiguousarray(inputs['L1_attn_A_log'].reshape(32, 1))
    g['alog_tile'] = np.ascontiguousarray(np.tile(inputs['L1_attn_A_log'].reshape(1, 32), (128, 1)))
    g['dskip_tile'] = np.ascontiguousarray(np.tile(np.repeat(inputs['L1_attn_D_skip'], 64).reshape(1, T), (128, 1)))
    g['rmsw_tile'] = np.ascontiguousarray(np.tile(inputs['L1_attn_rms_w'].reshape(1, DIN), (128, 1)))
    g['w_out'] = inputs['L1_attn_W_out']
    g['glu_w'] = inputs['L1_attn_glu_w']
    g['glu_b_row'] = np.ascontiguousarray(inputs['L1_attn_glu_b'].reshape(1, DIN))
    g['lnw_tile'] = np.ascontiguousarray(np.tile(inputs['L1_attn_ln_w'].reshape(1, D), (128, 1)))
    g['lnb_tile'] = np.ascontiguousarray(np.tile(inputs['L1_attn_ln_b'].reshape(1, D), (128, 1)))
    g['gate_t'] = np.ascontiguousarray(inputs['L1_ffn_gate_w'].T)
    g['w1n'] = inputs['L1_ffn_W1']
    g['b1n_c'] = np.ascontiguousarray(inputs['L1_ffn_b1'].reshape(NE, 32, 128).transpose(0, 2, 1))
    g['w2n'] = inputs['L1_ffn_W2']
    g['b2n_r'] = np.ascontiguousarray(inputs['L1_ffn_b2'].reshape(1, NE * D))
    g['sw1'] = inputs['L1_ffn_sW1']
    g['sb1_c'] = np.ascontiguousarray(inputs['L1_ffn_sb1'].reshape(32, 128).T)
    g['sw2'] = inputs['L1_ffn_sW2']
    g['sb2_row'] = np.ascontiguousarray(inputs['L1_ffn_sb2'].reshape(1, D))
    return g


def run_l0a(inputs, static):
    nc = build_l0a()
    x = inputs['L0_attn_x']
    in_maps = []
    for c in range(8):
        b, h = c // 2, c % 2
        m = {k: static[k] for k in ('wq_t', 'wk_t', 'wv_t', 'bq_row', 'bk_row', 'bv_row')}
        m['x'] = np.ascontiguousarray(x[b, h * T:(h + 1) * T, :])
        m['pos'] = np.arange(h * T, (h + 1) * T, dtype=np.float32).reshape(1, T)
        in_maps.append(m)
    res = run_bass_kernel_spmd(nc, in_maps, core_ids=list(range(8)))
    return res.results


def kernel(**inputs):
    return kernel_fused(**inputs)
